# version 01
# baseline (speedup 1.0000x reference)
import contextlib

import numpy as np

import concourse.bass as bass
import concourse.mybir as mybir
from concourse.bass_utils import run_bass_kernel_spmd

F32 = mybir.dt.float32
BF16 = mybir.dt.bfloat16
AF = mybir.ActivationFunctionType
ALU = mybir.AluOpType
AX = mybir.AxisListType

NCORES = 8
D = 2048
T = 8192
TC = T // NCORES
KD = 16
H = 16
CH = 128
NCH = TC // CH
NE = 8
I_T = 32
NM = 16
QKV_M = 48
TOP_P = 0.9
EPS = 1e-5

DEBUG = False
_LAST = {}


class Ctr:
    __slots__ = ("h", "n")

    def __init__(self, h):
        self.h = h
        self.n = 0


class Prog:
    ENGINES = ("sync", "gpsimd", "tensor", "vector", "scalar")

    def __init__(self):
        self.ops = {e: [] for e in self.ENGINES}

    def op(self, eng, fn, waits=(), inc=None, amt=1):
        post = None
        if inc is not None:
            inc.n += amt
            post = inc.n
        ws = tuple((c, v) for (c, v) in waits if v > 0)
        self.ops[eng].append((fn, ws, inc, amt if inc is not None else 0))
        return post

    def emit(self, block):
        for eng in self.ENGINES:
            def mk(e):
                def body(h):
                    for fn, waits, inc, amt in self.ops[e]:
                        for (c, v) in waits:
                            h.wait_ge(c.h, v)
                        if fn is not None:
                            inst = fn(h)
                            if inc is not None:
                                inst.then_inc(inc.h, amt)
                return body
            getattr(block, eng)(mk(eng))


def _host_consts():
    hh = np.arange(H, dtype=np.float64)
    slopes = (2.0 ** -0.5) ** (hh + 1.0)
    lam = np.exp(-slopes)
    pos = np.arange(CH, dtype=np.float64)
    dq = lam[:, None] ** (pos[None, :] + 1.0)
    decayq = np.broadcast_to(dq.reshape(1, H * CH), (128, H * CH)).astype(np.float32)
    diff = pos[None, None, :] - pos[None, :, None]
    dm = np.where(diff >= 0, lam[:, None, None] ** np.maximum(diff, 0.0), 0.0)
    dmaskT = np.ascontiguousarray(dm.transpose(1, 0, 2).reshape(CH, H * CH)).astype(np.float32)
    dk = lam[:, None] ** (CH - 1.0 - pos[None, :])
    decayk = np.ascontiguousarray(
        np.broadcast_to(dk.T[:, :, None], (CH, H, CH)).reshape(CH, H * CH)).astype(np.float32)
    lc = lam ** CH
    lamc = np.ascontiguousarray(
        np.broadcast_to(lc[None, :, None], (CH, H, CH)).reshape(CH, H * CH)).astype(np.float32)
    return lam, decayq, dmaskT, decayk, lamc


def _host_coef(core, lam):
    out = np.zeros((128, NCORES), dtype=np.float64)
    for pp in range(128):
        h = pp // 8
        for r in range(core):
            out[pp, r] = lam[h] ** (TC * (core - 1 - r))
    return out.astype(np.float32)


def _tlt():
    out = np.zeros((128, NE * 64), dtype=np.float32)
    for e in range(NE):
        for tt in range(8):
            out[:, e * 64 + tt * 8: e * 64 + tt * 8 + e] = 1.0
    return out


def build():
    nc = bass.Bass("TRN2", target_bir_lowering=False, debug=False)

    xT = nc.dram_tensor("xT", [D, TC], F32, kind="ExternalInput").ap()
    qkv_s = nc.dram_tensor("qkv_s", [QKV_M, 128, D], F32, kind="ExternalInput").ap()
    bqkv_d = nc.dram_tensor("bqkv", [128, QKV_M], F32, kind="ExternalInput").ap()
    wout_s = nc.dram_tensor("wout_s", [NM, 128, D], F32, kind="ExternalInput").ap()
    bout_d = nc.dram_tensor("bout", [128, NM], F32, kind="ExternalInput").ap()
    router_d = nc.dram_tensor("router_p", [128, 128], F32, kind="ExternalInput").ap()
    gate_s = nc.dram_tensor("gate_s", [NE, I_T, 128, D], F32, kind="ExternalInput").ap()
    up_s = nc.dram_tensor("up_s", [NE, I_T, 128, D], F32, kind="ExternalInput").ap()
    down_s = nc.dram_tensor("down_s", [NE, NM, 128, I_T * 128], F32, kind="ExternalInput").ap()
    dmaskT_d = nc.dram_tensor("dmaskT", [128, H * CH], F32, kind="ExternalInput").ap()
    decayq_d = nc.dram_tensor("decayq", [128, H * CH], F32, kind="ExternalInput").ap()
    decayk_d = nc.dram_tensor("decayk", [128, H * CH], F32, kind="ExternalInput").ap()
    lamc_d = nc.dram_tensor("lamc", [128, H * CH], F32, kind="ExternalInput").ap()
    tlt_d = nc.dram_tensor("tlt", [128, NE * 64], F32, kind="ExternalInput").ap()
    coef_d = nc.dram_tensor("coef", [128, NCORES], F32, kind="ExternalInput").ap()
    ident_d = nc.dram_tensor("ident", [128, 128], F32, kind="ExternalInput").ap()
    outT = nc.dram_tensor("outT", [D, TC], F32, kind="ExternalOutput").ap()
    dbg = {}
    if DEBUG:
        for nm_, shape, dt in [
            ("dbg_xln1", [128, KD * TC], BF16), ("dbg_A", [128, H * CH], F32),
            ("dbg_P", [128, H * CH], F32), ("dbg_attnT", [128, H * TC], BF16),
            ("dbg_R1", [128, KD * TC], F32), ("dbg_probs", [128, 64], F32),
            ("dbg_w", [128, 64], F32), ("dbg_kT", [128, KD * TC], BF16),
            ("dbg_vtm", [128, NCH * D], BF16), ("dbg_xln2", [128, KD * TC], BF16),
        ]:
            dbg[nm_] = nc.dram_tensor(nm_, shape, dt, kind="ExternalOutput").ap()

    a_loc = nc.dram_tensor("a_loc", [128, H * CH], F32, kind="Internal")
    ag = nc.dram_tensor("ag", [NCORES * 128, H * CH], F32, kind="Internal")
    p_tmp = nc.dram_tensor("p_tmp", [128, H * CH], F32, kind="Internal")

    p = Prog()
    L0 = contextlib.ExitStack()

    def mk_alloc(stk):
        return (lambda name, shape, dt: stk.enter_context(nc.sbuf_tensor(name, shape, dt)),
                lambda name, shape, dt: stk.enter_context(nc.psum_tensor(name, shape, dt)))

    with L0:
        sb0, ps0 = mk_alloc(L0)
        sem = lambda name: Ctr(L0.enter_context(nc.semaphore(name)))
        cS = sem("cS")
        cW = sem("cW")
        cT = sem("cT")
        cV = sem("cV")
        cA = sem("cA")
        cCC = sem("cCC")

        ones_f = sb0("ones_f", [128, 1], F32)
        ones_row = sb0("ones_row", [1, 128], F32)
        id_f = sb0("id_f", [128, 128], F32)
        id_b = sb0("id_b", [128, 128], BF16)
        bqkv = sb0("bqkv_sb", [128, QKV_M], F32)
        bout = sb0("bout_sb", [128, NM], F32)
        router = sb0("router_sb", [128, 128], BF16)
        tlt = sb0("tlt_sb", [128, NE * 64], F32)
        coef = sb0("coef_sb", [128, NCORES], F32)
        pf = [ps0("pf%d" % i, [128, 512], F32) for i in range(6)]

        v_pre = 0
        for fn in [
            lambda s: s.dma_start(out=bqkv[:], in_=bqkv_d),
            lambda s: s.dma_start(out=bout[:], in_=bout_d),
            lambda s: s.dma_start(out=tlt[:], in_=tlt_d),
            lambda s: s.dma_start(out=coef[:], in_=coef_d),
            lambda s: s.dma_start(out=id_f[:], in_=ident_d),
        ]:
            v_pre = p.op("sync", fn, inc=cS, amt=16)
        v_idb = p.op("gpsimd", lambda g: g.dma_start(out=id_b[:], in_=ident_d), inc=cW, amt=16)
        v_rtr = p.op("gpsimd", lambda g: g.dma_start(out=router[:], in_=router_d), inc=cW, amt=16)
        p.op("vector", lambda v: v.memset(ones_f[:], 1.0), inc=cV)
        p.op("vector", lambda v: v.memset(ones_row[:], 1.0), inc=cV)
        v_ones = cV.n

        def emit_ln(src, dst, sqb, rows, nsc, ready):
            for half in range(2):
                for k in range(KD):
                    waits = list(ready) + [(cV, v_ones)] if (half == 0 and k == 0) else []

                    def mm(t, k=k, half=half):
                        return t.matmul(pf[0][0:1, 0:512], ones_f[:, 0:1],
                                        src[:, k * TC + half * 512: k * TC + half * 512 + 512],
                                        start=(k == 0), stop=(k == KD - 1))
                    tm = p.op("tensor", mm, waits=waits, inc=(cT if k == KD - 1 else None))
                ve = p.op("vector", (lambda v, half=half: v.tensor_scalar(
                    rows[0:1, half * 512: half * 512 + 512], pf[0][0:1, 0:512],
                    1.0 / D, None, op0=ALU.mult)), waits=[(cT, tm)], inc=cV)
                if half == 0:
                    p.op("tensor", None, waits=[(cV, ve)])
            sq_free = [(cT, 0), (cT, 0)]
            for half in range(2):
                for k in range(KD):
                    par = k % 2
                    aw = [sq_free[par]]
                    if half == 0 and k < 2:
                        aw = aw + list(ready)
                    va = p.op("scalar", (lambda a, k=k, half=half, par=par: a.activation(
                        sqb[par][:, 0:512],
                        src[:, k * TC + half * 512: k * TC + half * 512 + 512],
                        AF.Square)), waits=aw, inc=cA)

                    def mm(t, par=par, k=k):
                        return t.matmul(pf[1][0:1, 0:512], ones_f[:, 0:1], sqb[par][:, 0:512],
                                        start=(k == 0), stop=(k == KD - 1))
                    tm = p.op("tensor", mm, waits=[(cA, va)], inc=cT)
                    sq_free[par] = (cT, tm)
                ve = p.op("vector", (lambda v, half=half: v.tensor_scalar(
                    rows[0:1, 1024 + half * 512: 1024 + half * 512 + 512], pf[1][0:1, 0:512],
                    1.0 / D, None, op0=ALU.mult)), waits=[(cT, tm)], inc=cV)
                if half == 0:
                    p.op("tensor", None, waits=[(cV, ve)])
            p.op("vector", lambda v: v.tensor_tensor(
                rows[0:1, 2048:3072], rows[0:1, 0:1024], rows[0:1, 0:1024], op=ALU.mult), inc=cV)
            v_var = p.op("vector", lambda v: v.scalar_tensor_tensor(
                rows[0:1, 3072:4096], rows[0:1, 1024:2048], EPS, rows[0:1, 2048:3072],
                op0=ALU.add, op1=ALU.subtract), inc=cV)
            a_sd = p.op("scalar", lambda a: a.activation(
                rows[0:1, 2048:3072], rows[0:1, 3072:4096], AF.Sqrt),
                waits=[(cV, v_var)], inc=cA)
            v_rstd = p.op("vector", lambda v: v.reciprocal(
                rows[0:1, 3072:4096], rows[0:1, 2048:3072]), waits=[(cA, a_sd)], inc=cV)
            for half in range(2):
                p.op("tensor", (lambda t, half=half: t.matmul(
                    pf[2 + half][:, 0:512], ones_row[0:1, :],
                    rows[0:1, half * 512: half * 512 + 512], start=True, stop=True)),
                    waits=[(cV, v_rstd)] if half == 0 else [], inc=cT)
                t_bc = p.op("tensor", (lambda t, half=half: t.matmul(
                    pf[4 + half][:, 0:512], ones_row[0:1, :],
                    rows[0:1, 3072 + half * 512: 3072 + half * 512 + 512],
                    start=True, stop=True)), inc=cT)
            last = 0
            for k in range(KD):
                for half in range(2):
                    par = k % 2
                    p.op("vector", (lambda v, k=k, half=half, par=par: v.tensor_tensor(
                        nsc[par][:, 0:512],
                        src[:, k * TC + half * 512: k * TC + half * 512 + 512],
                        pf[2 + half][:, 0:512], op=ALU.subtract)),
                        waits=([(cT, t_bc)] if (k == 0 and half == 0) else []), inc=cV)
                    last = p.op("vector", (lambda v, k=k, half=half, par=par: v.tensor_tensor(
                        dst[:, k * TC + half * 512: k * TC + half * 512 + 512],
                        nsc[par][:, 0:512],
                        pf[4 + half][:, 0:512], op=ALU.mult)), inc=cV)
            return last

        def emit_proj(wdram, m0, m1, act, wbuf, wbuf_free, evac, act_ready, bank_init):
            bank_free = dict(bank_init)
            for i, m in enumerate(range(m0, m1)):
                par = i % 2
                wload = p.op("gpsimd", (lambda g, m=m, par=par: g.dma_start(
                    out=wbuf[par][:], in_=wdram[m])), waits=[wbuf_free[par]], inc=cW, amt=16)
                tm = [0, 0]
                for half in range(2):
                    bk = par * 2 + half
                    for k in range(KD):
                        waits = []
                        if k == 0:
                            waits = [(cW, wload), bank_free[bk]]
                            if i < 2:
                                waits = waits + list(act_ready)

                        def mm(t, m=m, par=par, half=half, k=k, bk=bk):
                            return t.matmul(pf[bk][:, 0:512],
                                            wbuf[par][:, k * 128:(k + 1) * 128],
                                            act[:, k * TC + half * 512: k * TC + half * 512 + 512],
                                            start=(k == 0), stop=(k == KD - 1))
                        tmv = p.op("tensor", mm, waits=waits, inc=(cT if k == KD - 1 else None))
                    tm[half] = tmv
                wbuf_free[par] = (cT, tm[1])
                for half in range(2):
                    bk = par * 2 + half
                    bank_free[bk] = evac(i, m, half, bk, [(cT, tm[half])])

        bigA_stk = contextlib.ExitStack()
        sbBA, _ = mk_alloc(bigA_stk)
        bigA = sbBA("bigA", [128, KD * TC], BF16)
        x_ln1 = bigA
        ln1_stk = contextlib.ExitStack()
        sbL, _ = mk_alloc(ln1_stk)
        x_f32 = sbL("x_f32", [128, KD * TC], F32)
        sqb = [sbL("sqb0", [128, 512], F32), sbL("sqb1", [128, 512], F32)]
        rows = sbL("rows", [1, 4096], F32)
        nsc = [sbL("nsc0", [128, 512], F32), sbL("nsc1", [128, 512], F32)]
        v_xload = p.op("sync", lambda s: s.dma_start(
            out=x_f32[:], in_=xT.rearrange("(k p) t -> p k t", k=KD, p=128)), inc=cS, amt=16)
        v_ln1 = emit_ln(x_f32, x_ln1, sqb, rows, nsc, [(cS, v_xload)])
        if DEBUG:
            p.op("sync", lambda s: s.dma_start(out=dbg["dbg_xln1"], in_=x_ln1[:]),
                 waits=[(cV, v_ln1)], inc=cS, amt=16)
        ln1_t, ln1_v, ln1_a = cT.n, cV.n, cA.n
        ln1_stk.close()
        snap1 = [(cT, ln1_t), (cV, ln1_v), (cA, ln1_a)]

        att_stk = contextlib.ExitStack()
        sbA, psA = mk_alloc(att_stk)
        dmaskT = sbA("dmaskT_sb", [128, H * CH], F32)
        decayq = sbA("decayq_sb", [128, H * CH], F32)
        decayk = sbA("decayk_sb", [128, H * CH], F32)
        lamct = sbA("lamc_sb", [128, H * CH], F32)
        S_run = sbA("S_run", [128, H * CH], F32)
        S_bf = sbA("S_bf", [128, H * CH], BF16)
        kd_tm = sbA("kd_tm", [128, H * CH], BF16)
        qd_s = sbA("qd_s", [128, H * CH], BF16)
        sT_s = sbA("sT_s", [128, H * CH], BF16)
        s_tmp = sbA("s_tmp", [128, 512], F32)
        v_tm = sbA("v_tm", [128, NCH * D], BF16)
        kT = sbA("kT", [128, KD * TC], BF16)
        bigB = sbA("bigB", [128, KD * TC], BF16)
        vT = bigB
        qT = bigB
        wbufA = [sbA("wbufA0", [128, D], BF16), sbA("wbufA1", [128, D], BF16)]
        ptb = [psA("ptb0", [128, 1024], BF16), psA("ptb1", [128, 1024], BF16)]

        v_tabs = 0
        for i, (dst_t, src_t) in enumerate([(dmaskT, dmaskT_d), (decayq, decayq_d),
                                            (decayk, decayk_d), (lamct, lamc_d)]):
            v_tabs = p.op("sync", (lambda s, dst_t=dst_t, src_t=src_t: s.dma_start(
                out=dst_t[:], in_=src_t)), waits=(snap1 if i == 0 else []), inc=cS, amt=16)

        def evac_kv(i, m, half, bk, waits):
            dst = kT if m < 32 else vT
            mrel = (m - 16) if m < 32 else (m - 32)
            va = p.op("scalar", (lambda a, m=m, half=half, bk=bk, dst=dst, mrel=mrel: a.activation(
                dst[:, mrel * TC + half * 512: mrel * TC + half * 512 + 512],
                pf[bk][:, 0:512], AF.Identity, bias=bqkv[:, m:m + 1], scale=1.0)),
                waits=waits + [(cS, v_pre)], inc=cA)
            return (cA, va)

        wbuf_free = [snap1[0], snap1[0]]
        emit_proj(qkv_s, 16, QKV_M, x_ln1, wbufA, wbuf_free, evac_kv,
                  [(cV, v_ln1)] + snap1, {i: snap1[1] for i in range(4)})
        v_kv_done = cA.n

        v_s0 = p.op("vector", lambda v: v.memset(S_run[:], 0.0),
                    waits=[(cA, ln1_a), (cT, ln1_t)], inc=cV)

        tb_free = [snap1[1], snap1[1]]
        kv_free = [(cV, v_ln1), (cV, v_ln1)]

        def emit_transposes(src, jj, dst, scale_tbl, first_waits):
            outs = []
            for g in range(2):
                for hh in range(8):
                    h = g * 8 + hh
                    waits = []
                    if hh == 0:
                        waits = [tb_free[g]]
                        if g == 0:
                            waits = waits + first_waits
                    tm = p.op("tensor", (lambda t, h=h, g=g, hh=hh, jj=jj, src=src: t.transpose(
                        ptb[g][:, hh * 128:(hh + 1) * 128],
                        src[:, h * TC + jj * 128: h * TC + jj * 128 + 128],
                        id_b[:, :])), waits=waits, inc=(cT if hh == 7 else None))
                if scale_tbl is None:
                    ve = p.op("vector", (lambda v, g=g, jj=jj, dst=dst: v.tensor_copy(
                        dst[:, g * 1024:(g + 1) * 1024], ptb[g][:, :])),
                        waits=[(cT, tm)], inc=cV)
                else:
                    ve = p.op("vector", (lambda v, g=g, dst=dst, scale_tbl=scale_tbl: v.tensor_tensor(
                        dst[:, g * 1024:(g + 1) * 1024], ptb[g][:, :],
                        scale_tbl[:, g * 1024:(g + 1) * 1024], op=ALU.mult)),
                        waits=[(cT, tm)], inc=cV)
                tb_free[g] = (cV, ve)
                outs.append(ve)
            return outs

        def emit_kv_s(jj, kd_ready, vtm_ready):
            for g4 in range(4):
                kb = 4 + (g4 % 2)
                for hh in range(4):
                    h = g4 * 4 + hh
                    waits = []
                    if hh == 0:
                        waits = [kv_free[g4 % 2]]
                        if g4 < 2:
                            waits.append((cV, kd_ready[g4 % 2]))
                            if vtm_ready is not None:
                                waits.append((cV, vtm_ready[g4 % 2]))
                        else:
                            waits.append((cV, kd_ready[1]))
                            if vtm_ready is not None:
                                waits.append((cV, vtm_ready[1]))
                    tm = p.op("tensor", (lambda t, h=h, jj=jj, kb=kb, hh=hh: t.matmul(
                        pf[kb][:, hh * 128:(hh + 1) * 128],
                        kd_tm[:, h * 128:(h + 1) * 128],
                        v_tm[:, jj * D + h * 128: jj * D + (h + 1) * 128],
                        start=True, stop=True, skip_group_check=True)),
                        waits=waits, inc=(cT if hh == 3 else None))
                p.op("vector", (lambda v, g4=g4: v.tensor_tensor(
                    s_tmp[:, 0:512], S_run[:, g4 * 512:(g4 + 1) * 512],
                    lamct[:, g4 * 512:(g4 + 1) * 512], op=ALU.mult)), inc=cV)
                v2 = p.op("vector", (lambda v, g4=g4, kb=kb: v.tensor_tensor(
                    S_run[:, g4 * 512:(g4 + 1) * 512], s_tmp[:, 0:512],
                    pf[kb][:, 0:512], op=ALU.add)), waits=[(cT, tm)], inc=cV)
                kv_free[g4 % 2] = (cV, v2)

        for j in range(NCH):
            fw = [(cA, v_kv_done), (cW, v_idb)] if j == 0 else []
            vte = emit_transposes(vT, j, v_tm.ap()[:, j * D:(j + 1) * D], None, fw)
            fw2 = [(cS, v_tabs), (cV, v_s0)] if j == 0 else []
            kde = emit_transposes(kT, j, kd_tm.ap(), decayk, fw2)
            emit_kv_s(j, kde, vte)
        v_scan1_v = cV.n
        v_scan1_t = cT.n
        if DEBUG:
            p.op("sync", lambda s: s.dma_start(out=dbg["dbg_A"], in_=S_run[:]),
                 waits=[(cV, v_scan1_v)], inc=cS, amt=16)
            p.op("sync", lambda s: s.dma_start(out=dbg["dbg_kT"], in_=kT[:]), inc=cS, amt=16)
            p.op("sync", lambda s: s.dma_start(out=dbg["dbg_vtm"], in_=v_tm[:]), inc=cS, amt=16)

        aL = a_loc.ap().rearrange("(h2 dhi) (dlo e) -> h2 dhi dlo e", h2=16, dhi=8, dlo=16, e=128)
        for h in range(H):
            v_exp = p.op("sync", (lambda s, h=h: s.dma_start(
                out=aL[h], in_=S_run[:, h * 128:(h + 1) * 128])),
                waits=([(cV, v_scan1_v)] if h == 0 else []), inc=cS, amt=16)

        def evac_q(i, m, half, bk, waits):
            va = p.op("scalar", (lambda a, m=m, half=half, bk=bk: a.activation(
                qT[:, m * TC + half * 512: m * TC + half * 512 + 512],
                pf[bk][:, 0:512], AF.Identity, bias=bqkv[:, m:m + 1], scale=1.0)),
                waits=waits, inc=cA)
            return (cA, va)

        emit_proj(qkv_s, 0, 16, x_ln1, wbufA, wbuf_free, evac_q,
                  [(cV, v_ln1), (cT, v_scan1_t)], {i: (cV, v_scan1_v) for i in range(4)})
        v_q_done_a = cA.n
        v_q_done_t = cT.n

        v_cc = p.op("gpsimd", lambda g: g.collective_compute(
            "AllGather", ALU.bypass, replica_groups=[list(range(NCORES))],
            ins=[a_loc.ap()], outs=[ag.ap()]), waits=[(cS, v_exp)], inc=cCC)

        exch_stk = contextlib.ExitStack()
        sbE, _ = mk_alloc(exch_stk)
        ar_sb = sbE("ar_sb", [128, 1024], F32)
        P_sb = sbE("P_sb", [128, 1024], F32)
        v_pexp = 0
        for half in range(2):
            v_cmb = 0
            for r in range(NCORES):
                war = p.op("gpsimd", (lambda g, r=r, half=half: g.dma_start(
                    out=ar_sb[:], in_=ag[r * 128:(r + 1) * 128, half * 1024:(half + 1) * 1024])),
                    waits=([(cCC, 1)] if (r == 0 and half == 0) else [(cV, v_cmb)]),
                    inc=cW, amt=16)
                if r == 0:
                    v_cmb = p.op("vector", (lambda v, r=r: v.tensor_scalar(
                        P_sb[:], ar_sb[:], coef[:, r:r + 1], None, op0=ALU.mult)),
                        waits=[(cW, war), (cS, v_pre)] + ([(cS, v_pexp)] if half == 1 else []),
                        inc=cV)
                else:
                    v_cmb = p.op("vector", (lambda v, r=r: v.scalar_tensor_tensor(
                        P_sb[:], ar_sb[:], coef[:, r:r + 1], P_sb[:],
                        op0=ALU.mult, op1=ALU.add)), waits=[(cW, war)], inc=cV)
            pT = p_tmp.ap().rearrange("(dhi dlo) (h2 e) -> h2 dhi dlo e",
                                      dhi=8, dlo=16, h2=16, e=128)
            for h in range(H):
                v_pexp = p.op("sync", (lambda s, h=h, half=half, pT=pT: s.dma_start(
                    out=pT[h][:, half * 8:(half + 1) * 8, :],
                    in_=P_sb.ap()[h * 8:(h + 1) * 8, :].rearrange(
                        "dhi (dlo e) -> dhi dlo e", dlo=8, e=128))),
                    waits=([(cV, v_cmb)] if h == 0 else []), inc=cS, amt=16)
        v_pimp = p.op("sync", lambda s: s.dma_start(out=S_run[:], in_=p_tmp.ap()),
                      waits=[(cS, v_pexp), (cV, v_scan1_v)], inc=cS, amt=16)
        exch_stk.close()
        if DEBUG:
            p.op("sync", lambda s: s.dma_start(out=dbg["dbg_P"], in_=S_run[:]),
                 waits=[(cS, v_pimp)], inc=cS, amt=16)

        attnT = bigA
        sc_free = [(cA, v_q_done_a), (cA, v_q_done_a)]
        po_free = [(cA, v_q_done_a), (cA, v_q_done_a)]
        t_prev_j = 0
        for j in range(NCH):
            w_sbf = [(cT, t_prev_j)] if j > 0 else [(cS, v_pimp)]
            v_sbf = p.op("vector", (lambda v: v.tensor_copy(S_bf[:], S_run[:])),
                         waits=w_sbf, inc=cV)
            v_qd = p.op("vector", (lambda v, j=j: v.tensor_tensor(
                qd_s.ap().rearrange("p (h t) -> p h t", h=H, t=CH),
                qT.ap().rearrange("p (h t) -> p h t", h=H, t=TC)[:, :, j * 128:(j + 1) * 128],
                decayq.ap().rearrange("p (h t) -> p h t", h=H, t=CH), op=ALU.mult)),
                waits=([(cT, t_prev_j)] if j > 0 else [(cA, v_q_done_a)]), inc=cV)
            kde = emit_transposes(kT, j, kd_tm.ap(), decayk,
                                  [(cT, v_q_done_t)] if j == 0 else [])
            sT_ready = [0, 0, 0, 0]
            for g4 in range(4):
                sb_ = g4 % 2
                for hh in range(4):
                    h = g4 * 4 + hh
                    waits = [sc_free[sb_]] if hh == 0 else []
                    tm = p.op("tensor", (lambda t, h=h, j=j, sb_=sb_, hh=hh: t.matmul(
                        pf[sb_][:, hh * 128:(hh + 1) * 128],
                        kT[:, h * TC + j * 128: h * TC + j * 128 + 128],
                        qT[:, h * TC + j * 128: h * TC + j * 128 + 128],
                        start=True, stop=True, skip_group_check=True)),
                        waits=waits, inc=(cT if hh == 3 else None))
                vm = p.op("vector", (lambda v, g4=g4, sb_=sb_: v.tensor_tensor(
                    sT_s[:, g4 * 512:(g4 + 1) * 512], pf[sb_][:, 0:512],
                    dmaskT[:, g4 * 512:(g4 + 1) * 512], op=ALU.mult)),
                    waits=[(cT, tm)], inc=cV)
                sc_free[sb_] = (cV, vm)
                sT_ready[g4] = vm
            for g4 in range(4):
                pb_ = 2 + (g4 % 2)
                for hh in range(4):
                    h = g4 * 4 + hh
                    waits = []
                    if hh == 0:
                        waits = [po_free[g4 % 2], (cV, sT_ready[g4])]
                        if g4 == 0:
                            waits.append((cV, v_qd))
                            waits.append((cV, v_sbf))
                    p.op("tensor", (lambda t, h=h, j=j, pb_=pb_, hh=hh: t.matmul(
                        pf[pb_][:, hh * 128:(hh + 1) * 128],
                        v_tm[:, j * D + h * 128: j * D + (h + 1) * 128],
                        sT_s[:, h * 128:(h + 1) * 128],
                        start=True, stop=False, skip_group_check=True)), waits=waits)
                    tm = p.op("tensor", (lambda t, h=h, pb_=pb_, hh=hh: t.matmul(
                        pf[pb_][:, hh * 128:(hh + 1) * 128],
                        S_bf[:, h * 128:(h + 1) * 128],
                        qd_s[:, h * 128:(h + 1) * 128],
                        start=False, stop=True, skip_group_check=True)),
                        inc=(cT if hh == 3 else None))
                va = p.op("scalar", (lambda a, g4=g4, j=j, pb_=pb_: a.activation(
                    attnT.ap().rearrange("p (h t) -> p h t", h=H, t=TC)[:, g4 * 4:(g4 + 1) * 4, j * 128:(j + 1) * 128],
                    pf[pb_].ap().rearrange("p (h t) -> p h t", h=4, t=128),
                    AF.Identity)), waits=[(cT, tm)], inc=cA)
                po_free[g4 % 2] = (cA, va)
            emit_kv_s(j, kde, None)
            t_prev_j = cT.n
        v_scan2_t = cT.n
        v_scan2_v = cV.n
        v_scan2_a = cA.n
        if DEBUG:
            p.op("sync", lambda s: s.dma_start(out=dbg["dbg_attnT"], in_=attnT[:]),
                 waits=[(cA, v_scan2_a)], inc=cS, amt=16)

        att_stk.close()
        snapA = [(cT, v_scan2_t), (cV, v_scan2_v), (cA, v_scan2_a)]

        tail_stk = contextlib.ExitStack()
        sbT, _ = mk_alloc(tail_stk)
        R = sbT("R", [128, KD * TC], F32)
        w_eT = sbT("w_eT", [8, TC], F32)

        p6_stk = contextlib.ExitStack()
        sbP6, _ = mk_alloc(p6_stk)
        wbufB = [sbP6("wbufB0", [128, D], BF16), sbP6("wbufB1", [128, D], BF16)]
        xrb = [sbP6("xrb0", [128, TC], F32), sbP6("xrb1", [128, TC], F32)]
        xr_ready = {}
        for m in range(2):
            xr_ready[m] = p.op("sync", (lambda s, m=m: s.dma_start(
                out=xrb[m][:], in_=xT[m * 128:(m + 1) * 128, :])),
                waits=(snapA if m == 0 else []), inc=cS, amt=16)

        def evac_proj(i, m, half, bk, waits):
            par = m % 2
            vv = p.op("vector", (lambda v, m=m, half=half, bk=bk, par=par: v.scalar_tensor_tensor(
                R[:, m * TC + half * 512: m * TC + half * 512 + 512],
                pf[bk][:, 0:512], bout[:, m:m + 1],
                xrb[par][:, half * 512: half * 512 + 512],
                op0=ALU.add, op1=ALU.add)),
                waits=waits + [(cS, xr_ready[m])], inc=cV)
            if half == 1 and m + 2 < NM:
                xr_ready[m + 2] = p.op("sync", (lambda s, m=m, par=par: s.dma_start(
                    out=xrb[par][:], in_=xT[(m + 2) * 128:(m + 3) * 128, :])),
                    waits=[(cV, vv)], inc=cS, amt=16)
            return (cV, vv)

        wbufB_free = [snapA[0], snapA[0]]
        emit_proj(wout_s, 0, NM, attnT, wbufB, wbufB_free, evac_proj,
                  [(cA, v_scan2_a)] + snapA, {i: snapA[1] for i in range(4)})
        v_R1 = cV.n
        if DEBUG:
            p.op("sync", lambda s: s.dma_start(out=dbg["dbg_R1"], in_=R[:]),
                 waits=[(cV, v_R1)], inc=cS, amt=16)
        p6_t, p6_v, p6_a = cT.n, cV.n, cA.n
        p6_stk.close()
        snap6 = [(cT, p6_t), (cV, p6_v), (cA, p6_a)]

        x_ln2 = bigA
        ln2_stk = contextlib.ExitStack()
        sbL2, _ = mk_alloc(ln2_stk)
        sqb2 = [sbL2("sqb2_0", [128, 512], F32), sbL2("sqb2_1", [128, 512], F32)]
        rows2 = sbL2("rows2", [1, 4096], F32)
        nsc2 = [sbL2("nsc2_0", [128, 512], F32), sbL2("nsc2_1", [128, 512], F32)]
        v_ln2 = emit_ln(R, x_ln2, sqb2, rows2, nsc2, [(cV, v_R1)] + snap6)
        ln2_t, ln2_a = cT.n, cA.n
        if DEBUG:
            p.op("sync", lambda s: s.dma_start(out=dbg["dbg_xln2"], in_=x_ln2[:]),
                 waits=[(cV, v_ln2)], inc=cS, amt=16)

        rt_stk = contextlib.ExitStack()
        sbR, _ = mk_alloc(rt_stk)
        lgT = sbR("lgT", [8, TC], F32)
        lg_tm = sbR("lg_tm", [128, 64], F32)
        probs = sbR("probs", [128, 64], F32)
        g1 = sbR("g1", [128, 64], F32)
        g2 = sbR("g2", [128, 64], F32)
        se = sbR("se", [128, 8], F32)
        ke = sbR("ke", [128, 8], F32)
        w_all = sbR("w_all", [128, 64], F32)

        r3 = lambda apx: apx.rearrange("p (tt e) -> p tt e", tt=8, e=8)
        o3 = lambda apx: apx.rearrange("p (tt o) -> p tt o", tt=8, o=1)
        for half in range(2):
            for k in range(KD):
                waits = []
                if k == 0:
                    waits = ([(cV, v_ln2), (cW, v_rtr)] + snap6) if half == 0 else []

                def mm(t, half=half, k=k):
                    return t.matmul(pf[half][0:8, 0:512],
                                    router[:, k * 8:(k + 1) * 8],
                                    x_ln2[:, k * TC + half * 512: k * TC + half * 512 + 512],
                                    start=(k == 0), stop=(k == KD - 1))
                tm = p.op("tensor", mm, waits=waits, inc=(cT if k == KD - 1 else None))
            p.op("vector", (lambda v, half=half: v.tensor_copy(
                lgT[0:8, half * 512:(half + 1) * 512], pf[half][0:8, 0:512])),
                waits=[(cT, tm)], inc=cV)
        v_lgT = cV.n
        for tt in range(8):
            tm = p.op("tensor", (lambda t, tt=tt: t.transpose(
                pf[4][:, tt * 8:(tt + 1) * 8], lgT[0:8, tt * 128:(tt + 1) * 128],
                id_f[0:8, 0:8])), waits=([(cV, v_lgT)] if tt == 0 else []),
                inc=(cT if tt == 7 else None))
        p.op("vector", lambda v: v.tensor_copy(lg_tm[:], pf[4][:, 0:64]),
             waits=[(cT, tm)], inc=cV)
        vrm = p.op("vector", lambda v: v.tensor_reduce(
            o3(se.ap()), r3(lg_tm.ap()), axis=AX.X, op=ALU.max), inc=cV)
        p.op("vector", lambda v: v.tensor_tensor(
            r3(g1.ap()), r3(lg_tm.ap()), o3(se.ap()).broadcast_to((128, 8, 8)),
            op=ALU.subtract), waits=[(cV, vrm)], inc=cV)
        v_exp8 = p.op("scalar", lambda a: a.activation(probs[:], g1[:], AF.Exp),
                      waits=[(cV, cV.n)], inc=cA)
        vred = p.op("vector", lambda v: v.tensor_reduce(
            o3(se.ap()), r3(probs.ap()), axis=AX.X, op=ALU.add),
            waits=[(cA, v_exp8)], inc=cV)
        vrec = p.op("vector", lambda v: v.reciprocal(ke[:], se[:]),
                    waits=[(cV, vred)], inc=cV)
        v_pr = p.op("vector", lambda v: v.tensor_tensor(
            r3(probs.ap()), r3(probs.ap()), o3(ke.ap()).broadcast_to((128, 8, 8)),
            op=ALU.mult), waits=[(cV, vrec)], inc=cV)
        if DEBUG:
            p.op("sync", lambda s: s.dma_start(out=dbg["dbg_probs"], in_=probs[:]),
                 waits=[(cV, v_pr)], inc=cS, amt=16)
        for e in range(NE):
            colb = r3(probs.ap())[:, :, e:e + 1].broadcast_to((128, 8, 8))
            p.op("vector", (lambda v, colb=colb: v.tensor_tensor(
                r3(g1.ap()), r3(probs.ap()), colb, op=ALU.is_gt)), inc=cV)
            p.op("vector", (lambda v, colb=colb: v.tensor_tensor(
                r3(g2.ap()), r3(probs.ap()), colb, op=ALU.is_equal)), inc=cV)
            p.op("vector", (lambda v, e=e: v.tensor_tensor(
                r3(g2.ap()), r3(g2.ap()),
                r3(tlt.ap()[:, e * 64:(e + 1) * 64]), op=ALU.mult)),
                waits=([(cS, v_pre)] if e == 0 else []), inc=cV)
            p.op("vector", lambda v: v.tensor_tensor(
                r3(g1.ap()), r3(g1.ap()), r3(g2.ap()), op=ALU.add), inc=cV)
            p.op("vector", lambda v: v.tensor_tensor(
                r3(g1.ap()), r3(g1.ap()), r3(probs.ap()), op=ALU.mult), inc=cV)
            vr2 = p.op("vector", lambda v: v.tensor_reduce(
                o3(se.ap()), r3(g1.ap()), axis=AX.X, op=ALU.add), inc=cV)
            vk2 = p.op("vector", lambda v: v.tensor_scalar(
                ke[:], se[:], TOP_P, None, op0=ALU.is_le), waits=[(cV, vr2)], inc=cV)
            v_wcol = p.op("vector", (lambda v, e=e: v.tensor_tensor(
                r3(w_all.ap())[:, :, e:e + 1], o3(ke.ap()),
                r3(probs.ap())[:, :, e:e + 1], op=ALU.mult)), waits=[(cV, vk2)], inc=cV)
        if DEBUG:
            p.op("sync", lambda s: s.dma_start(out=dbg["dbg_w"], in_=w_all[:]),
                 waits=[(cV, v_wcol)], inc=cS, amt=16)
        v_weT = 0
        for rnd in range(2):
            for i4 in range(4):
                tt = rnd * 4 + i4
                waits = []
                if i4 == 0:
                    waits = [(cV, v_wcol)] if rnd == 0 else [(cV, v_weT)]
                tm = p.op("tensor", (lambda t, tt=tt, i4=i4: t.transpose(
                    pf[5][0:8, i4 * 128:(i4 + 1) * 128], w_all[:, tt * 8:(tt + 1) * 8],
                    id_f[:, :])), waits=waits, inc=(cT if i4 == 3 else None))
            v_weT = p.op("vector", (lambda v, rnd=rnd: v.tensor_copy(
                w_eT[0:8, rnd * 512:(rnd + 1) * 512], pf[5][0:8, 0:512])),
                waits=[(cT, tm)], inc=cV)
        rt_stk.close()
        ln2_stk.close()
        snapR = [(cT, cT.n), (cV, cV.n), (cA, cA.n)]

        moe_stk = contextlib.ExitStack()
        sbM, psM = mk_alloc(moe_stk)
        pf.append(psM("pf6", [128, 512], F32))
        pf.append(psM("pf7", [128, 512], F32))
        hT = sbM("hT", [128, I_T * TC], BF16)
        Wb = sbM("Wb", [128, TC], BF16)
        gbuf = [sbM("gbuf0", [128, D], BF16), sbM("gbuf1", [128, D], BF16)]
        ubuf = [sbM("ubuf0", [128, D], BF16), sbM("ubuf1", [128, D], BF16)]
        dbuf = [sbM("dbuf0", [128, 2048], BF16), sbM("dbuf1", [128, 2048], BF16)]
        sg = [sbM("sg0", [128, TC], BF16), sbM("sg1", [128, TC], BF16)]
        hu = [sbM("hu0", [128, TC], BF16), sbM("hu1", [128, TC], BF16)]
        wrow = sbM("wrow", [1, TC], F32)

        free_g = [snapR[2], snapR[2]]
        free_u = [snapR[1], snapR[1]]
        free_d = [snapR[0], snapR[0]]
        bank_free = {i: snapR[1] for i in range(6)}
        bank_free[4] = snapR[2]
        bank_free[4] = snapR[1]
        bank_free[6] = (cV, 0)
        bank_free[7] = (cV, 0)
        wb_free = (cV, 0)
        sg_free = [snapR[1], snapR[1]]
        hu_free = [snapR[1], snapR[1]]
        wrow_free = (cT, 0)

        for e in range(NE):
            v_wr = p.op("sync", (lambda s, e=e: s.dma_start(
                out=wrow[:], in_=w_eT[e:e + 1, :])),
                waits=[(cV, v_weT), wrow_free] + (snapR if e == 0 else []), inc=cS, amt=16)
            for hf in range(2):
                bk = hf
                tpost = p.op("tensor", (lambda t, hf=hf, bk=bk: t.matmul(
                    pf[bk][:, 0:512], ones_row[0:1, :],
                    wrow[0:1, hf * 512:(hf + 1) * 512], start=True, stop=True)),
                    waits=[(cS, v_wr), bank_free[bk]], inc=cT)
                wrow_free = (cT, tpost)
                vpost = p.op("vector", (lambda v, hf=hf, bk=bk: v.tensor_copy(
                    Wb[:, hf * 512:(hf + 1) * 512], pf[bk][:, 0:512])),
                    waits=[(cT, tpost), wb_free], inc=cV)
                bank_free[bk] = (cV, vpost)
            for it in range(I_T):
                par = it % 2
                p.op("gpsimd", (lambda g, e=e, it=it, par=par: g.dma_start(
                    out=gbuf[par][:], in_=gate_s[e, it])), waits=[free_g[par]], inc=cW, amt=16)
                wu = p.op("gpsimd", (lambda g, e=e, it=it, par=par: g.dma_start(
                    out=ubuf[par][:], in_=up_s[e, it])), waits=[free_u[par]], inc=cW, amt=16)
                tg = [0, 0]
                tu = [0, 0]
                for hf in range(2):
                    bkg = par * 4 + hf
                    bku = par * 4 + 2 + hf
                    for k in range(KD):
                        waits = []
                        if k == 0:
                            waits = [(cW, wu), bank_free[bkg]]
                            if e == 0 and it < 2:
                                waits.append((cV, v_ln2))

                        def g_mm(t, par=par, hf=hf, k=k, bkg=bkg):
                            return t.matmul(pf[bkg][:, 0:512],
                                            gbuf[par][:, k * 128:(k + 1) * 128],
                                            x_ln2[:, k * TC + hf * 512: k * TC + hf * 512 + 512],
                                            start=(k == 0), stop=(k == KD - 1))
                        tg[hf] = p.op("tensor", g_mm, waits=waits,
                                      inc=(cT if k == KD - 1 else None))
                    for k in range(KD):
                        waits = [bank_free[bku]] if k == 0 else []

                        def u_mm(t, par=par, hf=hf, k=k, bku=bku):
                            return t.matmul(pf[bku][:, 0:512],
                                            ubuf[par][:, k * 128:(k + 1) * 128],
                                            x_ln2[:, k * TC + hf * 512: k * TC + hf * 512 + 512],
                                            start=(k == 0), stop=(k == KD - 1))
                        tu[hf] = p.op("tensor", u_mm, waits=waits,
                                      inc=(cT if k == KD - 1 else None))
                for hf in range(2):
                    bkg = par * 4 + hf
                    bku = par * 4 + 2 + hf
                    apost = p.op("scalar", (lambda a, par=par, hf=hf, bkg=bkg: a.activation(
                        sg[par][:, hf * 512:(hf + 1) * 512], pf[bkg][:, 0:512], AF.Silu)),
                        waits=[(cT, tg[hf]), sg_free[par]], inc=cA)
                    free_g[par] = (cA, apost)
                    bank_free[bkg] = (cA, apost)
                    v1 = p.op("vector", (lambda v, par=par, hf=hf, bku=bku: v.tensor_tensor(
                        hu[par][:, hf * 512:(hf + 1) * 512],
                        sg[par][:, hf * 512:(hf + 1) * 512],
                        pf[bku][:, 0:512], op=ALU.mult)),
                        waits=[(cA, apost), (cT, tu[hf]), hu_free[par]], inc=cV)
                    free_u[par] = (cV, v1)
                    bank_free[bku] = (cV, v1)
                    v2 = p.op("vector", (lambda v, it=it, par=par, hf=hf: v.tensor_tensor(
                        hT[:, it * TC + hf * 512: it * TC + hf * 512 + 512],
                        hu[par][:, hf * 512:(hf + 1) * 512],
                        Wb[:, hf * 512:(hf + 1) * 512], op=ALU.mult)),
                        waits=([(cV, vpost)] if it == 0 and hf == 0 else []), inc=cV)
                    hu_free[par] = (cV, v2)
                    if hf == 1:
                        sg_free[par] = (cV, v2)
            hT_done = v2
            wb_free = (cV, hT_done)
            for m in range(NM):
                bks = ((m % 2) * 2, (m % 2) * 2 + 1)
                td = [0, 0]
                for kg in range(2):
                    par = (2 * m + kg) % 2
                    wd = p.op("gpsimd", (lambda g, e=e, m=m, kg=kg, par=par: g.dma_start(
                        out=dbuf[par][:], in_=down_s[e, m][:, kg * 2048:(kg + 1) * 2048])),
                        waits=[free_d[par]], inc=cW, amt=16)
                    first_waits = [(cW, wd)]
                    if kg == 0:
                        first_waits += [(cV, hT_done), bank_free[bks[0]], bank_free[bks[1]]]
                    for kk in range(16):
                        k = kg * 16 + kk
                        for hf in range(2):
                            waits = first_waits if (kk == 0 and hf == 0) else []

                            def d_mm(t, par=par, k=k, kk=kk, hf=hf, bk=bks[hf]):
                                return t.matmul(pf[bk][:, 0:512],
                                                dbuf[par][:, kk * 128:(kk + 1) * 128],
                                                hT[:, k * TC + hf * 512: k * TC + hf * 512 + 512],
                                                start=(k == 0), stop=(k == I_T - 1))
                            td[hf] = p.op("tensor", d_mm, waits=waits,
                                          inc=(cT if k == I_T - 1 else None))
                    free_d[par] = (cT, p.op("tensor", None) or cT.n) if False else free_d[par]
                    free_d[par] = (cT, td[1] if kg == 1 else 0)
                    if kg == 0:
                        tmk = p.op("tensor", (lambda t: t.nop()), inc=cT)
                        free_d[par] = (cT, tmk)
                for hf in range(2):
                    vr = p.op("vector", (lambda v, m=m, hf=hf, bk=bks[hf]: v.tensor_tensor(
                        R[:, m * TC + hf * 512: m * TC + hf * 512 + 512],
                        R[:, m * TC + hf * 512: m * TC + hf * 512 + 512],
                        pf[bk][:, 0:512], op=ALU.add)), waits=[(cT, td[hf])], inc=cV)
                    bank_free[bks[hf]] = (cV, vr)

        p.op("sync", lambda s: s.dma_start(
            out=outT.rearrange("(k p) t -> p k t", k=KD, p=128), in_=R[:]),
            waits=[(cV, cV.n)], inc=cS, amt=16)
        p.op("sync", None, waits=[(cS, cS.n)])

        moe_stk.close()
        p6_stk = None
        tail_stk.close()
        bigA_stk.close()
        block = L0.enter_context(nc.Block())
        p.emit(block)
    return nc


def _pack(inputs):
    x = np.asarray(inputs["L0_attn_x"], dtype=np.float32)
    w_qkv = np.asarray(inputs["L0_attn_w_qkv"], dtype=np.float32)
    b_qkv = np.asarray(inputs["L0_attn_b_qkv"], dtype=np.float32)
    w_out = np.asarray(inputs["L0_attn_w_out"], dtype=np.float32)
    b_out = np.asarray(inputs["L0_attn_b_out"], dtype=np.float32)
    router_w = np.asarray(inputs["L0_ffn_router_w"], dtype=np.float32)
    gate_w = np.asarray(inputs["L0_ffn_gate_w"], dtype=np.float32)
    up_w = np.asarray(inputs["L0_ffn_up_w"], dtype=np.float32)
    down_w = np.asarray(inputs["L0_ffn_down_w"], dtype=np.float32)

    xT = np.ascontiguousarray(x.reshape(T, D).T)
    qkv_s = np.ascontiguousarray(
        w_qkv.reshape(QKV_M, 128, KD, 128).transpose(0, 3, 2, 1)).reshape(QKV_M, 128, D)
    wout_s = np.ascontiguousarray(
        w_out.reshape(NM, 128, KD, 128).transpose(0, 3, 2, 1)).reshape(NM, 128, D)
    bqkv = np.ascontiguousarray(b_qkv.reshape(QKV_M, 128).T)
    boutp = np.ascontiguousarray(b_out.reshape(NM, 128).T)
    router_p = np.ascontiguousarray(
        router_w.T.reshape(KD, 128, NE).transpose(1, 0, 2)).reshape(128, 128)
    gate_sv = np.ascontiguousarray(
        gate_w.reshape(NE, I_T, 128, KD, 128).transpose(0, 1, 4, 3, 2)).reshape(NE, I_T, 128, D)
    up_sv = np.ascontiguousarray(
        up_w.reshape(NE, I_T, 128, KD, 128).transpose(0, 1, 4, 3, 2)).reshape(NE, I_T, 128, D)
    down_sv = np.ascontiguousarray(
        down_w.reshape(NE, NM, 128, I_T, 128).transpose(0, 1, 4, 3, 2)).reshape(NE, NM, 128, I_T * 128)

    lam, decayq, dmaskT, decayk, lamc = _host_consts()
    shared = {
        "qkv_s": qkv_s, "bqkv": bqkv, "wout_s": wout_s, "bout": boutp,
        "router_p": router_p, "gate_s": gate_sv, "up_s": up_sv, "down_s": down_sv,
        "dmaskT": dmaskT, "decayq": decayq, "decayk": decayk, "lamc": lamc,
        "tlt": _tlt(), "ident": np.eye(128, dtype=np.float32),
    }
    in_maps = []
    for c in range(NCORES):
        m = dict(shared)
        m["xT"] = np.ascontiguousarray(xT[:, c * TC:(c + 1) * TC])
        m["coef"] = _host_coef(c, lam)
        in_maps.append(m)
    return in_maps


def kernel(**inputs):
    in_maps = _pack(inputs)
    nc = build()
    res = run_bass_kernel_spmd(nc, in_maps, core_ids=list(range(NCORES)))
    if DEBUG:
        _LAST["res"] = res
        _LAST["in_maps"] = in_maps
    parts = [np.asarray(res.results[c]["outT"], dtype=np.float32) for c in range(NCORES)]
    full = np.concatenate(parts, axis=1)
    return np.ascontiguousarray(full.T).reshape(1, T, D)


# version 03
# speedup vs baseline: 1.0043x; 1.0043x over previous
import contextlib

import numpy as np

import concourse.bass as bass
import concourse.mybir as mybir
from concourse.bass_utils import run_bass_kernel_spmd

F32 = mybir.dt.float32
BF16 = mybir.dt.bfloat16
AF = mybir.ActivationFunctionType
ALU = mybir.AluOpType
AX = mybir.AxisListType

NCORES = 8
D = 2048
T = 8192
TC = T // NCORES
KD = 16
H = 16
CH = 128
NCH = TC // CH
NE = 8
I_T = 32
NM = 16
QKV_M = 48
TOP_P = 0.9
EPS = 1e-5

DEBUG = False
_LAST = {}


class Ctr:
    __slots__ = ("h", "n")

    def __init__(self, h):
        self.h = h
        self.n = 0


class Prog:
    ENGINES = ("sync", "gpsimd", "tensor", "vector", "scalar")

    def __init__(self):
        self.ops = {e: [] for e in self.ENGINES}

    def op(self, eng, fn, waits=(), inc=None, amt=1):
        post = None
        if inc is not None:
            inc.n += amt
            post = inc.n
        ws = tuple((c, v) for (c, v) in waits if v > 0)
        self.ops[eng].append((fn, ws, inc, amt if inc is not None else 0))
        return post

    def emit(self, block):
        for eng in self.ENGINES:
            def mk(e):
                def body(h):
                    for fn, waits, inc, amt in self.ops[e]:
                        for (c, v) in waits:
                            h.wait_ge(c.h, v)
                        if fn is not None:
                            inst = fn(h)
                            if inc is not None:
                                inst.then_inc(inc.h, amt)
                return body
            getattr(block, eng)(mk(eng))


def _host_consts():
    hh = np.arange(H, dtype=np.float64)
    slopes = (2.0 ** -0.5) ** (hh + 1.0)
    lam = np.exp(-slopes)
    pos = np.arange(CH, dtype=np.float64)
    dq = lam[:, None] ** (pos[None, :] + 1.0)
    decayq = np.broadcast_to(dq.reshape(1, H * CH), (128, H * CH)).astype(np.float32)
    diff = pos[None, None, :] - pos[None, :, None]
    dm = np.where(diff >= 0, lam[:, None, None] ** np.maximum(diff, 0.0), 0.0)
    dmaskT = np.ascontiguousarray(dm.transpose(1, 0, 2).reshape(CH, H * CH)).astype(np.float32)
    dk = lam[:, None] ** (CH - 1.0 - pos[None, :])
    decayk = np.ascontiguousarray(
        np.broadcast_to(dk.T[:, :, None], (CH, H, CH)).reshape(CH, H * CH)).astype(np.float32)
    lc = lam ** CH
    lamc = np.ascontiguousarray(
        np.broadcast_to(lc[None, :, None], (CH, H, CH)).reshape(CH, H * CH)).astype(np.float32)
    return lam, decayq, dmaskT, decayk, lamc


def _host_coef(core, lam):
    out = np.zeros((128, NCORES), dtype=np.float64)
    for pp in range(128):
        h = pp // 8
        for r in range(core):
            out[pp, r] = lam[h] ** (TC * (core - 1 - r))
    return out.astype(np.float32)


def _tlt():
    out = np.zeros((128, NE * 64), dtype=np.float32)
    for e in range(NE):
        for tt in range(8):
            out[:, e * 64 + tt * 8: e * 64 + tt * 8 + e] = 1.0
    return out


def build():
    nc = bass.Bass("TRN2", target_bir_lowering=False, debug=False)

    xT = nc.dram_tensor("xT", [D, TC], F32, kind="ExternalInput").ap()
    qkv_s = nc.dram_tensor("qkv_s", [QKV_M, 128, D], F32, kind="ExternalInput").ap()
    bqkv_d = nc.dram_tensor("bqkv", [128, QKV_M], F32, kind="ExternalInput").ap()
    wout_s = nc.dram_tensor("wout_s", [NM, 128, D], F32, kind="ExternalInput").ap()
    bout_d = nc.dram_tensor("bout", [128, NM], F32, kind="ExternalInput").ap()
    router_d = nc.dram_tensor("router_p", [128, 128], F32, kind="ExternalInput").ap()
    gate_s = nc.dram_tensor("gate_s", [NE, I_T, 128, D], F32, kind="ExternalInput").ap()
    up_s = nc.dram_tensor("up_s", [NE, I_T, 128, D], F32, kind="ExternalInput").ap()
    down_s = nc.dram_tensor("down_s", [NE, NM, 128, I_T * 128], F32, kind="ExternalInput").ap()
    dmaskT_d = nc.dram_tensor("dmaskT", [128, H * CH], F32, kind="ExternalInput").ap()
    decayq_d = nc.dram_tensor("decayq", [128, H * CH], F32, kind="ExternalInput").ap()
    decayk_d = nc.dram_tensor("decayk", [128, H * CH], F32, kind="ExternalInput").ap()
    lamc_d = nc.dram_tensor("lamc", [128, H * CH], F32, kind="ExternalInput").ap()
    tlt_d = nc.dram_tensor("tlt", [128, NE * 64], F32, kind="ExternalInput").ap()
    coef_d = nc.dram_tensor("coef", [128, NCORES], F32, kind="ExternalInput").ap()
    ident_d = nc.dram_tensor("ident", [128, 128], F32, kind="ExternalInput").ap()
    outT = nc.dram_tensor("outT", [D, TC], F32, kind="ExternalOutput").ap()
    dbg = {}
    if DEBUG:
        for nm_, shape, dt in [
            ("dbg_xln1", [128, KD * TC], BF16), ("dbg_A", [128, H * CH], F32),
            ("dbg_P", [128, H * CH], F32), ("dbg_attnT", [128, H * TC], BF16),
            ("dbg_R1", [128, KD * TC], F32), ("dbg_probs", [128, 64], F32),
            ("dbg_w", [128, 64], F32), ("dbg_kT", [128, KD * TC], BF16),
            ("dbg_vtm", [128, NCH * D], BF16), ("dbg_xln2", [128, KD * TC], BF16),
            ("dbg_ag", [NCORES * 128, H * CH], BF16),
        ]:
            dbg[nm_] = nc.dram_tensor(nm_, shape, dt, kind="ExternalOutput").ap()

    a_loc = nc.dram_tensor("a_loc", [128, H * CH], BF16, kind="Internal")
    ag = nc.dram_tensor("ag", [NCORES * 128, H * CH], BF16, kind="Internal")
    p_tmp = nc.dram_tensor("p_tmp", [128, H * CH], F32, kind="Internal")

    p = Prog()
    L0 = contextlib.ExitStack()

    def mk_alloc(stk):
        return (lambda name, shape, dt: stk.enter_context(nc.sbuf_tensor(name, shape, dt)),
                lambda name, shape, dt: stk.enter_context(nc.psum_tensor(name, shape, dt)))

    with L0:
        sb0, ps0 = mk_alloc(L0)
        sem = lambda name: Ctr(L0.enter_context(nc.semaphore(name)))
        cS = sem("cS")
        cW = sem("cW")
        cT = sem("cT")
        cV = sem("cV")
        cA = sem("cA")
        cCC = sem("cCC")

        ones_b = sb0("ones_b", [128, 1], BF16)
        ones_row = sb0("ones_row", [1, 128], F32)
        id_f = sb0("id_f", [128, 128], F32)
        id_b = sb0("id_b", [128, 128], BF16)
        bqkv = sb0("bqkv_sb", [128, QKV_M], F32)
        bout = sb0("bout_sb", [128, NM], F32)
        router = sb0("router_sb", [128, 128], BF16)
        tlt = sb0("tlt_sb", [128, NE * 64], F32)
        coef = sb0("coef_sb", [128, NCORES], F32)
        pf = [ps0("pf%d" % i, [128, 512], F32) for i in range(6)]

        v_pre = 0
        for fn in [
            lambda s: s.dma_start(out=bqkv[:], in_=bqkv_d),
            lambda s: s.dma_start(out=bout[:], in_=bout_d),
            lambda s: s.dma_start(out=tlt[:], in_=tlt_d),
            lambda s: s.dma_start(out=coef[:], in_=coef_d),
            lambda s: s.dma_start(out=id_f[:], in_=ident_d),
        ]:
            v_pre = p.op("sync", fn, inc=cS, amt=16)
        v_idb = p.op("gpsimd", lambda g: g.dma_start(out=id_b[:], in_=ident_d), inc=cW, amt=16)
        v_rtr = p.op("gpsimd", lambda g: g.dma_start(out=router[:], in_=router_d), inc=cW, amt=16)
        p.op("vector", lambda v: v.memset(ones_b[:], 1.0), inc=cV)
        p.op("vector", lambda v: v.memset(ones_row[:], 1.0), inc=cV)
        v_ones = cV.n

        def emit_ln(src, dst, xb, sqb, rows, nsc, ready_k, extra0=()):
            xb_free = [(cT, 0), (cT, 0)]
            sq_free = [(cT, 0), (cT, 0)]
            for half in range(2):
                for k in range(KD):
                    par = k % 2
                    vw = [xb_free[par]]
                    aw = [sq_free[par]]
                    if half == 0:
                        vw = vw + list(ready_k[k])
                        aw = aw + list(ready_k[k])
                        if k == 0:
                            vw = vw + [(cV, v_ones)] + list(extra0)
                            aw = aw + list(extra0)
                    vc = p.op("vector", (lambda v, k=k, half=half, par=par: v.tensor_copy(
                        xb[par][:, 0:512],
                        src[:, k * TC + half * 512: k * TC + half * 512 + 512])),
                        waits=vw, inc=cV)
                    va = p.op("scalar", (lambda a, k=k, half=half, par=par: a.activation(
                        sqb[par][:, 0:512],
                        src[:, k * TC + half * 512: k * TC + half * 512 + 512],
                        AF.Square)), waits=aw, inc=cA)
                    tm_s = p.op("tensor", (lambda t, k=k, par=par: t.matmul(
                        pf[0][0:1, 0:512], ones_b[:, 0:1], xb[par][:, 0:512],
                        start=(k == 0), stop=(k == KD - 1))), waits=[(cV, vc)], inc=cT)
                    xb_free[par] = (cT, tm_s)
                    tm_q = p.op("tensor", (lambda t, k=k, par=par: t.matmul(
                        pf[1][0:1, 0:512], ones_b[:, 0:1], sqb[par][:, 0:512],
                        start=(k == 0), stop=(k == KD - 1))), waits=[(cA, va)], inc=cT)
                    sq_free[par] = (cT, tm_q)
                ve = p.op("vector", (lambda v, half=half: v.tensor_scalar(
                    rows[0:1, half * 512: half * 512 + 512], pf[0][0:1, 0:512],
                    1.0 / D, None, op0=ALU.mult)), waits=[(cT, tm_s)], inc=cV)
                ve2 = p.op("vector", (lambda v, half=half: v.tensor_scalar(
                    rows[0:1, 1024 + half * 512: 1024 + half * 512 + 512], pf[1][0:1, 0:512],
                    1.0 / D, None, op0=ALU.mult)), waits=[(cT, tm_q)], inc=cV)
                if half == 0:
                    p.op("tensor", None, waits=[(cV, ve2)])
            p.op("vector", lambda v: v.tensor_tensor(
                rows[0:1, 2048:3072], rows[0:1, 0:1024], rows[0:1, 0:1024], op=ALU.mult), inc=cV)
            v_var = p.op("vector", lambda v: v.scalar_tensor_tensor(
                rows[0:1, 3072:4096], rows[0:1, 1024:2048], EPS, rows[0:1, 2048:3072],
                op0=ALU.add, op1=ALU.subtract), inc=cV)
            a_sd = p.op("scalar", lambda a: a.activation(
                rows[0:1, 2048:3072], rows[0:1, 3072:4096], AF.Sqrt),
                waits=[(cV, v_var)], inc=cA)
            v_rstd = p.op("vector", lambda v: v.reciprocal(
                rows[0:1, 3072:4096], rows[0:1, 2048:3072]), waits=[(cA, a_sd)], inc=cV)
            for half in range(2):
                p.op("tensor", (lambda t, half=half: t.matmul(
                    pf[2 + half][:, 0:512], ones_row[0:1, :],
                    rows[0:1, half * 512: half * 512 + 512], start=True, stop=True)),
                    waits=[(cV, v_rstd)] if half == 0 else [], inc=cT)
                t_bc = p.op("tensor", (lambda t, half=half: t.matmul(
                    pf[4 + half][:, 0:512], ones_row[0:1, :],
                    rows[0:1, 3072 + half * 512: 3072 + half * 512 + 512],
                    start=True, stop=True)), inc=cT)
            last = 0
            for k in range(KD):
                for half in range(2):
                    par = k % 2
                    p.op("vector", (lambda v, k=k, half=half, par=par: v.tensor_tensor(
                        nsc[par][:, 0:512],
                        src[:, k * TC + half * 512: k * TC + half * 512 + 512],
                        pf[2 + half][:, 0:512], op=ALU.subtract)),
                        waits=([(cT, t_bc)] if (k == 0 and half == 0) else []), inc=cV)
                    last = p.op("vector", (lambda v, k=k, half=half, par=par: v.tensor_tensor(
                        dst[:, k * TC + half * 512: k * TC + half * 512 + 512],
                        nsc[par][:, 0:512],
                        pf[4 + half][:, 0:512], op=ALU.mult)), inc=cV)
            return last

        def emit_proj(wdram, m0, m1, act, wbuf, wbuf_free, evac, act_ready, bank_init):
            bank_free = dict(bank_init)
            for i, m in enumerate(range(m0, m1)):
                par = i % 2
                wload = p.op("gpsimd", (lambda g, m=m, par=par: g.dma_start(
                    out=wbuf[par][:], in_=wdram[m])), waits=[wbuf_free[par]], inc=cW, amt=16)
                tm = [0, 0]
                for half in range(2):
                    bk = par * 2 + half
                    for k in range(KD):
                        waits = []
                        if k == 0:
                            waits = [bank_free[bk]]
                            if half == 0:
                                waits.append((cW, wload))
                            if i < 2:
                                waits = waits + list(act_ready)

                        def mm(t, m=m, par=par, half=half, k=k, bk=bk):
                            return t.matmul(pf[bk][:, 0:512],
                                            wbuf[par][:, k * 128:(k + 1) * 128],
                                            act[:, k * TC + half * 512: k * TC + half * 512 + 512],
                                            start=(k == 0), stop=(k == KD - 1))
                        tmv = p.op("tensor", mm, waits=waits, inc=(cT if k == KD - 1 else None))
                    tm[half] = tmv
                wbuf_free[par] = (cT, tm[1])
                for half in range(2):
                    bk = par * 2 + half
                    bank_free[bk] = evac(i, m, half, bk, [(cT, tm[half])])

        bigA_stk = contextlib.ExitStack()
        sbBA, _ = mk_alloc(bigA_stk)
        bigA = sbBA("bigA", [128, KD * TC], BF16)
        x_ln1 = bigA

        base_stk = contextlib.ExitStack()
        sbB, _ = mk_alloc(base_stk)
        decayq = sbB("decayq_sb", [128, H * CH], F32)
        lamct = sbB("lamc_sb", [128, H * CH], F32)
        S_run = sbB("S_run", [128, H * CH], F32)
        S_bf = sbB("S_bf", [128, H * CH], BF16)
        qd_s = sbB("qd_s", [128, H * CH], BF16)
        s_tmp = sbB("s_tmp", [128, 512], F32)
        qT = sbB("qT", [128, KD * TC], BF16)

        v_tb0 = 0
        for dst_t, src_t in [(decayq, decayq_d), (lamct, lamc_d)]:
            v_tb0 = p.op("sync", (lambda s, dst_t=dst_t, src_t=src_t: s.dma_start(
                out=dst_t[:], in_=src_t)), inc=cS, amt=16)

        ln1_stk = contextlib.ExitStack()
        sbL, _ = mk_alloc(ln1_stk)
        x_f32 = sbL("x_f32", [128, KD * TC], F32)
        xb1 = [sbL("xb1_0", [128, 512], BF16), sbL("xb1_1", [128, 512], BF16)]
        sqb = [sbL("sqb0", [128, 512], BF16), sbL("sqb1", [128, 512], BF16)]
        rows = sbL("rows", [1, 4096], F32)
        nsc = [sbL("nsc0", [128, 512], F32), sbL("nsc1", [128, 512], F32)]
        xload = []
        for g4 in range(4):
            xload.append(p.op("sync", (lambda s, g4=g4: s.dma_start(
                out=x_f32[:, g4 * 4 * TC:(g4 + 1) * 4 * TC],
                in_=xT[g4 * 512:(g4 + 1) * 512, :].rearrange("(k p) t -> p k t", k=4, p=128))),
                inc=cS, amt=16))
        ready_k1 = [[(cS, xload[k // 4])] for k in range(KD)]
        v_ln1 = emit_ln(x_f32, x_ln1, xb1, sqb, rows, nsc, ready_k1)
        if DEBUG:
            p.op("sync", lambda s: s.dma_start(out=dbg["dbg_xln1"], in_=x_ln1[:]),
                 waits=[(cV, v_ln1)], inc=cS, amt=16)
        ln1_t, ln1_v, ln1_a = cT.n, cV.n, cA.n
        ln1_stk.close()
        snap1 = [(cT, ln1_t), (cV, ln1_v), (cA, ln1_a)]

        scan_stk = contextlib.ExitStack()
        sbA, psA = mk_alloc(scan_stk)
        dmaskT = sbA("dmaskT_sb", [128, H * CH], F32)
        decayk = sbA("decayk_sb", [128, H * CH], F32)
        kT = sbA("kT", [128, KD * TC], BF16)
        v_tm = sbA("v_tm", [128, NCH * D], BF16)
        kd_tm = sbA("kd_tm", [128, H * CH], BF16)
        sT_s = sbA("sT_s", [128, H * CH], BF16)
        vtmp = [sbA("vtmp0", [128, TC], BF16), sbA("vtmp1", [128, TC], BF16)]
        wbufA = [sbA("wbufA0", [128, D], BF16), sbA("wbufA1", [128, D], BF16)]
        ptb = [psA("ptb0", [128, 1024], BF16), psA("ptb1", [128, 1024], BF16)]

        v_tabs = 0
        for i, (dst_t, src_t) in enumerate([(dmaskT, dmaskT_d), (decayk, decayk_d)]):
            v_tabs = p.op("sync", (lambda s, dst_t=dst_t, src_t=src_t: s.dma_start(
                out=dst_t[:], in_=src_t)), waits=(snap1 if i == 0 else []), inc=cS, amt=16)

        wbuf_free = [snap1[0], snap1[0]]
        bank_free_p = {i: snap1[1] for i in range(4)}
        tb_free = [snap1[1], snap1[1]]
        vtmp_free = [(cT, 0), (cT, 0)]
        vtm_last = 0
        for h in range(H):
            m = 32 + h
            par = h % 2
            wload = p.op("gpsimd", (lambda g, m=m, par=par: g.dma_start(
                out=wbufA[par][:], in_=qkv_s[m])), waits=[wbuf_free[par]], inc=cW, amt=16)
            tmh = [0, 0]
            for half in range(2):
                bk = par * 2 + half
                for k in range(KD):
                    waits = []
                    if k == 0:
                        waits = [bank_free_p[bk]]
                        if half == 0:
                            waits.append((cW, wload))
                        if h < 2:
                            waits.append((cV, v_ln1))

                    def mm(t, m=m, par=par, half=half, k=k, bk=bk):
                        return t.matmul(pf[bk][:, 0:512],
                                        wbufA[par][:, k * 128:(k + 1) * 128],
                                        x_ln1[:, k * TC + half * 512: k * TC + half * 512 + 512],
                                        start=(k == 0), stop=(k == KD - 1))
                    tmv = p.op("tensor", mm, waits=waits, inc=(cT if k == KD - 1 else None))
                tmh[half] = tmv
            wbuf_free[par] = (cT, tmh[1])
            va = 0
            for half in range(2):
                bk = par * 2 + half
                va = p.op("scalar", (lambda a, m=m, half=half, bk=bk, par=par: a.activation(
                    vtmp[par][:, half * 512:(half + 1) * 512],
                    pf[bk][:, 0:512], AF.Identity, bias=bqkv[:, m:m + 1], scale=1.0)),
                    waits=[(cT, tmh[half])] + ([vtmp_free[par], (cS, v_pre)] if half == 0 else []),
                    inc=cA)
                bank_free_p[bk] = (cA, va)
            tmt = 0
            for j in range(NCH):
                w0 = []
                if j == 0:
                    w0 = [(cA, va), tb_free[par]]
                    if h == 0:
                        w0.append((cW, v_idb))
                tmt = p.op("tensor", (lambda t, j=j, par=par: t.transpose(
                    ptb[par][:, j * 128:(j + 1) * 128],
                    vtmp[par][:, j * 128:(j + 1) * 128], id_b[:, :])),
                    waits=w0, inc=(cT if j == NCH - 1 else None))
            vtmp_free[par] = (cT, tmt)
            vtm_last = p.op("vector", (lambda v, h=h, par=par: v.tensor_copy(
                v_tm.ap().rearrange("p (j he) -> p j he", j=NCH, he=D)[:, :, h * 128:(h + 1) * 128],
                ptb[par].ap().rearrange("p (j e) -> p j e", j=NCH, e=128))),
                waits=[(cT, tmt)], inc=cV)
            tb_free[par] = (cV, vtm_last)

        def evac_kv(i, m, half, bk, waits):
            mrel = m - 16
            va = p.op("scalar", (lambda a, m=m, half=half, bk=bk, mrel=mrel: a.activation(
                kT[:, mrel * TC + half * 512: mrel * TC + half * 512 + 512],
                pf[bk][:, 0:512], AF.Identity, bias=bqkv[:, m:m + 1], scale=1.0)),
                waits=waits, inc=cA)
            return (cA, va)

        def evac_q(i, m, half, bk, waits):
            va = p.op("scalar", (lambda a, m=m, half=half, bk=bk: a.activation(
                qT[:, m * TC + half * 512: m * TC + half * 512 + 512],
                pf[bk][:, 0:512], AF.Identity, bias=bqkv[:, m:m + 1], scale=1.0)),
                waits=waits, inc=cA)
            return (cA, va)

        emit_proj(qkv_s, 16, 32, x_ln1, wbufA, wbuf_free, evac_kv,
                  [(cV, v_ln1)], dict(bank_free_p))
        emit_proj(qkv_s, 0, 16, x_ln1, wbufA, wbuf_free, evac_q,
                  [(cV, v_ln1)], {i: (cA, cA.n) for i in range(4)})
        v_proj_a = cA.n
        v_proj_t = cT.n

        v_s0 = p.op("vector", lambda v: v.memset(S_run[:], 0.0),
                    waits=[(cA, ln1_a), (cT, ln1_t)], inc=cV)
        sc_free = [(cV, 0), (cV, 0)]
        po_free = [(cA, v_proj_a), (cA, v_proj_a)]
        kv_free = [(cV, v_ln1), (cV, v_ln1)]

        def emit_kd(jj, first_waits):
            outs = []
            for g in range(2):
                tmx = 0
                for hh in range(8):
                    hx = g * 8 + hh
                    waits = []
                    if hh == 0:
                        waits = [tb_free[g]]
                        if g == 0:
                            waits = waits + first_waits
                    tmx = p.op("tensor", (lambda t, hx=hx, hh=hh, g=g, jj=jj: t.transpose(
                        ptb[g][:, hh * 128:(hh + 1) * 128],
                        kT[:, hx * TC + jj * 128: hx * TC + jj * 128 + 128],
                        id_b[:, :])), waits=waits, inc=(cT if hh == 7 else None))
                ve = p.op("vector", (lambda v, g=g: v.tensor_tensor(
                    kd_tm[:, g * 1024:(g + 1) * 1024], ptb[g][:, :],
                    decayk[:, g * 1024:(g + 1) * 1024], op=ALU.mult)),
                    waits=[(cT, tmx)], inc=cV)
                tb_free[g] = (cV, ve)
                outs.append(ve)
            return outs

        def emit_kv_s(jj, kd_ready, vtm_ready):
            for g4 in range(4):
                kb = 4 + (g4 % 2)
                tmx = 0
                for hh in range(4):
                    hx = g4 * 4 + hh
                    waits = []
                    if hh == 0:
                        waits = [kv_free[g4 % 2], (cV, kd_ready[0 if g4 < 2 else 1])]
                        if vtm_ready:
                            waits.append((cV, vtm_ready))
                    tmx = p.op("tensor", (lambda t, hx=hx, jj=jj, kb=kb, hh=hh: t.matmul(
                        pf[kb][:, hh * 128:(hh + 1) * 128],
                        kd_tm[:, hx * 128:(hx + 1) * 128],
                        v_tm[:, jj * D + hx * 128: jj * D + (hx + 1) * 128],
                        start=True, stop=True, skip_group_check=True)),
                        waits=waits, inc=(cT if hh == 3 else None))
                p.op("vector", (lambda v, g4=g4: v.tensor_tensor(
                    s_tmp[:, 0:512], S_run[:, g4 * 512:(g4 + 1) * 512],
                    lamct[:, g4 * 512:(g4 + 1) * 512], op=ALU.mult)), inc=cV)
                v2 = p.op("vector", (lambda v, g4=g4, kb=kb: v.tensor_tensor(
                    S_run[:, g4 * 512:(g4 + 1) * 512], s_tmp[:, 0:512],
                    pf[kb][:, 0:512], op=ALU.add)), waits=[(cT, tmx)], inc=cV)
                kv_free[g4 % 2] = (cV, v2)

        attnT = bigA
        t_prev_j = 0
        for j in range(NCH):
            w_sbf = [(cT, t_prev_j)] if j > 0 else [(cV, v_s0)]
            v_sbf = p.op("vector", (lambda v: v.tensor_copy(S_bf[:], S_run[:])),
                         waits=w_sbf, inc=cV)
            v_qd = p.op("vector", (lambda v, j=j: v.tensor_tensor(
                qd_s.ap().rearrange("p (h t) -> p h t", h=H, t=CH),
                qT.ap().rearrange("p (h t) -> p h t", h=H, t=TC)[:, :, j * 128:(j + 1) * 128],
                decayq.ap().rearrange("p (h t) -> p h t", h=H, t=CH), op=ALU.mult)),
                waits=([(cT, t_prev_j)] if j > 0 else [(cA, v_proj_a), (cS, v_tb0)]), inc=cV)
            kde = emit_kd(j, [(cS, v_tabs), (cT, v_proj_t)] if j == 0 else [])
            sT_ready = [0, 0, 0, 0]
            for g4 in range(4):
                sb_ = g4 % 2
                tmx = 0
                for hh in range(4):
                    hx = g4 * 4 + hh
                    waits = [sc_free[sb_]] if hh == 0 else []
                    tmx = p.op("tensor", (lambda t, hx=hx, j=j, sb_=sb_, hh=hh: t.matmul(
                        pf[sb_][:, hh * 128:(hh + 1) * 128],
                        kT[:, hx * TC + j * 128: hx * TC + j * 128 + 128],
                        qT[:, hx * TC + j * 128: hx * TC + j * 128 + 128],
                        start=True, stop=True, skip_group_check=True)),
                        waits=waits, inc=(cT if hh == 3 else None))
                vm = p.op("vector", (lambda v, g4=g4, sb_=sb_: v.tensor_tensor(
                    sT_s[:, g4 * 512:(g4 + 1) * 512], pf[sb_][:, 0:512],
                    dmaskT[:, g4 * 512:(g4 + 1) * 512], op=ALU.mult)),
                    waits=[(cT, tmx)], inc=cV)
                sc_free[sb_] = (cV, vm)
                sT_ready[g4] = vm
            for g4 in range(4):
                pb_ = 2 + (g4 % 2)
                tmx = 0
                for hh in range(4):
                    hx = g4 * 4 + hh
                    waits = []
                    if hh == 0:
                        waits = [po_free[g4 % 2], (cV, sT_ready[g4])]
                        if g4 == 0:
                            waits.append((cV, v_qd))
                            waits.append((cV, v_sbf))
                    p.op("tensor", (lambda t, hx=hx, j=j, pb_=pb_, hh=hh: t.matmul(
                        pf[pb_][:, hh * 128:(hh + 1) * 128],
                        v_tm[:, j * D + hx * 128: j * D + (hx + 1) * 128],
                        sT_s[:, hx * 128:(hx + 1) * 128],
                        start=True, stop=False, skip_group_check=True)), waits=waits)
                    tmx = p.op("tensor", (lambda t, hx=hx, pb_=pb_, hh=hh: t.matmul(
                        pf[pb_][:, hh * 128:(hh + 1) * 128],
                        S_bf[:, hx * 128:(hx + 1) * 128],
                        qd_s[:, hx * 128:(hx + 1) * 128],
                        start=False, stop=True, skip_group_check=True)),
                        inc=(cT if hh == 3 else None))
                va = p.op("scalar", (lambda a, g4=g4, j=j, pb_=pb_: a.activation(
                    attnT.ap().rearrange("p (h t) -> p h t", h=H, t=TC)[:, g4 * 4:(g4 + 1) * 4, j * 128:(j + 1) * 128],
                    pf[pb_].ap().rearrange("p (h t) -> p h t", h=4, t=128),
                    AF.Identity)), waits=[(cT, tmx)], inc=cA)
                po_free[g4 % 2] = (cA, va)
            emit_kv_s(j, kde, vtm_last if j == 0 else 0)
            t_prev_j = cT.n
        v_scan_t = cT.n
        v_scan_v = cV.n
        v_scan_a = cA.n

        aL = a_loc.ap().rearrange("(h2 dhi) (dlo e) -> h2 dhi dlo e", h2=16, dhi=8, dlo=16, e=128)
        v_exp = 0
        for h in range(H):
            v_exp = p.op("gpsimd", (lambda g, h=h: g.dma_start(
                out=aL[h], in_=S_run[:, h * 128:(h + 1) * 128])),
                waits=([(cV, v_scan_v)] if h == 0 else []), inc=cW, amt=16)
        if DEBUG:
            p.op("sync", lambda s: s.dma_start(out=dbg["dbg_A"], in_=S_run[:]),
                 waits=[(cV, v_scan_v)], inc=cS, amt=16)
            p.op("sync", lambda s: s.dma_start(out=dbg["dbg_kT"], in_=kT[:]), inc=cS, amt=16)
            p.op("sync", lambda s: s.dma_start(out=dbg["dbg_vtm"], in_=v_tm[:]), inc=cS, amt=16)
        v_cc = p.op("gpsimd", lambda g: g.collective_compute(
            "AllGather", ALU.bypass, replica_groups=[list(range(NCORES))],
            ins=[a_loc.ap()], outs=[ag.ap()]), waits=[(cW, v_exp)], inc=cCC)

        scan_stk.close()
        snapS = [(cT, v_scan_t), (cV, v_scan_v), (cA, v_scan_a)]

        exch_stk = contextlib.ExitStack()
        sbE, _ = mk_alloc(exch_stk)
        ar_sb = sbE("ar_sb", [128, 1024], BF16)
        P_sb = sbE("P_sb", [128, 1024], F32)
        v_pexp = 0
        for half in range(2):
            v_cmb = 0
            for r in range(NCORES):
                war = p.op("gpsimd", (lambda g, r=r, half=half: g.dma_start(
                    out=ar_sb[:], in_=ag[r * 128:(r + 1) * 128, half * 1024:(half + 1) * 1024])),
                    waits=([(cCC, 1)] + snapS if (r == 0 and half == 0) else [(cV, v_cmb)]),
                    inc=cW, amt=16)
                if r == 0:
                    v_cmb = p.op("vector", (lambda v, r=r: v.tensor_scalar(
                        P_sb[:], ar_sb[:], coef[:, r:r + 1], None, op0=ALU.mult)),
                        waits=[(cW, war), (cS, v_pre)] +
                              (snapS if half == 0 else [(cS, v_pexp)]), inc=cV)
                else:
                    v_cmb = p.op("vector", (lambda v, r=r: v.scalar_tensor_tensor(
                        P_sb[:], ar_sb[:], coef[:, r:r + 1], P_sb[:],
                        op0=ALU.mult, op1=ALU.add)), waits=[(cW, war)], inc=cV)
            pT = p_tmp.ap().rearrange("(dhi dlo) (h2 e) -> h2 dhi dlo e",
                                      dhi=8, dlo=16, h2=16, e=128)
            for h in range(H):
                v_pexp = p.op("sync", (lambda s, h=h, half=half, pT=pT: s.dma_start(
                    out=pT[h][:, half * 8:(half + 1) * 8, :],
                    in_=P_sb.ap()[h * 8:(h + 1) * 8, :].rearrange(
                        "dhi (dlo e) -> dhi dlo e", dlo=8, e=128))),
                    waits=([(cV, v_cmb)] if h == 0 else []), inc=cS, amt=16)
        v_pimp = p.op("sync", lambda s: s.dma_start(out=S_run[:], in_=p_tmp.ap()),
                      waits=[(cS, v_pexp), (cW, v_exp), (cV, v_scan_v)], inc=cS, amt=16)
        exch_stk.close()
        v_dbgP = 0
        if DEBUG:
            p.op("sync", lambda s: s.dma_start(out=dbg["dbg_ag"], in_=ag.ap()),
                 waits=[(cCC, 1)], inc=cS, amt=16)
            v_dbgP = p.op("sync", lambda s: s.dma_start(out=dbg["dbg_P"], in_=S_run[:]),
                 waits=[(cS, v_pimp)], inc=cS, amt=16)

        t_corr_prev = 0
        v_corr = 0
        for j in range(NCH):
            if j > 0:
                p.op("vector", lambda v: v.tensor_tensor(
                    S_run[:], S_run[:], lamct[:], op=ALU.mult), inc=cV)
            v_pbf = p.op("vector", (lambda v: v.tensor_copy(S_bf[:], S_run[:])),
                         waits=([(cS, max(v_pimp, v_dbgP)), (cT, v_scan_t)] if j == 0 else
                                [(cT, t_corr_prev)]), inc=cV)
            v_qd = p.op("vector", (lambda v, j=j: v.tensor_tensor(
                qd_s.ap().rearrange("p (h t) -> p h t", h=H, t=CH),
                qT.ap().rearrange("p (h t) -> p h t", h=H, t=TC)[:, :, j * 128:(j + 1) * 128],
                decayq.ap().rearrange("p (h t) -> p h t", h=H, t=CH), op=ALU.mult)),
                waits=([(cT, t_corr_prev)] if j > 0 else []), inc=cV)
            for g4 in range(4):
                pb_ = 2 + (g4 % 2)
                tmx = 0
                for hh in range(4):
                    hx = g4 * 4 + hh
                    waits = []
                    if hh == 0:
                        waits = [po_free[g4 % 2]]
                        if g4 == 0:
                            waits += [(cV, v_pbf), (cV, v_qd)]
                    tmx = p.op("tensor", (lambda t, hx=hx, pb_=pb_, hh=hh: t.matmul(
                        pf[pb_][:, hh * 128:(hh + 1) * 128],
                        S_bf[:, hx * 128:(hx + 1) * 128],
                        qd_s[:, hx * 128:(hx + 1) * 128],
                        start=True, stop=True, skip_group_check=True)),
                        waits=waits, inc=(cT if hh == 3 else None))
                v_corr = p.op("vector", (lambda v, g4=g4, j=j, pb_=pb_: v.tensor_tensor(
                    attnT.ap().rearrange("p (h t) -> p h t", h=H, t=TC)[:, g4 * 4:(g4 + 1) * 4, j * 128:(j + 1) * 128],
                    attnT.ap().rearrange("p (h t) -> p h t", h=H, t=TC)[:, g4 * 4:(g4 + 1) * 4, j * 128:(j + 1) * 128],
                    pf[pb_].ap().rearrange("p (h t) -> p h t", h=4, t=128), op=ALU.add)),
                    waits=[(cT, tmx)] + ([(cA, v_scan_a)] if j == 0 and g4 == 0 else []),
                    inc=cV)
                po_free[g4 % 2] = (cV, v_corr)
            t_corr_prev = cT.n
        if DEBUG:
            p.op("sync", lambda s: s.dma_start(out=dbg["dbg_attnT"], in_=attnT[:]),
                 waits=[(cV, v_corr)], inc=cS, amt=16)
        base_stk.close()
        snapC = [(cT, cT.n), (cV, cV.n), (cA, cA.n)]

        tail_stk = contextlib.ExitStack()
        sbT, _ = mk_alloc(tail_stk)
        R = sbT("R", [128, KD * TC], F32)
        w_eT = sbT("w_eT", [8, TC], F32)

        mf_stk = contextlib.ExitStack()
        sbMF, psMF = mk_alloc(mf_stk)
        gbuf = [sbMF("gbuf0", [128, D], BF16), sbMF("gbuf1", [128, D], BF16)]
        ubuf = [sbMF("ubuf0", [128, D], BF16), sbMF("ubuf1", [128, D], BF16)]
        dbuf = [sbMF("dbuf0", [128, 2048], BF16), sbMF("dbuf1", [128, 2048], BF16)]
        sg = [sbMF("sg0", [128, TC], BF16), sbMF("sg1", [128, TC], BF16)]
        hu = [sbMF("hu0", [128, TC], BF16), sbMF("hu1", [128, TC], BF16)]
        Wb = sbMF("Wb", [128, TC], BF16)
        wrow = sbMF("wrow", [1, TC], F32)
        pf.append(psMF("pf6", [128, 512], F32))
        pf.append(psMF("pf7", [128, 512], F32))

        p6_stk = contextlib.ExitStack()
        sbP6, _ = mk_alloc(p6_stk)
        wbufB = [sbP6("wbufB0", [128, D], BF16), sbP6("wbufB1", [128, D], BF16)]
        xrb = [sbP6("xrb0", [128, TC], F32), sbP6("xrb1", [128, TC], F32)]
        xr_ready = {}
        for m in range(2):
            xr_ready[m] = p.op("sync", (lambda s, m=m: s.dma_start(
                out=xrb[m][:], in_=xT[m * 128:(m + 1) * 128, :])),
                waits=(snapC if m == 0 else []), inc=cS, amt=16)

        r_ready = {}

        def evac_proj(i, m, half, bk, waits):
            par = m % 2
            vv = p.op("vector", (lambda v, m=m, half=half, bk=bk, par=par: v.scalar_tensor_tensor(
                R[:, m * TC + half * 512: m * TC + half * 512 + 512],
                pf[bk][:, 0:512], bout[:, m:m + 1],
                xrb[par][:, half * 512: half * 512 + 512],
                op0=ALU.add, op1=ALU.add)),
                waits=waits + [(cS, xr_ready[m])], inc=cV)
            if half == 1:
                r_ready[m] = vv
                if m + 2 < NM:
                    xr_ready[m + 2] = p.op("sync", (lambda s, m=m, par=par: s.dma_start(
                        out=xrb[par][:], in_=xT[(m + 2) * 128:(m + 3) * 128, :])),
                        waits=[(cV, vv)], inc=cS, amt=16)
            return (cV, vv)

        wbufB_free = [snapC[0], snapC[0]]
        emit_proj(wout_s, 0, NM, attnT, wbufB, wbufB_free, evac_proj,
                  [(cV, v_corr)] + snapC, {i: snapC[1] for i in range(4)})
        v_R1 = cV.n
        if DEBUG:
            p.op("sync", lambda s: s.dma_start(out=dbg["dbg_R1"], in_=R[:]),
                 waits=[(cV, v_R1)], inc=cS, amt=16)

        x_ln2 = bigA
        ln2_stk = contextlib.ExitStack()
        sbL2, _ = mk_alloc(ln2_stk)
        xb2 = [sbL2("xb2_0", [128, 512], BF16), sbL2("xb2_1", [128, 512], BF16)]
        sqb2 = [sbL2("sqb2_0", [128, 512], BF16), sbL2("sqb2_1", [128, 512], BF16)]
        rows2 = sbL2("rows2", [1, 4096], F32)
        nsc2 = [sbL2("nsc2_0", [128, 512], F32), sbL2("nsc2_1", [128, 512], F32)]
        ready_k2 = [[(cV, r_ready[k])] for k in range(KD)]
        v_ln2 = emit_ln(R, x_ln2, xb2, sqb2, rows2, nsc2, ready_k2, extra0=snapC)
        if DEBUG:
            p.op("sync", lambda s: s.dma_start(out=dbg["dbg_xln2"], in_=x_ln2[:]),
                 waits=[(cV, v_ln2)], inc=cS, amt=16)

        rt_stk = contextlib.ExitStack()
        sbR, _ = mk_alloc(rt_stk)
        lgT = sbR("lgT", [8, TC], F32)
        lg_tm = sbR("lg_tm", [128, 64], F32)
        probs = sbR("probs", [128, 64], F32)
        g1 = sbR("g1", [128, 64], F32)
        g2 = sbR("g2", [128, 64], F32)
        se = sbR("se", [128, 8], F32)
        ke = sbR("ke", [128, 8], F32)
        w_all = sbR("w_all", [128, 64], F32)

        r3 = lambda apx: apx.rearrange("p (tt e) -> p tt e", tt=8, e=8)
        o3 = lambda apx: apx.rearrange("p (tt o) -> p tt o", tt=8, o=1)
        for half in range(2):
            for k in range(KD):
                waits = []
                if k == 0:
                    waits = [(cV, v_ln2), (cW, v_rtr)] if half == 0 else []

                def mm(t, half=half, k=k):
                    return t.matmul(pf[half][0:8, 0:512],
                                    router[:, k * 8:(k + 1) * 8],
                                    x_ln2[:, k * TC + half * 512: k * TC + half * 512 + 512],
                                    start=(k == 0), stop=(k == KD - 1))
                tm = p.op("tensor", mm, waits=waits, inc=(cT if k == KD - 1 else None))
            p.op("vector", (lambda v, half=half: v.tensor_copy(
                lgT[0:8, half * 512:(half + 1) * 512], pf[half][0:8, 0:512])),
                waits=[(cT, tm)], inc=cV)
        v_lgT = cV.n
        for tt in range(8):
            tm = p.op("tensor", (lambda t, tt=tt: t.transpose(
                pf[4][:, tt * 8:(tt + 1) * 8], lgT[0:8, tt * 128:(tt + 1) * 128],
                id_f[0:8, 0:8])), waits=([(cV, v_lgT)] if tt == 0 else []),
                inc=(cT if tt == 7 else None))
        v_lgtm = p.op("vector", lambda v: v.tensor_copy(lg_tm[:], pf[4][:, 0:64]),
                      waits=[(cT, tm)], inc=cV)
        vrm = p.op("vector", lambda v: v.tensor_reduce(
            o3(se.ap()), r3(lg_tm.ap()), axis=AX.X, op=ALU.max), inc=cV)
        p.op("vector", lambda v: v.tensor_tensor(
            r3(g1.ap()), r3(lg_tm.ap()), o3(se.ap()).broadcast_to((128, 8, 8)),
            op=ALU.subtract), waits=[(cV, vrm)], inc=cV)
        v_exp8 = p.op("scalar", lambda a: a.activation(probs[:], g1[:], AF.Exp),
                      waits=[(cV, cV.n)], inc=cA)
        vred = p.op("vector", lambda v: v.tensor_reduce(
            o3(se.ap()), r3(probs.ap()), axis=AX.X, op=ALU.add),
            waits=[(cA, v_exp8)], inc=cV)
        vrec = p.op("vector", lambda v: v.reciprocal(ke[:], se[:]),
                    waits=[(cV, vred)], inc=cV)
        v_pr = p.op("vector", lambda v: v.tensor_tensor(
            r3(probs.ap()), r3(probs.ap()), o3(ke.ap()).broadcast_to((128, 8, 8)),
            op=ALU.mult), waits=[(cV, vrec)], inc=cV)
        if DEBUG:
            p.op("sync", lambda s: s.dma_start(out=dbg["dbg_probs"], in_=probs[:]),
                 waits=[(cV, v_pr)], inc=cS, amt=16)
        v_wcol = 0
        for e in range(NE):
            colb = r3(probs.ap())[:, :, e:e + 1].broadcast_to((128, 8, 8))
            p.op("vector", (lambda v, colb=colb: v.tensor_tensor(
                r3(g1.ap()), r3(probs.ap()), colb, op=ALU.is_gt)), inc=cV)
            p.op("vector", (lambda v, colb=colb: v.tensor_tensor(
                r3(g2.ap()), r3(probs.ap()), colb, op=ALU.is_equal)), inc=cV)
            p.op("vector", (lambda v, e=e: v.tensor_tensor(
                r3(g2.ap()), r3(g2.ap()),
                r3(tlt.ap()[:, e * 64:(e + 1) * 64]), op=ALU.mult)),
                waits=([(cS, v_pre)] if e == 0 else []), inc=cV)
            p.op("vector", lambda v: v.tensor_tensor(
                r3(g1.ap()), r3(g1.ap()), r3(g2.ap()), op=ALU.add), inc=cV)
            p.op("vector", lambda v: v.tensor_tensor(
                r3(g1.ap()), r3(g1.ap()), r3(probs.ap()), op=ALU.mult), inc=cV)
            vr2 = p.op("vector", lambda v: v.tensor_reduce(
                o3(se.ap()), r3(g1.ap()), axis=AX.X, op=ALU.add), inc=cV)
            vk2 = p.op("vector", lambda v: v.tensor_scalar(
                ke[:], se[:], TOP_P, None, op0=ALU.is_le), waits=[(cV, vr2)], inc=cV)
            v_wcol = p.op("vector", (lambda v, e=e: v.tensor_tensor(
                r3(w_all.ap())[:, :, e:e + 1], o3(ke.ap()),
                r3(probs.ap())[:, :, e:e + 1], op=ALU.mult)), waits=[(cV, vk2)], inc=cV)
        if DEBUG:
            p.op("sync", lambda s: s.dma_start(out=dbg["dbg_w"], in_=w_all[:]),
                 waits=[(cV, v_wcol)], inc=cS, amt=16)
        v_weT = 0
        for rnd in range(2):
            tm = 0
            for i4 in range(4):
                tt = rnd * 4 + i4
                waits = []
                if i4 == 0:
                    waits = [(cV, v_wcol)] if rnd == 0 else [(cV, v_weT)]
                tm = p.op("tensor", (lambda t, tt=tt, i4=i4: t.transpose(
                    pf[5][0:8, i4 * 128:(i4 + 1) * 128], w_all[:, tt * 8:(tt + 1) * 8],
                    id_f[:, :])), waits=waits, inc=(cT if i4 == 3 else None))
            v_weT = p.op("vector", (lambda v, rnd=rnd: v.tensor_copy(
                w_eT[0:8, rnd * 512:(rnd + 1) * 512], pf[5][0:8, 0:512])),
                waits=[(cT, tm)], inc=cV)
        v_lgT_v = v_lgT
        rt_stk.close()
        ln2_stk.close()
        p6_t, p6_v, p6_a = cT.n, cV.n, cA.n
        p6_stk.close()
        snapR = [(cT, p6_t), (cV, p6_v), (cA, p6_a)]

        moe_stk = contextlib.ExitStack()
        sbM, _ = mk_alloc(moe_stk)
        hT = sbM("hT", [128, I_T * TC], BF16)

        free_g = [snapC[2], snapC[2]]
        free_u = [snapC[1], snapC[1]]
        free_d = [snapC[0], snapC[0]]
        bank_free = {0: (cV, v_lgT_v), 1: (cV, v_lgT_v), 2: (cV, v_ln2), 3: (cV, v_ln2),
                     4: (cV, v_lgtm), 5: (cV, v_weT), 6: snapC[1], 7: snapC[1]}
        wb_free = (cV, 0)
        sg_free = [snapC[1], snapC[1]]
        hu_free = [snapC[1], snapC[1]]
        hu_guard = [[snapC[1], snapC[1]], [snapC[1], snapC[1]]]
        wrow_free = (cT, 0)
        v2 = 0

        for e in range(NE):
            v_wr = p.op("sync", (lambda s, e=e: s.dma_start(
                out=wrow[:], in_=w_eT[e:e + 1, :])),
                waits=[(cV, v_weT), wrow_free] + (snapC if e == 0 else []), inc=cS, amt=16)
            wb_vpost = [0, 0]
            for hf in range(2):
                bk = hf
                tpost = p.op("tensor", (lambda t, hf=hf, bk=bk: t.matmul(
                    pf[bk][:, 0:512], ones_row[0:1, :],
                    wrow[0:1, hf * 512:(hf + 1) * 512], start=True, stop=True)),
                    waits=[(cS, v_wr), bank_free[bk]], inc=cT)
                wrow_free = (cT, tpost)
                vpost = p.op("vector", (lambda v, hf=hf, bk=bk: v.tensor_copy(
                    Wb[:, hf * 512:(hf + 1) * 512], pf[bk][:, 0:512])),
                    waits=[(cT, tpost), wb_free], inc=cV)
                bank_free[bk] = (cV, vpost)
                wb_vpost[hf] = vpost
            for it in range(I_T):
                par = it % 2
                p.op("gpsimd", (lambda g, e=e, it=it, par=par: g.dma_start(
                    out=gbuf[par][:], in_=gate_s[e, it])),
                    waits=[free_g[par]] + (snapC if (e == 0 and it < 2) else []),
                    inc=cW, amt=16)
                wu = p.op("gpsimd", (lambda g, e=e, it=it, par=par: g.dma_start(
                    out=ubuf[par][:], in_=up_s[e, it])), waits=[free_u[par]], inc=cW, amt=16)
                tg = [0, 0]
                tu = [0, 0]
                for hf in range(2):
                    bkg = par * 4 + hf
                    bku = par * 4 + 2 + hf
                    for k in range(KD):
                        waits = []
                        if k == 0:
                            waits = [hu_guard[par][hf]]
                            if hf == 0:
                                waits.append((cW, wu))
                            if it == 0:
                                waits.append((cV, wb_vpost[hf]))
                            if it < 2:
                                waits.append(bank_free[bkg])
                                waits.append(bank_free[bku])
                            if e == 0 and it < 2:
                                waits.append((cV, v_ln2))

                        def g_mm(t, par=par, hf=hf, k=k, bkg=bkg):
                            return t.matmul(pf[bkg][:, 0:512],
                                            gbuf[par][:, k * 128:(k + 1) * 128],
                                            x_ln2[:, k * TC + hf * 512: k * TC + hf * 512 + 512],
                                            start=(k == 0), stop=(k == KD - 1))
                        tg[hf] = p.op("tensor", g_mm, waits=waits,
                                      inc=(cT if k == KD - 1 else None))
                    for k in range(KD):

                        def u_mm(t, par=par, hf=hf, k=k, bku=bku):
                            return t.matmul(pf[bku][:, 0:512],
                                            ubuf[par][:, k * 128:(k + 1) * 128],
                                            x_ln2[:, k * TC + hf * 512: k * TC + hf * 512 + 512],
                                            start=(k == 0), stop=(k == KD - 1))
                        tu[hf] = p.op("tensor", u_mm, waits=[],
                                      inc=(cT if k == KD - 1 else None))
                for hf in range(2):
                    bkg = par * 4 + hf
                    bku = par * 4 + 2 + hf
                    apost = p.op("scalar", (lambda a, par=par, hf=hf, bkg=bkg: a.activation(
                        sg[par][:, hf * 512:(hf + 1) * 512], pf[bkg][:, 0:512], AF.Silu)),
                        waits=[(cT, tg[hf]), sg_free[par]], inc=cA)
                    free_g[par] = (cA, apost)
                    bank_free[bkg] = (cA, apost)
                    v1 = p.op("vector", (lambda v, par=par, hf=hf, bku=bku: v.tensor_tensor(
                        hu[par][:, hf * 512:(hf + 1) * 512],
                        sg[par][:, hf * 512:(hf + 1) * 512],
                        pf[bku][:, 0:512], op=ALU.mult)),
                        waits=[(cA, apost), (cT, tu[hf]), hu_free[par]], inc=cV)
                    free_u[par] = (cV, v1)
                    bank_free[bku] = (cV, v1)
                    hu_guard[par][hf] = (cV, v1)
                    v2 = p.op("vector", (lambda v, it=it, par=par, hf=hf: v.tensor_tensor(
                        hT[:, it * TC + hf * 512: it * TC + hf * 512 + 512],
                        hu[par][:, hf * 512:(hf + 1) * 512],
                        Wb[:, hf * 512:(hf + 1) * 512], op=ALU.mult)),
                        waits=(([(cV, wb_vpost[1])] + (snapR if e == 0 else []))
                               if (it == 0 and hf == 0) else []), inc=cV)
                    hu_free[par] = (cV, v2)
                    if hf == 1:
                        sg_free[par] = (cV, v2)
            hT_done = v2
            wb_free = (cV, hT_done)
            for m in range(NM):
                bks = ((m % 2) * 2, (m % 2) * 2 + 1)
                td = [0, 0]
                for kg in range(2):
                    par = (2 * m + kg) % 2
                    wd = p.op("gpsimd", (lambda g, e=e, m=m, kg=kg, par=par: g.dma_start(
                        out=dbuf[par][:], in_=down_s[e, m][:, kg * 2048:(kg + 1) * 2048])),
                        waits=[free_d[par]], inc=cW, amt=16)
                    first_waits = [(cW, wd)]
                    if kg == 0:
                        first_waits += [bank_free[bks[1]]]
                        if m < 2:
                            first_waits += [(cV, hT_done), bank_free[bks[0]]]
                    for kk in range(16):
                        k = kg * 16 + kk
                        for hf in range(2):
                            waits = first_waits if (kk == 0 and hf == 0) else []

                            def d_mm(t, par=par, k=k, kk=kk, hf=hf, bk=bks[hf]):
                                return t.matmul(pf[bk][:, 0:512],
                                                dbuf[par][:, kk * 128:(kk + 1) * 128],
                                                hT[:, k * TC + hf * 512: k * TC + hf * 512 + 512],
                                                start=(k == 0), stop=(k == I_T - 1))
                            td[hf] = p.op("tensor", d_mm, waits=waits,
                                          inc=(cT if k == I_T - 1 else None))
                    if kg == 0:
                        tmk = p.op("tensor", (lambda t: t.nop()), inc=cT)
                        free_d[par] = (cT, tmk)
                    else:
                        free_d[par] = (cT, td[1])
                for hf in range(2):
                    vr = p.op("vector", (lambda v, m=m, hf=hf, bk=bks[hf]: v.tensor_tensor(
                        R[:, m * TC + hf * 512: m * TC + hf * 512 + 512],
                        R[:, m * TC + hf * 512: m * TC + hf * 512 + 512],
                        pf[bk][:, 0:512], op=ALU.add)), waits=[(cT, td[hf])], inc=cV)
                    bank_free[bks[hf]] = (cV, vr)

        p.op("sync", lambda s: s.dma_start(
            out=outT.rearrange("(k p) t -> p k t", k=KD, p=128), in_=R[:]),
            waits=[(cV, cV.n)], inc=cS, amt=16)
        p.op("sync", None, waits=[(cS, cS.n)])

        moe_stk.close()
        mf_stk.close()
        tail_stk.close()
        bigA_stk.close()
        block = L0.enter_context(nc.Block())
        p.emit(block)
    return nc


def _pack(inputs):
    x = np.asarray(inputs["L0_attn_x"], dtype=np.float32)
    w_qkv = np.asarray(inputs["L0_attn_w_qkv"], dtype=np.float32)
    b_qkv = np.asarray(inputs["L0_attn_b_qkv"], dtype=np.float32)
    w_out = np.asarray(inputs["L0_attn_w_out"], dtype=np.float32)
    b_out = np.asarray(inputs["L0_attn_b_out"], dtype=np.float32)
    router_w = np.asarray(inputs["L0_ffn_router_w"], dtype=np.float32)
    gate_w = np.asarray(inputs["L0_ffn_gate_w"], dtype=np.float32)
    up_w = np.asarray(inputs["L0_ffn_up_w"], dtype=np.float32)
    down_w = np.asarray(inputs["L0_ffn_down_w"], dtype=np.float32)

    xT = np.ascontiguousarray(x.reshape(T, D).T)
    qkv_sv = np.ascontiguousarray(
        w_qkv.reshape(QKV_M, 128, KD, 128).transpose(0, 3, 2, 1)).reshape(QKV_M, 128, D)
    wout_sv = np.ascontiguousarray(
        w_out.reshape(NM, 128, KD, 128).transpose(0, 3, 2, 1)).reshape(NM, 128, D)
    bqkv = np.ascontiguousarray(b_qkv.reshape(QKV_M, 128).T)
    boutp = np.ascontiguousarray(b_out.reshape(NM, 128).T)
    router_p = np.ascontiguousarray(
        router_w.T.reshape(KD, 128, NE).transpose(1, 0, 2)).reshape(128, 128)
    gate_sv = np.ascontiguousarray(
        gate_w.reshape(NE, I_T, 128, KD, 128).transpose(0, 1, 4, 3, 2)).reshape(NE, I_T, 128, D)
    up_sv = np.ascontiguousarray(
        up_w.reshape(NE, I_T, 128, KD, 128).transpose(0, 1, 4, 3, 2)).reshape(NE, I_T, 128, D)
    down_sv = np.ascontiguousarray(
        down_w.reshape(NE, NM, 128, I_T, 128).transpose(0, 1, 4, 3, 2)).reshape(NE, NM, 128, I_T * 128)

    lam, decayq, dmaskT, decayk, lamc = _host_consts()
    shared = {
        "qkv_s": qkv_sv, "bqkv": bqkv, "wout_s": wout_sv, "bout": boutp,
        "router_p": router_p, "gate_s": gate_sv, "up_s": up_sv, "down_s": down_sv,
        "dmaskT": dmaskT, "decayq": decayq, "decayk": decayk, "lamc": lamc,
        "tlt": _tlt(), "ident": np.eye(128, dtype=np.float32),
    }
    in_maps = []
    for c in range(NCORES):
        m = dict(shared)
        m["xT"] = np.ascontiguousarray(xT[:, c * TC:(c + 1) * TC])
        m["coef"] = _host_coef(c, lam)
        in_maps.append(m)
    return in_maps


def kernel(**inputs):
    in_maps = _pack(inputs)
    nc = build()
    res = run_bass_kernel_spmd(nc, in_maps, core_ids=list(range(NCORES)))
    if DEBUG:
        _LAST["res"] = res
        _LAST["in_maps"] = in_maps
    parts = [np.asarray(res.results[c]["outT"], dtype=np.float32) for c in range(NCORES)]
    full = np.concatenate(parts, axis=1)
    return np.ascontiguousarray(full.T).reshape(1, T, D)


# version 04
# speedup vs baseline: 1.0046x; 1.0003x over previous
import contextlib

import numpy as np

import concourse.bass as bass
import concourse.mybir as mybir
from concourse.bass_utils import run_bass_kernel_spmd

F32 = mybir.dt.float32
BF16 = mybir.dt.bfloat16
AF = mybir.ActivationFunctionType
ALU = mybir.AluOpType
AX = mybir.AxisListType

NCORES = 8
D = 2048
T = 8192
TC = T // NCORES
KD = 16
H = 16
CH = 128
NCH = TC // CH
NE = 8
I_T = 32
NM = 16
QKV_M = 48
TOP_P = 0.9
EPS = 1e-5

DEBUG = False
_LAST = {}


class Ctr:
    __slots__ = ("h", "n")

    def __init__(self, h):
        self.h = h
        self.n = 0


class Prog:
    ENGINES = ("sync", "gpsimd", "tensor", "vector", "scalar")

    def __init__(self):
        self.ops = {e: [] for e in self.ENGINES}

    def op(self, eng, fn, waits=(), inc=None, amt=1):
        post = None
        if inc is not None:
            inc.n += amt
            post = inc.n
        ws = tuple((c, v) for (c, v) in waits if v > 0)
        self.ops[eng].append((fn, ws, inc, amt if inc is not None else 0))
        return post

    def emit(self, block):
        for eng in self.ENGINES:
            def mk(e):
                def body(h):
                    for fn, waits, inc, amt in self.ops[e]:
                        for (c, v) in waits:
                            h.wait_ge(c.h, v)
                        if fn is not None:
                            inst = fn(h)
                            if inc is not None:
                                inst.then_inc(inc.h, amt)
                return body
            getattr(block, eng)(mk(eng))


def _host_consts():
    hh = np.arange(H, dtype=np.float64)
    slopes = (2.0 ** -0.5) ** (hh + 1.0)
    lam = np.exp(-slopes)
    pos = np.arange(CH, dtype=np.float64)
    dq = lam[:, None] ** (pos[None, :] + 1.0)
    decayq = np.broadcast_to(dq.reshape(1, H * CH), (128, H * CH)).astype(np.float32)
    diff = pos[None, None, :] - pos[None, :, None]
    dm = np.where(diff >= 0, lam[:, None, None] ** np.maximum(diff, 0.0), 0.0)
    dmaskT = np.ascontiguousarray(dm.transpose(1, 0, 2).reshape(CH, H * CH)).astype(np.float32)
    dk = lam[:, None] ** (CH - 1.0 - pos[None, :])
    decayk = np.ascontiguousarray(
        np.broadcast_to(dk.T[:, :, None], (CH, H, CH)).reshape(CH, H * CH)).astype(np.float32)
    lc = lam ** CH
    lamc = np.ascontiguousarray(
        np.broadcast_to(lc[None, :, None], (CH, H, CH)).reshape(CH, H * CH)).astype(np.float32)
    return lam, decayq, dmaskT, decayk, lamc


def _host_coef(core, lam):
    out = np.zeros((128, NCORES), dtype=np.float64)
    for pp in range(128):
        h = pp // 8
        for r in range(core):
            out[pp, r] = lam[h] ** (TC * (core - 1 - r))
    return out.astype(np.float32)


def _tlt():
    out = np.zeros((128, NE * 64), dtype=np.float32)
    for e in range(NE):
        for tt in range(8):
            out[:, e * 64 + tt * 8: e * 64 + tt * 8 + e] = 1.0
    return out


def build():
    nc = bass.Bass("TRN2", target_bir_lowering=False, debug=False)

    xT = nc.dram_tensor("xT", [D, TC], F32, kind="ExternalInput").ap()
    qkv_s = nc.dram_tensor("qkv_s", [QKV_M, 128, D], F32, kind="ExternalInput").ap()
    bqkv_d = nc.dram_tensor("bqkv", [128, QKV_M], F32, kind="ExternalInput").ap()
    wout_s = nc.dram_tensor("wout_s", [NM, 128, D], F32, kind="ExternalInput").ap()
    bout_d = nc.dram_tensor("bout", [128, NM], F32, kind="ExternalInput").ap()
    router_d = nc.dram_tensor("router_p", [128, 128], F32, kind="ExternalInput").ap()
    gate_s = nc.dram_tensor("gate_s", [NE, I_T, 128, D], F32, kind="ExternalInput").ap()
    up_s = nc.dram_tensor("up_s", [NE, I_T, 128, D], F32, kind="ExternalInput").ap()
    down_s = nc.dram_tensor("down_s", [NE, NM, 128, I_T * 128], F32, kind="ExternalInput").ap()
    dmaskT_d = nc.dram_tensor("dmaskT", [128, H * CH], F32, kind="ExternalInput").ap()
    decayq_d = nc.dram_tensor("decayq", [128, H * CH], F32, kind="ExternalInput").ap()
    decayk_d = nc.dram_tensor("decayk", [128, H * CH], F32, kind="ExternalInput").ap()
    lamc_d = nc.dram_tensor("lamc", [128, H * CH], F32, kind="ExternalInput").ap()
    tlt_d = nc.dram_tensor("tlt", [128, NE * 64], F32, kind="ExternalInput").ap()
    coef_d = nc.dram_tensor("coef", [128, NCORES], F32, kind="ExternalInput").ap()
    ident_d = nc.dram_tensor("ident", [128, 128], F32, kind="ExternalInput").ap()
    outT = nc.dram_tensor("outT", [D, TC], F32, kind="ExternalOutput").ap()
    dbg = {}
    if DEBUG:
        for nm_, shape, dt in [
            ("dbg_xln1", [128, KD * TC], BF16), ("dbg_A", [128, H * CH], F32),
            ("dbg_P", [128, H * CH], F32), ("dbg_attnT", [128, H * TC], BF16),
            ("dbg_R1", [128, KD * TC], F32), ("dbg_probs", [128, 64], F32),
            ("dbg_w", [128, 64], F32), ("dbg_kT", [128, KD * TC], BF16),
            ("dbg_vtm", [128, NCH * D], BF16), ("dbg_xln2", [128, KD * TC], BF16),
            ("dbg_ag", [NCORES * 128, H * CH], BF16),
        ]:
            dbg[nm_] = nc.dram_tensor(nm_, shape, dt, kind="ExternalOutput").ap()

    a_loc = nc.dram_tensor("a_loc", [128, H * CH], BF16, kind="Internal")
    ag = nc.dram_tensor("ag", [NCORES * 128, H * CH], BF16, kind="Internal",
                        addr_space="Shared")
    p_tmp = nc.dram_tensor("p_tmp", [128, H * CH], F32, kind="Internal")

    p = Prog()
    L0 = contextlib.ExitStack()

    def mk_alloc(stk):
        return (lambda name, shape, dt: stk.enter_context(nc.sbuf_tensor(name, shape, dt)),
                lambda name, shape, dt: stk.enter_context(nc.psum_tensor(name, shape, dt)))

    with L0:
        sb0, ps0 = mk_alloc(L0)
        sem = lambda name: Ctr(L0.enter_context(nc.semaphore(name)))
        cS = sem("cS")
        cW = sem("cW")
        cT = sem("cT")
        cV = sem("cV")
        cA = sem("cA")
        cCC = sem("cCC")

        ones_b = sb0("ones_b", [128, 1], BF16)
        ones_row = sb0("ones_row", [1, 128], F32)
        id_f = sb0("id_f", [128, 128], F32)
        id_b = sb0("id_b", [128, 128], BF16)
        bqkv = sb0("bqkv_sb", [128, QKV_M], F32)
        bout = sb0("bout_sb", [128, NM], F32)
        router = sb0("router_sb", [128, 128], BF16)
        tlt = sb0("tlt_sb", [128, NE * 64], F32)
        coef = sb0("coef_sb", [128, NCORES], F32)
        pf = [ps0("pf%d" % i, [128, 512], F32) for i in range(6)]

        v_pre = 0
        for fn in [
            lambda s: s.dma_start(out=bqkv[:], in_=bqkv_d),
            lambda s: s.dma_start(out=bout[:], in_=bout_d),
            lambda s: s.dma_start(out=tlt[:], in_=tlt_d),
            lambda s: s.dma_start(out=coef[:], in_=coef_d),
            lambda s: s.dma_start(out=id_f[:], in_=ident_d),
        ]:
            v_pre = p.op("sync", fn, inc=cS, amt=16)
        v_idb = p.op("gpsimd", lambda g: g.dma_start(out=id_b[:], in_=ident_d), inc=cW, amt=16)
        v_rtr = p.op("gpsimd", lambda g: g.dma_start(out=router[:], in_=router_d), inc=cW, amt=16)
        p.op("vector", lambda v: v.memset(ones_b[:], 1.0), inc=cV)
        p.op("vector", lambda v: v.memset(ones_row[:], 1.0), inc=cV)
        v_ones = cV.n

        def emit_ln(src, dst, xb, sqb, rows, nsc, ready_k, extra0=()):
            xb_free = [(cT, 0), (cT, 0)]
            sq_free = [(cT, 0), (cT, 0)]
            for half in range(2):
                for k in range(KD):
                    par = k % 2
                    vw = [xb_free[par]]
                    aw = [sq_free[par]]
                    if half == 0:
                        vw = vw + list(ready_k[k])
                        aw = aw + list(ready_k[k])
                        if k == 0:
                            vw = vw + [(cV, v_ones)] + list(extra0)
                            aw = aw + list(extra0)
                    vc = p.op("vector", (lambda v, k=k, half=half, par=par: v.tensor_copy(
                        xb[par][:, 0:512],
                        src[:, k * TC + half * 512: k * TC + half * 512 + 512])),
                        waits=vw, inc=cV)
                    va = p.op("scalar", (lambda a, k=k, half=half, par=par: a.activation(
                        sqb[par][:, 0:512],
                        src[:, k * TC + half * 512: k * TC + half * 512 + 512],
                        AF.Square)), waits=aw, inc=cA)
                    tm_s = p.op("tensor", (lambda t, k=k, par=par: t.matmul(
                        pf[0][0:1, 0:512], ones_b[:, 0:1], xb[par][:, 0:512],
                        start=(k == 0), stop=(k == KD - 1))), waits=[(cV, vc)], inc=cT)
                    xb_free[par] = (cT, tm_s)
                    tm_q = p.op("tensor", (lambda t, k=k, par=par: t.matmul(
                        pf[1][0:1, 0:512], ones_b[:, 0:1], sqb[par][:, 0:512],
                        start=(k == 0), stop=(k == KD - 1))), waits=[(cA, va)], inc=cT)
                    sq_free[par] = (cT, tm_q)
                ve = p.op("vector", (lambda v, half=half: v.tensor_scalar(
                    rows[0:1, half * 512: half * 512 + 512], pf[0][0:1, 0:512],
                    1.0 / D, None, op0=ALU.mult)), waits=[(cT, tm_s)], inc=cV)
                ve2 = p.op("vector", (lambda v, half=half: v.tensor_scalar(
                    rows[0:1, 1024 + half * 512: 1024 + half * 512 + 512], pf[1][0:1, 0:512],
                    1.0 / D, None, op0=ALU.mult)), waits=[(cT, tm_q)], inc=cV)
                if half == 0:
                    p.op("tensor", None, waits=[(cV, ve2)])
            p.op("vector", lambda v: v.tensor_tensor(
                rows[0:1, 2048:3072], rows[0:1, 0:1024], rows[0:1, 0:1024], op=ALU.mult), inc=cV)
            v_var = p.op("vector", lambda v: v.scalar_tensor_tensor(
                rows[0:1, 3072:4096], rows[0:1, 1024:2048], EPS, rows[0:1, 2048:3072],
                op0=ALU.add, op1=ALU.subtract), inc=cV)
            a_sd = p.op("scalar", lambda a: a.activation(
                rows[0:1, 2048:3072], rows[0:1, 3072:4096], AF.Sqrt),
                waits=[(cV, v_var)], inc=cA)
            v_rstd = p.op("vector", lambda v: v.reciprocal(
                rows[0:1, 3072:4096], rows[0:1, 2048:3072]), waits=[(cA, a_sd)], inc=cV)
            for half in range(2):
                p.op("tensor", (lambda t, half=half: t.matmul(
                    pf[2 + half][:, 0:512], ones_row[0:1, :],
                    rows[0:1, half * 512: half * 512 + 512], start=True, stop=True)),
                    waits=[(cV, v_rstd)] if half == 0 else [], inc=cT)
                t_bc = p.op("tensor", (lambda t, half=half: t.matmul(
                    pf[4 + half][:, 0:512], ones_row[0:1, :],
                    rows[0:1, 3072 + half * 512: 3072 + half * 512 + 512],
                    start=True, stop=True)), inc=cT)
            last = 0
            for k in range(KD):
                for half in range(2):
                    par = k % 2
                    p.op("vector", (lambda v, k=k, half=half, par=par: v.tensor_tensor(
                        nsc[par][:, 0:512],
                        src[:, k * TC + half * 512: k * TC + half * 512 + 512],
                        pf[2 + half][:, 0:512], op=ALU.subtract)),
                        waits=([(cT, t_bc)] if (k == 0 and half == 0) else []), inc=cV)
                    last = p.op("vector", (lambda v, k=k, half=half, par=par: v.tensor_tensor(
                        dst[:, k * TC + half * 512: k * TC + half * 512 + 512],
                        nsc[par][:, 0:512],
                        pf[4 + half][:, 0:512], op=ALU.mult)), inc=cV)
            return last

        def emit_proj(wdram, m0, m1, act, wbuf, wbuf_free, evac, act_ready, bank_init):
            bank_free = dict(bank_init)
            for i, m in enumerate(range(m0, m1)):
                par = i % 2
                wload = p.op("gpsimd", (lambda g, m=m, par=par: g.dma_start(
                    out=wbuf[par][:], in_=wdram[m])), waits=[wbuf_free[par]], inc=cW, amt=16)
                tm = [0, 0]
                for half in range(2):
                    bk = par * 2 + half
                    for k in range(KD):
                        waits = []
                        if k == 0:
                            waits = [bank_free[bk]]
                            if half == 0:
                                waits.append((cW, wload))
                            if i < 2:
                                waits = waits + list(act_ready)

                        def mm(t, m=m, par=par, half=half, k=k, bk=bk):
                            return t.matmul(pf[bk][:, 0:512],
                                            wbuf[par][:, k * 128:(k + 1) * 128],
                                            act[:, k * TC + half * 512: k * TC + half * 512 + 512],
                                            start=(k == 0), stop=(k == KD - 1))
                        tmv = p.op("tensor", mm, waits=waits, inc=(cT if k == KD - 1 else None))
                    tm[half] = tmv
                wbuf_free[par] = (cT, tm[1])
                for half in range(2):
                    bk = par * 2 + half
                    bank_free[bk] = evac(i, m, half, bk, [(cT, tm[half])])

        bigA_stk = contextlib.ExitStack()
        sbBA, _ = mk_alloc(bigA_stk)
        bigA = sbBA("bigA", [128, KD * TC], BF16)
        x_ln1 = bigA

        base_stk = contextlib.ExitStack()
        sbB, _ = mk_alloc(base_stk)
        decayq = sbB("decayq_sb", [128, H * CH], F32)
        lamct = sbB("lamc_sb", [128, H * CH], F32)
        S_run = sbB("S_run", [128, H * CH], F32)
        S_bf = sbB("S_bf", [128, H * CH], BF16)
        qd_s = sbB("qd_s", [128, H * CH], BF16)
        s_tmp = sbB("s_tmp", [128, 512], F32)
        qT = sbB("qT", [128, KD * TC], BF16)

        v_tb0 = 0
        for dst_t, src_t in [(decayq, decayq_d), (lamct, lamc_d)]:
            v_tb0 = p.op("sync", (lambda s, dst_t=dst_t, src_t=src_t: s.dma_start(
                out=dst_t[:], in_=src_t)), inc=cS, amt=16)

        ln1_stk = contextlib.ExitStack()
        sbL, _ = mk_alloc(ln1_stk)
        x_f32 = sbL("x_f32", [128, KD * TC], F32)
        xb1 = [sbL("xb1_0", [128, 512], BF16), sbL("xb1_1", [128, 512], BF16)]
        sqb = [sbL("sqb0", [128, 512], BF16), sbL("sqb1", [128, 512], BF16)]
        rows = sbL("rows", [1, 4096], F32)
        nsc = [sbL("nsc0", [128, 512], F32), sbL("nsc1", [128, 512], F32)]
        v_xload = p.op("sync", lambda s: s.dma_start(
            out=x_f32[:], in_=xT.rearrange("(k p) t -> p k t", k=KD, p=128)), inc=cS, amt=16)
        ready_k1 = [([(cS, cS.n)] if k == 0 else []) for k in range(KD)]
        v_ln1 = emit_ln(x_f32, x_ln1, xb1, sqb, rows, nsc, ready_k1)
        if DEBUG:
            p.op("sync", lambda s: s.dma_start(out=dbg["dbg_xln1"], in_=x_ln1[:]),
                 waits=[(cV, v_ln1)], inc=cS, amt=16)
        ln1_t, ln1_v, ln1_a = cT.n, cV.n, cA.n
        ln1_stk.close()
        snap1 = [(cT, ln1_t), (cV, ln1_v), (cA, ln1_a)]

        scan_stk = contextlib.ExitStack()
        sbA, psA = mk_alloc(scan_stk)
        dmaskT = sbA("dmaskT_sb", [128, H * CH], F32)
        decayk = sbA("decayk_sb", [128, H * CH], F32)
        kT = sbA("kT", [128, KD * TC], BF16)
        v_tm = sbA("v_tm", [128, NCH * D], BF16)
        kd_tm = sbA("kd_tm", [128, H * CH], BF16)
        sT_s = sbA("sT_s", [128, H * CH], BF16)
        vtmp = [sbA("vtmp0", [128, TC], BF16), sbA("vtmp1", [128, TC], BF16)]
        wbufA = [sbA("wbufA0", [128, D], BF16), sbA("wbufA1", [128, D], BF16)]
        ptb = [psA("ptb0", [128, 1024], BF16), psA("ptb1", [128, 1024], BF16)]

        v_tabs = 0
        for i, (dst_t, src_t) in enumerate([(dmaskT, dmaskT_d), (decayk, decayk_d)]):
            v_tabs = p.op("sync", (lambda s, dst_t=dst_t, src_t=src_t: s.dma_start(
                out=dst_t[:], in_=src_t)), waits=(snap1 if i == 0 else []), inc=cS, amt=16)

        wbuf_free = [snap1[0], snap1[0]]
        bank_free_p = {i: snap1[1] for i in range(4)}
        tb_free = [snap1[1], snap1[1]]
        vtmp_free = [(cT, 0), (cT, 0)]
        vtm_last = 0
        for h in range(H):
            m = 32 + h
            par = h % 2
            wload = p.op("gpsimd", (lambda g, m=m, par=par: g.dma_start(
                out=wbufA[par][:], in_=qkv_s[m])), waits=[wbuf_free[par]], inc=cW, amt=16)
            tmh = [0, 0]
            for half in range(2):
                bk = par * 2 + half
                for k in range(KD):
                    waits = []
                    if k == 0:
                        waits = [bank_free_p[bk]]
                        if half == 0:
                            waits.append((cW, wload))
                        if h < 2:
                            waits.append((cV, v_ln1))

                    def mm(t, m=m, par=par, half=half, k=k, bk=bk):
                        return t.matmul(pf[bk][:, 0:512],
                                        wbufA[par][:, k * 128:(k + 1) * 128],
                                        x_ln1[:, k * TC + half * 512: k * TC + half * 512 + 512],
                                        start=(k == 0), stop=(k == KD - 1))
                    tmv = p.op("tensor", mm, waits=waits, inc=(cT if k == KD - 1 else None))
                tmh[half] = tmv
            wbuf_free[par] = (cT, tmh[1])
            va = 0
            for half in range(2):
                bk = par * 2 + half
                va = p.op("scalar", (lambda a, m=m, half=half, bk=bk, par=par: a.activation(
                    vtmp[par][:, half * 512:(half + 1) * 512],
                    pf[bk][:, 0:512], AF.Identity, bias=bqkv[:, m:m + 1], scale=1.0)),
                    waits=[(cT, tmh[half])] + ([vtmp_free[par], (cS, v_pre)] if half == 0 else []),
                    inc=cA)
                bank_free_p[bk] = (cA, va)
            tmt = 0
            for j in range(NCH):
                w0 = []
                if j == 0:
                    w0 = [(cA, va), tb_free[par]]
                    if h == 0:
                        w0.append((cW, v_idb))
                tmt = p.op("tensor", (lambda t, j=j, par=par: t.transpose(
                    ptb[par][:, j * 128:(j + 1) * 128],
                    vtmp[par][:, j * 128:(j + 1) * 128], id_b[:, :])),
                    waits=w0, inc=(cT if j == NCH - 1 else None))
            vtmp_free[par] = (cT, tmt)
            vtm_last = p.op("vector", (lambda v, h=h, par=par: v.tensor_copy(
                v_tm.ap().rearrange("p (j he) -> p j he", j=NCH, he=D)[:, :, h * 128:(h + 1) * 128],
                ptb[par].ap().rearrange("p (j e) -> p j e", j=NCH, e=128))),
                waits=[(cT, tmt)], inc=cV)
            tb_free[par] = (cV, vtm_last)

        def evac_kv(i, m, half, bk, waits):
            mrel = m - 16
            va = p.op("scalar", (lambda a, m=m, half=half, bk=bk, mrel=mrel: a.activation(
                kT[:, mrel * TC + half * 512: mrel * TC + half * 512 + 512],
                pf[bk][:, 0:512], AF.Identity, bias=bqkv[:, m:m + 1], scale=1.0)),
                waits=waits, inc=cA)
            return (cA, va)

        def evac_q(i, m, half, bk, waits):
            va = p.op("scalar", (lambda a, m=m, half=half, bk=bk: a.activation(
                qT[:, m * TC + half * 512: m * TC + half * 512 + 512],
                pf[bk][:, 0:512], AF.Identity, bias=bqkv[:, m:m + 1], scale=1.0)),
                waits=waits, inc=cA)
            return (cA, va)

        emit_proj(qkv_s, 16, 32, x_ln1, wbufA, wbuf_free, evac_kv,
                  [(cV, v_ln1)], dict(bank_free_p))
        emit_proj(qkv_s, 0, 16, x_ln1, wbufA, wbuf_free, evac_q,
                  [(cV, v_ln1)], {i: (cA, cA.n) for i in range(4)})
        v_proj_a = cA.n
        v_proj_t = cT.n

        v_s0 = p.op("vector", lambda v: v.memset(S_run[:], 0.0),
                    waits=[(cA, ln1_a), (cT, ln1_t)], inc=cV)
        sc_free = [(cV, 0), (cV, 0)]
        po_free = [(cA, v_proj_a), (cA, v_proj_a)]
        kv_free = [(cV, v_ln1), (cV, v_ln1)]

        def emit_kd(jj, first_waits):
            outs = []
            for g in range(2):
                tmx = 0
                for hh in range(8):
                    hx = g * 8 + hh
                    waits = []
                    if hh == 0:
                        waits = [tb_free[g]]
                        if g == 0:
                            waits = waits + first_waits
                    tmx = p.op("tensor", (lambda t, hx=hx, hh=hh, g=g, jj=jj: t.transpose(
                        ptb[g][:, hh * 128:(hh + 1) * 128],
                        kT[:, hx * TC + jj * 128: hx * TC + jj * 128 + 128],
                        id_b[:, :])), waits=waits, inc=(cT if hh == 7 else None))
                ve = p.op("vector", (lambda v, g=g: v.tensor_tensor(
                    kd_tm[:, g * 1024:(g + 1) * 1024], ptb[g][:, :],
                    decayk[:, g * 1024:(g + 1) * 1024], op=ALU.mult)),
                    waits=[(cT, tmx)], inc=cV)
                tb_free[g] = (cV, ve)
                outs.append(ve)
            return outs

        def emit_kv_s(jj, kd_ready, vtm_ready):
            for g4 in range(4):
                kb = 4 + (g4 % 2)
                tmx = 0
                for hh in range(4):
                    hx = g4 * 4 + hh
                    waits = []
                    if hh == 0:
                        waits = [kv_free[g4 % 2], (cV, kd_ready[0 if g4 < 2 else 1])]
                        if vtm_ready:
                            waits.append((cV, vtm_ready))
                    tmx = p.op("tensor", (lambda t, hx=hx, jj=jj, kb=kb, hh=hh: t.matmul(
                        pf[kb][:, hh * 128:(hh + 1) * 128],
                        kd_tm[:, hx * 128:(hx + 1) * 128],
                        v_tm[:, jj * D + hx * 128: jj * D + (hx + 1) * 128],
                        start=True, stop=True, skip_group_check=True)),
                        waits=waits, inc=(cT if hh == 3 else None))
                p.op("vector", (lambda v, g4=g4: v.tensor_tensor(
                    s_tmp[:, 0:512], S_run[:, g4 * 512:(g4 + 1) * 512],
                    lamct[:, g4 * 512:(g4 + 1) * 512], op=ALU.mult)), inc=cV)
                v2 = p.op("vector", (lambda v, g4=g4, kb=kb: v.tensor_tensor(
                    S_run[:, g4 * 512:(g4 + 1) * 512], s_tmp[:, 0:512],
                    pf[kb][:, 0:512], op=ALU.add)), waits=[(cT, tmx)], inc=cV)
                kv_free[g4 % 2] = (cV, v2)

        attnT = bigA
        t_prev_j = 0
        for j in range(NCH):
            w_sbf = [(cT, t_prev_j)] if j > 0 else [(cV, v_s0)]
            v_sbf = p.op("vector", (lambda v: v.tensor_copy(S_bf[:], S_run[:])),
                         waits=w_sbf, inc=cV)
            v_qd = p.op("vector", (lambda v, j=j: v.tensor_tensor(
                qd_s.ap().rearrange("p (h t) -> p h t", h=H, t=CH),
                qT.ap().rearrange("p (h t) -> p h t", h=H, t=TC)[:, :, j * 128:(j + 1) * 128],
                decayq.ap().rearrange("p (h t) -> p h t", h=H, t=CH), op=ALU.mult)),
                waits=([(cT, t_prev_j)] if j > 0 else [(cA, v_proj_a), (cS, cS.n)]), inc=cV)
            kde = emit_kd(j, [(cS, cS.n), (cT, v_proj_t)] if j == 0 else [])
            sT_ready = [0, 0, 0, 0]
            for g4 in range(4):
                sb_ = g4 % 2
                tmx = 0
                for hh in range(4):
                    hx = g4 * 4 + hh
                    waits = [sc_free[sb_]] if hh == 0 else []
                    tmx = p.op("tensor", (lambda t, hx=hx, j=j, sb_=sb_, hh=hh: t.matmul(
                        pf[sb_][:, hh * 128:(hh + 1) * 128],
                        kT[:, hx * TC + j * 128: hx * TC + j * 128 + 128],
                        qT[:, hx * TC + j * 128: hx * TC + j * 128 + 128],
                        start=True, stop=True, skip_group_check=True)),
                        waits=waits, inc=(cT if hh == 3 else None))
                vm = p.op("vector", (lambda v, g4=g4, sb_=sb_: v.tensor_tensor(
                    sT_s[:, g4 * 512:(g4 + 1) * 512], pf[sb_][:, 0:512],
                    dmaskT[:, g4 * 512:(g4 + 1) * 512], op=ALU.mult)),
                    waits=[(cT, tmx)], inc=cV)
                sc_free[sb_] = (cV, vm)
                sT_ready[g4] = vm
            for g4 in range(4):
                pb_ = 2 + (g4 % 2)
                tmx = 0
                for hh in range(4):
                    hx = g4 * 4 + hh
                    waits = []
                    if hh == 0:
                        waits = [po_free[g4 % 2], (cV, sT_ready[g4])]
                        if g4 == 0:
                            waits.append((cV, v_qd))
                            waits.append((cV, v_sbf))
                    p.op("tensor", (lambda t, hx=hx, j=j, pb_=pb_, hh=hh: t.matmul(
                        pf[pb_][:, hh * 128:(hh + 1) * 128],
                        v_tm[:, j * D + hx * 128: j * D + (hx + 1) * 128],
                        sT_s[:, hx * 128:(hx + 1) * 128],
                        start=True, stop=False, skip_group_check=True)), waits=waits)
                    tmx = p.op("tensor", (lambda t, hx=hx, pb_=pb_, hh=hh: t.matmul(
                        pf[pb_][:, hh * 128:(hh + 1) * 128],
                        S_bf[:, hx * 128:(hx + 1) * 128],
                        qd_s[:, hx * 128:(hx + 1) * 128],
                        start=False, stop=True, skip_group_check=True)),
                        inc=(cT if hh == 3 else None))
                va = p.op("scalar", (lambda a, g4=g4, j=j, pb_=pb_: a.activation(
                    attnT.ap().rearrange("p (h t) -> p h t", h=H, t=TC)[:, g4 * 4:(g4 + 1) * 4, j * 128:(j + 1) * 128],
                    pf[pb_].ap().rearrange("p (h t) -> p h t", h=4, t=128),
                    AF.Identity)), waits=[(cT, tmx)], inc=cA)
                po_free[g4 % 2] = (cA, va)
            emit_kv_s(j, kde, vtm_last if j == 0 else 0)
            t_prev_j = cT.n
        v_scan_t = cT.n
        v_scan_v = cV.n
        v_scan_a = cA.n

        aL = a_loc.ap().rearrange("(h2 dhi) (dlo e) -> h2 dhi dlo e", h2=16, dhi=8, dlo=16, e=128)
        v_exp = 0
        for h in range(H):
            v_exp = p.op("gpsimd", (lambda g, h=h: g.dma_start(
                out=aL[h], in_=S_run[:, h * 128:(h + 1) * 128])),
                waits=([(cV, v_scan_v)] if h == 0 else []), inc=cW, amt=16)
        if DEBUG:
            p.op("sync", lambda s: s.dma_start(out=dbg["dbg_A"], in_=S_run[:]),
                 waits=[(cV, v_scan_v)], inc=cS, amt=16)
            p.op("sync", lambda s: s.dma_start(out=dbg["dbg_kT"], in_=kT[:]), inc=cS, amt=16)
            p.op("sync", lambda s: s.dma_start(out=dbg["dbg_vtm"], in_=v_tm[:]), inc=cS, amt=16)
        v_cc = p.op("gpsimd", lambda g: g.collective_compute(
            "AllGather", ALU.bypass, replica_groups=[list(range(NCORES))],
            ins=[a_loc.ap()], outs=[ag.ap()]), waits=[(cW, v_exp)], inc=cCC)

        scan_stk.close()
        snapS = [(cT, v_scan_t), (cV, v_scan_v), (cA, v_scan_a)]

        exch_stk = contextlib.ExitStack()
        sbE, _ = mk_alloc(exch_stk)
        ar_sb = sbE("ar_sb", [128, 1024], BF16)
        P_sb = sbE("P_sb", [128, 1024], F32)
        v_pexp = 0
        for half in range(2):
            v_cmb = 0
            for r in range(NCORES):
                war = p.op("gpsimd", (lambda g, r=r, half=half: g.dma_start(
                    out=ar_sb[:], in_=ag[r * 128:(r + 1) * 128, half * 1024:(half + 1) * 1024])),
                    waits=([(cCC, 1)] + snapS if (r == 0 and half == 0) else [(cV, v_cmb)]),
                    inc=cW, amt=16)
                if r == 0:
                    v_cmb = p.op("vector", (lambda v, r=r: v.tensor_scalar(
                        P_sb[:], ar_sb[:], coef[:, r:r + 1], None, op0=ALU.mult)),
                        waits=[(cW, war), (cS, v_pre)] +
                              (snapS if half == 0 else [(cS, v_pexp)]), inc=cV)
                else:
                    v_cmb = p.op("vector", (lambda v, r=r: v.scalar_tensor_tensor(
                        P_sb[:], ar_sb[:], coef[:, r:r + 1], P_sb[:],
                        op0=ALU.mult, op1=ALU.add)), waits=[(cW, war)], inc=cV)
            pT = p_tmp.ap().rearrange("(dhi dlo) (h2 e) -> h2 dhi dlo e",
                                      dhi=8, dlo=16, h2=16, e=128)
            for h in range(H):
                v_pexp = p.op("sync", (lambda s, h=h, half=half, pT=pT: s.dma_start(
                    out=pT[h][:, half * 8:(half + 1) * 8, :],
                    in_=P_sb.ap()[h * 8:(h + 1) * 8, :].rearrange(
                        "dhi (dlo e) -> dhi dlo e", dlo=8, e=128))),
                    waits=([(cV, v_cmb)] if h == 0 else []), inc=cS, amt=16)
        v_pimp = p.op("sync", lambda s: s.dma_start(out=S_run[:], in_=p_tmp.ap()),
                      waits=[(cS, v_pexp), (cW, v_exp), (cV, v_scan_v)], inc=cS, amt=16)
        exch_stk.close()
        v_dbgP = 0
        if DEBUG:
            p.op("sync", lambda s: s.dma_start(out=dbg["dbg_ag"], in_=ag.ap()),
                 waits=[(cCC, 1)], inc=cS, amt=16)
            v_dbgP = p.op("sync", lambda s: s.dma_start(out=dbg["dbg_P"], in_=S_run[:]),
                 waits=[(cS, v_pimp)], inc=cS, amt=16)

        t_corr_prev = 0
        v_corr = 0
        for j in range(NCH):
            if j > 0:
                p.op("vector", lambda v: v.tensor_tensor(
                    S_run[:], S_run[:], lamct[:], op=ALU.mult), inc=cV)
            v_pbf = p.op("vector", (lambda v: v.tensor_copy(S_bf[:], S_run[:])),
                         waits=([(cS, max(v_pimp, v_dbgP)), (cT, v_scan_t)] if j == 0 else
                                [(cT, t_corr_prev)]), inc=cV)
            v_qd = p.op("vector", (lambda v, j=j: v.tensor_tensor(
                qd_s.ap().rearrange("p (h t) -> p h t", h=H, t=CH),
                qT.ap().rearrange("p (h t) -> p h t", h=H, t=TC)[:, :, j * 128:(j + 1) * 128],
                decayq.ap().rearrange("p (h t) -> p h t", h=H, t=CH), op=ALU.mult)),
                waits=([(cT, t_corr_prev)] if j > 0 else []), inc=cV)
            for g4 in range(4):
                pb_ = 2 + (g4 % 2)
                tmx = 0
                for hh in range(4):
                    hx = g4 * 4 + hh
                    waits = []
                    if hh == 0:
                        waits = [po_free[g4 % 2]]
                        if g4 == 0:
                            waits += [(cV, v_pbf), (cV, v_qd)]
                    tmx = p.op("tensor", (lambda t, hx=hx, pb_=pb_, hh=hh: t.matmul(
                        pf[pb_][:, hh * 128:(hh + 1) * 128],
                        S_bf[:, hx * 128:(hx + 1) * 128],
                        qd_s[:, hx * 128:(hx + 1) * 128],
                        start=True, stop=True, skip_group_check=True)),
                        waits=waits, inc=(cT if hh == 3 else None))
                v_corr = p.op("vector", (lambda v, g4=g4, j=j, pb_=pb_: v.tensor_tensor(
                    attnT.ap().rearrange("p (h t) -> p h t", h=H, t=TC)[:, g4 * 4:(g4 + 1) * 4, j * 128:(j + 1) * 128],
                    attnT.ap().rearrange("p (h t) -> p h t", h=H, t=TC)[:, g4 * 4:(g4 + 1) * 4, j * 128:(j + 1) * 128],
                    pf[pb_].ap().rearrange("p (h t) -> p h t", h=4, t=128), op=ALU.add)),
                    waits=[(cT, tmx)] + ([(cA, v_scan_a)] if j == 0 and g4 == 0 else []),
                    inc=cV)
                po_free[g4 % 2] = (cV, v_corr)
            t_corr_prev = cT.n
        if DEBUG:
            p.op("sync", lambda s: s.dma_start(out=dbg["dbg_attnT"], in_=attnT[:]),
                 waits=[(cV, v_corr)], inc=cS, amt=16)
        base_stk.close()
        snapC = [(cT, cT.n), (cV, cV.n), (cA, cA.n)]

        tail_stk = contextlib.ExitStack()
        sbT, _ = mk_alloc(tail_stk)
        R = sbT("R", [128, KD * TC], F32)
        w_eT = sbT("w_eT", [8, TC], F32)

        mf_stk = contextlib.ExitStack()
        sbMF, psMF = mk_alloc(mf_stk)
        gbuf = [sbMF("gbuf0", [128, D], BF16), sbMF("gbuf1", [128, D], BF16)]
        ubuf = [sbMF("ubuf0", [128, D], BF16), sbMF("ubuf1", [128, D], BF16)]
        dbuf = [sbMF("dbuf0", [128, 2048], BF16), sbMF("dbuf1", [128, 2048], BF16)]
        sg = [sbMF("sg0", [128, TC], BF16), sbMF("sg1", [128, TC], BF16)]
        hu = [sbMF("hu0", [128, TC], BF16), sbMF("hu1", [128, TC], BF16)]
        Wb = sbMF("Wb", [128, TC], BF16)
        wrow = sbMF("wrow", [1, TC], F32)
        pf.append(psMF("pf6", [128, 512], F32))
        pf.append(psMF("pf7", [128, 512], F32))

        p6_stk = contextlib.ExitStack()
        sbP6, _ = mk_alloc(p6_stk)
        wbufB = [sbP6("wbufB0", [128, D], BF16), sbP6("wbufB1", [128, D], BF16)]
        xrb = [sbP6("xrb0", [128, TC], F32), sbP6("xrb1", [128, TC], F32)]
        xr_ready = {}
        for m in range(2):
            xr_ready[m] = p.op("sync", (lambda s, m=m: s.dma_start(
                out=xrb[m][:], in_=xT[m * 128:(m + 1) * 128, :])),
                waits=(snapC if m == 0 else []), inc=cS, amt=16)

        r_ready = {}

        def evac_proj(i, m, half, bk, waits):
            par = m % 2
            vv = p.op("vector", (lambda v, m=m, half=half, bk=bk, par=par: v.scalar_tensor_tensor(
                R[:, m * TC + half * 512: m * TC + half * 512 + 512],
                pf[bk][:, 0:512], bout[:, m:m + 1],
                xrb[par][:, half * 512: half * 512 + 512],
                op0=ALU.add, op1=ALU.add)),
                waits=waits + [(cS, xr_ready[m])], inc=cV)
            if half == 1:
                r_ready[m] = vv
                if m + 2 < NM:
                    xr_ready[m + 2] = p.op("sync", (lambda s, m=m, par=par: s.dma_start(
                        out=xrb[par][:], in_=xT[(m + 2) * 128:(m + 3) * 128, :])),
                        waits=[(cV, vv)], inc=cS, amt=16)
            return (cV, vv)

        wbufB_free = [snapC[0], snapC[0]]
        emit_proj(wout_s, 0, NM, attnT, wbufB, wbufB_free, evac_proj,
                  [(cV, v_corr)] + snapC, {i: snapC[1] for i in range(4)})
        v_R1 = cV.n
        if DEBUG:
            p.op("sync", lambda s: s.dma_start(out=dbg["dbg_R1"], in_=R[:]),
                 waits=[(cV, v_R1)], inc=cS, amt=16)

        x_ln2 = bigA
        ln2_stk = contextlib.ExitStack()
        sbL2, _ = mk_alloc(ln2_stk)
        xb2 = [sbL2("xb2_0", [128, 512], BF16), sbL2("xb2_1", [128, 512], BF16)]
        sqb2 = [sbL2("sqb2_0", [128, 512], BF16), sbL2("sqb2_1", [128, 512], BF16)]
        rows2 = sbL2("rows2", [1, 4096], F32)
        nsc2 = [sbL2("nsc2_0", [128, 512], F32), sbL2("nsc2_1", [128, 512], F32)]
        ready_k2 = [[(cV, r_ready[k])] for k in range(KD)]
        v_ln2 = emit_ln(R, x_ln2, xb2, sqb2, rows2, nsc2, ready_k2, extra0=snapC)
        if DEBUG:
            p.op("sync", lambda s: s.dma_start(out=dbg["dbg_xln2"], in_=x_ln2[:]),
                 waits=[(cV, v_ln2)], inc=cS, amt=16)

        rt_stk = contextlib.ExitStack()
        sbR, _ = mk_alloc(rt_stk)
        lgT = sbR("lgT", [8, TC], F32)
        lg_tm = sbR("lg_tm", [128, 64], F32)
        probs = sbR("probs", [128, 64], F32)
        g1 = sbR("g1", [128, 64], F32)
        g2 = sbR("g2", [128, 64], F32)
        se = sbR("se", [128, 8], F32)
        ke = sbR("ke", [128, 8], F32)
        w_all = sbR("w_all", [128, 64], F32)

        r3 = lambda apx: apx.rearrange("p (tt e) -> p tt e", tt=8, e=8)
        o3 = lambda apx: apx.rearrange("p (tt o) -> p tt o", tt=8, o=1)
        for half in range(2):
            for k in range(KD):
                waits = []
                if k == 0:
                    waits = [(cV, v_ln2), (cW, v_rtr)] if half == 0 else []

                def mm(t, half=half, k=k):
                    return t.matmul(pf[half][0:8, 0:512],
                                    router[:, k * 8:(k + 1) * 8],
                                    x_ln2[:, k * TC + half * 512: k * TC + half * 512 + 512],
                                    start=(k == 0), stop=(k == KD - 1))
                tm = p.op("tensor", mm, waits=waits, inc=(cT if k == KD - 1 else None))
            p.op("vector", (lambda v, half=half: v.tensor_copy(
                lgT[0:8, half * 512:(half + 1) * 512], pf[half][0:8, 0:512])),
                waits=[(cT, tm)], inc=cV)
        v_lgT = cV.n
        for tt in range(8):
            tm = p.op("tensor", (lambda t, tt=tt: t.transpose(
                pf[4][:, tt * 8:(tt + 1) * 8], lgT[0:8, tt * 128:(tt + 1) * 128],
                id_f[0:8, 0:8])), waits=([(cV, v_lgT)] if tt == 0 else []),
                inc=(cT if tt == 7 else None))
        v_lgtm = p.op("vector", lambda v: v.tensor_copy(lg_tm[:], pf[4][:, 0:64]),
                      waits=[(cT, tm)], inc=cV)
        vrm = p.op("vector", lambda v: v.tensor_reduce(
            o3(se.ap()), r3(lg_tm.ap()), axis=AX.X, op=ALU.max), inc=cV)
        p.op("vector", lambda v: v.tensor_tensor(
            r3(g1.ap()), r3(lg_tm.ap()), o3(se.ap()).broadcast_to((128, 8, 8)),
            op=ALU.subtract), waits=[(cV, vrm)], inc=cV)
        v_exp8 = p.op("scalar", lambda a: a.activation(probs[:], g1[:], AF.Exp),
                      waits=[(cV, cV.n)], inc=cA)
        vred = p.op("vector", lambda v: v.tensor_reduce(
            o3(se.ap()), r3(probs.ap()), axis=AX.X, op=ALU.add),
            waits=[(cA, v_exp8)], inc=cV)
        vrec = p.op("vector", lambda v: v.reciprocal(ke[:], se[:]),
                    waits=[(cV, vred)], inc=cV)
        v_pr = p.op("vector", lambda v: v.tensor_tensor(
            r3(probs.ap()), r3(probs.ap()), o3(ke.ap()).broadcast_to((128, 8, 8)),
            op=ALU.mult), waits=[(cV, vrec)], inc=cV)
        if DEBUG:
            p.op("sync", lambda s: s.dma_start(out=dbg["dbg_probs"], in_=probs[:]),
                 waits=[(cV, v_pr)], inc=cS, amt=16)
        v_wcol = 0
        for e in range(NE):
            colb = r3(probs.ap())[:, :, e:e + 1].broadcast_to((128, 8, 8))
            p.op("vector", (lambda v, colb=colb: v.tensor_tensor(
                r3(g1.ap()), r3(probs.ap()), colb, op=ALU.is_gt)), inc=cV)
            p.op("vector", (lambda v, colb=colb: v.tensor_tensor(
                r3(g2.ap()), r3(probs.ap()), colb, op=ALU.is_equal)), inc=cV)
            p.op("vector", (lambda v, e=e: v.tensor_tensor(
                r3(g2.ap()), r3(g2.ap()),
                r3(tlt.ap()[:, e * 64:(e + 1) * 64]), op=ALU.mult)),
                waits=([(cS, v_pre)] if e == 0 else []), inc=cV)
            p.op("vector", lambda v: v.tensor_tensor(
                r3(g1.ap()), r3(g1.ap()), r3(g2.ap()), op=ALU.add), inc=cV)
            p.op("vector", lambda v: v.tensor_tensor(
                r3(g1.ap()), r3(g1.ap()), r3(probs.ap()), op=ALU.mult), inc=cV)
            vr2 = p.op("vector", lambda v: v.tensor_reduce(
                o3(se.ap()), r3(g1.ap()), axis=AX.X, op=ALU.add), inc=cV)
            vk2 = p.op("vector", lambda v: v.tensor_scalar(
                ke[:], se[:], TOP_P, None, op0=ALU.is_le), waits=[(cV, vr2)], inc=cV)
            v_wcol = p.op("vector", (lambda v, e=e: v.tensor_tensor(
                r3(w_all.ap())[:, :, e:e + 1], o3(ke.ap()),
                r3(probs.ap())[:, :, e:e + 1], op=ALU.mult)), waits=[(cV, vk2)], inc=cV)
        if DEBUG:
            p.op("sync", lambda s: s.dma_start(out=dbg["dbg_w"], in_=w_all[:]),
                 waits=[(cV, v_wcol)], inc=cS, amt=16)
        v_weT = 0
        for rnd in range(2):
            tm = 0
            for i4 in range(4):
                tt = rnd * 4 + i4
                waits = []
                if i4 == 0:
                    waits = [(cV, v_wcol)] if rnd == 0 else [(cV, v_weT)]
                tm = p.op("tensor", (lambda t, tt=tt, i4=i4: t.transpose(
                    pf[5][0:8, i4 * 128:(i4 + 1) * 128], w_all[:, tt * 8:(tt + 1) * 8],
                    id_f[:, :])), waits=waits, inc=(cT if i4 == 3 else None))
            v_weT = p.op("vector", (lambda v, rnd=rnd: v.tensor_copy(
                w_eT[0:8, rnd * 512:(rnd + 1) * 512], pf[5][0:8, 0:512])),
                waits=[(cT, tm)], inc=cV)
        v_lgT_v = v_lgT
        rt_stk.close()
        ln2_stk.close()
        p6_t, p6_v, p6_a = cT.n, cV.n, cA.n
        p6_stk.close()
        snapR = [(cT, p6_t), (cV, p6_v), (cA, p6_a)]

        moe_stk = contextlib.ExitStack()
        sbM, _ = mk_alloc(moe_stk)
        hT = sbM("hT", [128, I_T * TC], BF16)

        free_g = [snapC[2], snapC[2]]
        free_u = [snapC[1], snapC[1]]
        free_d = [snapC[0], snapC[0]]
        bank_free = {0: (cV, v_lgT_v), 1: (cV, v_lgT_v), 2: (cV, v_ln2), 3: (cV, v_ln2),
                     4: (cV, v_lgtm), 5: (cV, v_weT), 6: snapC[1], 7: snapC[1]}
        wb_free = (cV, 0)
        sg_free = [snapC[1], snapC[1]]
        hu_free = [snapC[1], snapC[1]]
        hu_guard = [[snapC[1], snapC[1]], [snapC[1], snapC[1]]]
        wrow_free = (cT, 0)
        v2 = 0

        for e in range(NE):
            v_wr = p.op("sync", (lambda s, e=e: s.dma_start(
                out=wrow[:], in_=w_eT[e:e + 1, :])),
                waits=[(cV, v_weT), wrow_free] + (snapC if e == 0 else []), inc=cS, amt=16)
            wb_vpost = [0, 0]
            for hf in range(2):
                bk = hf
                tpost = p.op("tensor", (lambda t, hf=hf, bk=bk: t.matmul(
                    pf[bk][:, 0:512], ones_row[0:1, :],
                    wrow[0:1, hf * 512:(hf + 1) * 512], start=True, stop=True)),
                    waits=[(cS, v_wr), bank_free[bk]], inc=cT)
                wrow_free = (cT, tpost)
                vpost = p.op("vector", (lambda v, hf=hf, bk=bk: v.tensor_copy(
                    Wb[:, hf * 512:(hf + 1) * 512], pf[bk][:, 0:512])),
                    waits=[(cT, tpost), wb_free], inc=cV)
                bank_free[bk] = (cV, vpost)
                wb_vpost[hf] = vpost
            for it in range(I_T):
                par = it % 2
                p.op("gpsimd", (lambda g, e=e, it=it, par=par: g.dma_start(
                    out=gbuf[par][:], in_=gate_s[e, it])),
                    waits=[free_g[par]] + (snapC if (e == 0 and it < 2) else []),
                    inc=cW, amt=16)
                wu = p.op("gpsimd", (lambda g, e=e, it=it, par=par: g.dma_start(
                    out=ubuf[par][:], in_=up_s[e, it])), waits=[free_u[par]], inc=cW, amt=16)
                tg = [0, 0]
                tu = [0, 0]
                for hf in range(2):
                    bkg = par * 4 + hf
                    bku = par * 4 + 2 + hf
                    for k in range(KD):
                        waits = []
                        if k == 0:
                            waits = [hu_guard[par][hf]]
                            if hf == 0:
                                waits.append((cW, wu))
                            if it == 0:
                                waits.append((cV, wb_vpost[hf]))
                            if it < 2:
                                waits.append(bank_free[bkg])
                                waits.append(bank_free[bku])
                            if e == 0 and it < 2:
                                waits.append((cV, v_ln2))

                        def g_mm(t, par=par, hf=hf, k=k, bkg=bkg):
                            return t.matmul(pf[bkg][:, 0:512],
                                            gbuf[par][:, k * 128:(k + 1) * 128],
                                            x_ln2[:, k * TC + hf * 512: k * TC + hf * 512 + 512],
                                            start=(k == 0), stop=(k == KD - 1))
                        tg[hf] = p.op("tensor", g_mm, waits=waits,
                                      inc=(cT if k == KD - 1 else None))
                    for k in range(KD):

                        def u_mm(t, par=par, hf=hf, k=k, bku=bku):
                            return t.matmul(pf[bku][:, 0:512],
                                            ubuf[par][:, k * 128:(k + 1) * 128],
                                            x_ln2[:, k * TC + hf * 512: k * TC + hf * 512 + 512],
                                            start=(k == 0), stop=(k == KD - 1))
                        tu[hf] = p.op("tensor", u_mm, waits=[],
                                      inc=(cT if k == KD - 1 else None))
                for hf in range(2):
                    bkg = par * 4 + hf
                    bku = par * 4 + 2 + hf
                    apost = p.op("scalar", (lambda a, par=par, hf=hf, bkg=bkg: a.activation(
                        sg[par][:, hf * 512:(hf + 1) * 512], pf[bkg][:, 0:512], AF.Silu)),
                        waits=[(cT, tg[hf]), sg_free[par]], inc=cA)
                    free_g[par] = (cA, apost)
                    bank_free[bkg] = (cA, apost)
                    v1 = p.op("vector", (lambda v, par=par, hf=hf, bku=bku: v.tensor_tensor(
                        hu[par][:, hf * 512:(hf + 1) * 512],
                        sg[par][:, hf * 512:(hf + 1) * 512],
                        pf[bku][:, 0:512], op=ALU.mult)),
                        waits=[(cA, apost), (cT, tu[hf]), hu_free[par]], inc=cV)
                    free_u[par] = (cV, v1)
                    bank_free[bku] = (cV, v1)
                    hu_guard[par][hf] = (cV, v1)
                    v2 = p.op("vector", (lambda v, it=it, par=par, hf=hf: v.tensor_tensor(
                        hT[:, it * TC + hf * 512: it * TC + hf * 512 + 512],
                        hu[par][:, hf * 512:(hf + 1) * 512],
                        Wb[:, hf * 512:(hf + 1) * 512], op=ALU.mult)),
                        waits=(([(cV, wb_vpost[1])] + (snapR if e == 0 else []))
                               if (it == 0 and hf == 0) else []), inc=cV)
                    hu_free[par] = (cV, v2)
                    if hf == 1:
                        sg_free[par] = (cV, v2)
            hT_done = v2
            wb_free = (cV, hT_done)
            for m in range(NM):
                bks = ((m % 2) * 2, (m % 2) * 2 + 1)
                td = [0, 0]
                for kg in range(2):
                    par = (2 * m + kg) % 2
                    wd = p.op("gpsimd", (lambda g, e=e, m=m, kg=kg, par=par: g.dma_start(
                        out=dbuf[par][:], in_=down_s[e, m][:, kg * 2048:(kg + 1) * 2048])),
                        waits=[free_d[par]], inc=cW, amt=16)
                    first_waits = [(cW, wd)]
                    if kg == 0:
                        first_waits += [bank_free[bks[1]]]
                        if m < 2:
                            first_waits += [(cV, hT_done), bank_free[bks[0]]]
                    for kk in range(16):
                        k = kg * 16 + kk
                        for hf in range(2):
                            waits = first_waits if (kk == 0 and hf == 0) else []

                            def d_mm(t, par=par, k=k, kk=kk, hf=hf, bk=bks[hf]):
                                return t.matmul(pf[bk][:, 0:512],
                                                dbuf[par][:, kk * 128:(kk + 1) * 128],
                                                hT[:, k * TC + hf * 512: k * TC + hf * 512 + 512],
                                                start=(k == 0), stop=(k == I_T - 1))
                            td[hf] = p.op("tensor", d_mm, waits=waits,
                                          inc=(cT if k == I_T - 1 else None))
                    if kg == 0:
                        tmk = p.op("tensor", (lambda t: t.nop()), inc=cT)
                        free_d[par] = (cT, tmk)
                    else:
                        free_d[par] = (cT, td[1])
                for hf in range(2):
                    vr = p.op("vector", (lambda v, m=m, hf=hf, bk=bks[hf]: v.tensor_tensor(
                        R[:, m * TC + hf * 512: m * TC + hf * 512 + 512],
                        R[:, m * TC + hf * 512: m * TC + hf * 512 + 512],
                        pf[bk][:, 0:512], op=ALU.add)), waits=[(cT, td[hf])], inc=cV)
                    bank_free[bks[hf]] = (cV, vr)

        p.op("sync", lambda s: s.dma_start(
            out=outT.rearrange("(k p) t -> p k t", k=KD, p=128), in_=R[:]),
            waits=[(cV, cV.n)], inc=cS, amt=16)
        p.op("sync", None, waits=[(cS, cS.n)])

        moe_stk.close()
        mf_stk.close()
        tail_stk.close()
        bigA_stk.close()
        block = L0.enter_context(nc.Block())
        p.emit(block)
    return nc


def _pack(inputs):
    x = np.asarray(inputs["L0_attn_x"], dtype=np.float32)
    w_qkv = np.asarray(inputs["L0_attn_w_qkv"], dtype=np.float32)
    b_qkv = np.asarray(inputs["L0_attn_b_qkv"], dtype=np.float32)
    w_out = np.asarray(inputs["L0_attn_w_out"], dtype=np.float32)
    b_out = np.asarray(inputs["L0_attn_b_out"], dtype=np.float32)
    router_w = np.asarray(inputs["L0_ffn_router_w"], dtype=np.float32)
    gate_w = np.asarray(inputs["L0_ffn_gate_w"], dtype=np.float32)
    up_w = np.asarray(inputs["L0_ffn_up_w"], dtype=np.float32)
    down_w = np.asarray(inputs["L0_ffn_down_w"], dtype=np.float32)

    xT = np.ascontiguousarray(x.reshape(T, D).T)
    qkv_sv = np.ascontiguousarray(
        w_qkv.reshape(QKV_M, 128, KD, 128).transpose(0, 3, 2, 1)).reshape(QKV_M, 128, D)
    wout_sv = np.ascontiguousarray(
        w_out.reshape(NM, 128, KD, 128).transpose(0, 3, 2, 1)).reshape(NM, 128, D)
    bqkv = np.ascontiguousarray(b_qkv.reshape(QKV_M, 128).T)
    boutp = np.ascontiguousarray(b_out.reshape(NM, 128).T)
    router_p = np.ascontiguousarray(
        router_w.T.reshape(KD, 128, NE).transpose(1, 0, 2)).reshape(128, 128)
    gate_sv = np.ascontiguousarray(
        gate_w.reshape(NE, I_T, 128, KD, 128).transpose(0, 1, 4, 3, 2)).reshape(NE, I_T, 128, D)
    up_sv = np.ascontiguousarray(
        up_w.reshape(NE, I_T, 128, KD, 128).transpose(0, 1, 4, 3, 2)).reshape(NE, I_T, 128, D)
    down_sv = np.ascontiguousarray(
        down_w.reshape(NE, NM, 128, I_T, 128).transpose(0, 1, 4, 3, 2)).reshape(NE, NM, 128, I_T * 128)

    lam, decayq, dmaskT, decayk, lamc = _host_consts()
    shared = {
        "qkv_s": qkv_sv, "bqkv": bqkv, "wout_s": wout_sv, "bout": boutp,
        "router_p": router_p, "gate_s": gate_sv, "up_s": up_sv, "down_s": down_sv,
        "dmaskT": dmaskT, "decayq": decayq, "decayk": decayk, "lamc": lamc,
        "tlt": _tlt(), "ident": np.eye(128, dtype=np.float32),
    }
    in_maps = []
    for c in range(NCORES):
        m = dict(shared)
        m["xT"] = np.ascontiguousarray(xT[:, c * TC:(c + 1) * TC])
        m["coef"] = _host_coef(c, lam)
        in_maps.append(m)
    return in_maps


def kernel(**inputs):
    in_maps = _pack(inputs)
    nc = build()
    res = run_bass_kernel_spmd(nc, in_maps, core_ids=list(range(NCORES)))
    if DEBUG:
        _LAST["res"] = res
        _LAST["in_maps"] = in_maps
    parts = [np.asarray(res.results[c]["outT"], dtype=np.float32) for c in range(NCORES)]
    full = np.concatenate(parts, axis=1)
    return np.ascontiguousarray(full.T).reshape(1, T, D)


# version 05
# speedup vs baseline: 1.0175x; 1.0128x over previous
import contextlib

import numpy as np

import concourse.bass as bass
import concourse.mybir as mybir
from concourse.bass_utils import run_bass_kernel_spmd

F32 = mybir.dt.float32
BF16 = mybir.dt.bfloat16
AF = mybir.ActivationFunctionType
ALU = mybir.AluOpType
AX = mybir.AxisListType

NCORES = 8
D = 2048
T = 8192
TC = T // NCORES
KD = 16
H = 16
CH = 128
NCH = TC // CH
NE = 8
I_T = 32
NM = 16
QKV_M = 48
TOP_P = 0.9
EPS = 1e-5

DEBUG = False
_LAST = {}


class Ctr:
    __slots__ = ("h", "n")

    def __init__(self, h):
        self.h = h
        self.n = 0


class Prog:
    ENGINES = ("sync", "gpsimd", "tensor", "vector", "scalar")

    def __init__(self):
        self.ops = {e: [] for e in self.ENGINES}

    def op(self, eng, fn, waits=(), inc=None, amt=1):
        post = None
        if inc is not None:
            inc.n += amt
            post = inc.n
        ws = tuple((c, v) for (c, v) in waits if v > 0)
        self.ops[eng].append((fn, ws, inc, amt if inc is not None else 0))
        return post

    def emit(self, block):
        for eng in self.ENGINES:
            def mk(e):
                def body(h):
                    for fn, waits, inc, amt in self.ops[e]:
                        for (c, v) in waits:
                            h.wait_ge(c.h, v)
                        if fn is not None:
                            inst = fn(h)
                            if inc is not None:
                                inst.then_inc(inc.h, amt)
                return body
            getattr(block, eng)(mk(eng))


def _host_consts():
    hh = np.arange(H, dtype=np.float64)
    slopes = (2.0 ** -0.5) ** (hh + 1.0)
    lam = np.exp(-slopes)
    pos = np.arange(CH, dtype=np.float64)
    dq = lam[:, None] ** (pos[None, :] + 1.0)
    decayq = np.broadcast_to(dq.reshape(1, H * CH), (128, H * CH)).astype(np.float32)
    diff = pos[None, None, :] - pos[None, :, None]
    dm = np.where(diff >= 0, lam[:, None, None] ** np.maximum(diff, 0.0), 0.0)
    dmaskT = np.ascontiguousarray(dm.transpose(1, 0, 2).reshape(CH, H * CH)).astype(np.float32)
    dk = lam[:, None] ** (CH - 1.0 - pos[None, :])
    decayk = np.ascontiguousarray(
        np.broadcast_to(dk.T[:, :, None], (CH, H, CH)).reshape(CH, H * CH)).astype(np.float32)
    lc = lam ** CH
    lamc = np.ascontiguousarray(
        np.broadcast_to(lc[None, :, None], (CH, H, CH)).reshape(CH, H * CH)).astype(np.float32)
    return lam, decayq, dmaskT, decayk, lamc


def _host_coef(core, lam):
    out = np.zeros((128, NCORES), dtype=np.float64)
    for pp in range(128):
        h = pp // 8
        for r in range(core):
            out[pp, r] = lam[h] ** (TC * (core - 1 - r))
    return out.astype(np.float32)


def _tlt():
    out = np.zeros((128, NE * 64), dtype=np.float32)
    for e in range(NE):
        for tt in range(8):
            out[:, e * 64 + tt * 8: e * 64 + tt * 8 + e] = 1.0
    return out


def build():
    nc = bass.Bass("TRN2", target_bir_lowering=False, debug=False)

    xT = nc.dram_tensor("xT", [D, TC], F32, kind="ExternalInput").ap()
    qkv_s = nc.dram_tensor("qkv_s", [QKV_M, 128, D], F32, kind="ExternalInput").ap()
    bqkv_d = nc.dram_tensor("bqkv", [128, QKV_M], F32, kind="ExternalInput").ap()
    wout_s = nc.dram_tensor("wout_s", [NM, 128, D], F32, kind="ExternalInput").ap()
    bout_d = nc.dram_tensor("bout", [128, NM], F32, kind="ExternalInput").ap()
    router_d = nc.dram_tensor("router_p", [128, 128], F32, kind="ExternalInput").ap()
    gate_s = nc.dram_tensor("gate_s", [NE, I_T, 128, D], F32, kind="ExternalInput").ap()
    up_s = nc.dram_tensor("up_s", [NE, I_T, 128, D], F32, kind="ExternalInput").ap()
    down_s = nc.dram_tensor("down_s", [NE, NM, 128, I_T * 128], F32, kind="ExternalInput").ap()
    dmaskT_d = nc.dram_tensor("dmaskT", [128, H * CH], F32, kind="ExternalInput").ap()
    decayq_d = nc.dram_tensor("decayq", [128, H * CH], F32, kind="ExternalInput").ap()
    decayk_d = nc.dram_tensor("decayk", [128, H * CH], F32, kind="ExternalInput").ap()
    lamc_d = nc.dram_tensor("lamc", [128, H * CH], F32, kind="ExternalInput").ap()
    tlt_d = nc.dram_tensor("tlt", [128, NE * 64], F32, kind="ExternalInput").ap()
    coef_d = nc.dram_tensor("coef", [128, NCORES], F32, kind="ExternalInput").ap()
    ident_d = nc.dram_tensor("ident", [128, 128], F32, kind="ExternalInput").ap()
    outT = nc.dram_tensor("outT", [D, TC], F32, kind="ExternalOutput").ap()
    dbg = {}
    if DEBUG:
        for nm_, shape, dt in [
            ("dbg_xln1", [128, KD * TC], BF16), ("dbg_A", [128, H * CH], F32),
            ("dbg_P", [128, H * CH], F32), ("dbg_attnT", [128, H * TC], BF16),
            ("dbg_R1", [128, KD * TC], F32), ("dbg_probs", [128, 64], F32),
            ("dbg_w", [128, 64], F32), ("dbg_kT", [128, KD * TC], BF16),
            ("dbg_vtm", [128, NCH * D], BF16), ("dbg_xln2", [128, KD * TC], BF16),
            ("dbg_ag", [NCORES * 128, H * CH], BF16),
        ]:
            dbg[nm_] = nc.dram_tensor(nm_, shape, dt, kind="ExternalOutput").ap()

    a_loc = nc.dram_tensor("a_loc", [128, H * CH], BF16, kind="Internal")
    ag = nc.dram_tensor("ag", [NCORES * 128, H * CH], BF16, kind="Internal",
                        addr_space="Shared")
    p_tmp = nc.dram_tensor("p_tmp", [128, H * CH], F32, kind="Internal")

    p = Prog()
    L0 = contextlib.ExitStack()

    def mk_alloc(stk):
        return (lambda name, shape, dt: stk.enter_context(nc.sbuf_tensor(name, shape, dt)),
                lambda name, shape, dt: stk.enter_context(nc.psum_tensor(name, shape, dt)))

    with L0:
        sb0, ps0 = mk_alloc(L0)
        sem = lambda name: Ctr(L0.enter_context(nc.semaphore(name)))
        cS = sem("cS")
        cW = sem("cW")
        cT = sem("cT")
        cV = sem("cV")
        cA = sem("cA")
        cCC = sem("cCC")
        cX = sem("cX")

        ones_b = sb0("ones_b", [128, 1], BF16)
        ones_row = sb0("ones_row", [1, 128], F32)
        id_f = sb0("id_f", [128, 128], F32)
        id_b = sb0("id_b", [128, 128], BF16)
        bqkv = sb0("bqkv_sb", [128, QKV_M], F32)
        bout = sb0("bout_sb", [128, NM], F32)
        router = sb0("router_sb", [128, 128], BF16)
        tlt = sb0("tlt_sb", [128, NE * 64], F32)
        coef = sb0("coef_sb", [128, NCORES], F32)
        pf = [ps0("pf%d" % i, [128, 512], F32) for i in range(6)]

        v_pre = 0
        for fn in [
            lambda s: s.dma_start(out=bqkv[:], in_=bqkv_d),
            lambda s: s.dma_start(out=bout[:], in_=bout_d),
            lambda s: s.dma_start(out=tlt[:], in_=tlt_d),
            lambda s: s.dma_start(out=coef[:], in_=coef_d),
            lambda s: s.dma_start(out=id_f[:], in_=ident_d),
        ]:
            v_pre = p.op("sync", fn, inc=cS, amt=16)
        v_idb = p.op("gpsimd", lambda g: g.dma_start(out=id_b[:], in_=ident_d), inc=cW, amt=16)
        v_rtr = p.op("gpsimd", lambda g: g.dma_start(out=router[:], in_=router_d), inc=cW, amt=16)
        p.op("vector", lambda v: v.memset(ones_b[:], 1.0), inc=cV)
        p.op("vector", lambda v: v.memset(ones_row[:], 1.0), inc=cV)
        v_ones = cV.n

        def emit_ln(src, dst, xb, sqb, rows, nsc, ready_k, extra0=()):
            xb_free = [(cT, 0), (cT, 0)]
            sq_free = [(cT, 0), (cT, 0)]
            for half in range(2):
                for k in range(KD):
                    par = k % 2
                    vw = [xb_free[par]]
                    aw = [sq_free[par]]
                    if half == 0:
                        vw = vw + list(ready_k[k])
                        aw = aw + list(ready_k[k])
                        if k == 0:
                            vw = vw + [(cV, v_ones)] + list(extra0)
                            aw = aw + list(extra0)
                    vc = p.op("vector", (lambda v, k=k, half=half, par=par: v.tensor_copy(
                        xb[par][:, 0:512],
                        src[:, k * TC + half * 512: k * TC + half * 512 + 512])),
                        waits=vw, inc=cV)
                    va = p.op("scalar", (lambda a, k=k, half=half, par=par: a.activation(
                        sqb[par][:, 0:512],
                        src[:, k * TC + half * 512: k * TC + half * 512 + 512],
                        AF.Square)), waits=aw, inc=cA)
                    tm_s = p.op("tensor", (lambda t, k=k, par=par: t.matmul(
                        pf[0][0:1, 0:512], ones_b[:, 0:1], xb[par][:, 0:512],
                        start=(k == 0), stop=(k == KD - 1))), waits=[(cV, vc)], inc=cT)
                    xb_free[par] = (cT, tm_s)
                    tm_q = p.op("tensor", (lambda t, k=k, par=par: t.matmul(
                        pf[1][0:1, 0:512], ones_b[:, 0:1], sqb[par][:, 0:512],
                        start=(k == 0), stop=(k == KD - 1))), waits=[(cA, va)], inc=cT)
                    sq_free[par] = (cT, tm_q)
                ve = p.op("vector", (lambda v, half=half: v.tensor_scalar(
                    rows[0:1, half * 512: half * 512 + 512], pf[0][0:1, 0:512],
                    1.0 / D, None, op0=ALU.mult)), waits=[(cT, tm_s)], inc=cV)
                ve2 = p.op("vector", (lambda v, half=half: v.tensor_scalar(
                    rows[0:1, 1024 + half * 512: 1024 + half * 512 + 512], pf[1][0:1, 0:512],
                    1.0 / D, None, op0=ALU.mult)), waits=[(cT, tm_q)], inc=cV)
                if half == 0:
                    p.op("tensor", None, waits=[(cV, ve2)])
            p.op("vector", lambda v: v.tensor_tensor(
                rows[0:1, 2048:3072], rows[0:1, 0:1024], rows[0:1, 0:1024], op=ALU.mult), inc=cV)
            v_var = p.op("vector", lambda v: v.scalar_tensor_tensor(
                rows[0:1, 3072:4096], rows[0:1, 1024:2048], EPS, rows[0:1, 2048:3072],
                op0=ALU.add, op1=ALU.subtract), inc=cV)
            a_sd = p.op("scalar", lambda a: a.activation(
                rows[0:1, 2048:3072], rows[0:1, 3072:4096], AF.Sqrt),
                waits=[(cV, v_var)], inc=cA)
            v_rstd = p.op("vector", lambda v: v.reciprocal(
                rows[0:1, 3072:4096], rows[0:1, 2048:3072]), waits=[(cA, a_sd)], inc=cV)
            for half in range(2):
                p.op("tensor", (lambda t, half=half: t.matmul(
                    pf[2 + half][:, 0:512], ones_row[0:1, :],
                    rows[0:1, half * 512: half * 512 + 512], start=True, stop=True)),
                    waits=[(cV, v_rstd)] if half == 0 else [], inc=cT)
                t_bc = p.op("tensor", (lambda t, half=half: t.matmul(
                    pf[4 + half][:, 0:512], ones_row[0:1, :],
                    rows[0:1, 3072 + half * 512: 3072 + half * 512 + 512],
                    start=True, stop=True)), inc=cT)
            last = 0
            for k in range(KD):
                for half in range(2):
                    par = k % 2
                    p.op("vector", (lambda v, k=k, half=half, par=par: v.tensor_tensor(
                        nsc[par][:, 0:512],
                        src[:, k * TC + half * 512: k * TC + half * 512 + 512],
                        pf[2 + half][:, 0:512], op=ALU.subtract)),
                        waits=([(cT, t_bc)] if (k == 0 and half == 0) else []), inc=cV)
                    last = p.op("vector", (lambda v, k=k, half=half, par=par: v.tensor_tensor(
                        dst[:, k * TC + half * 512: k * TC + half * 512 + 512],
                        nsc[par][:, 0:512],
                        pf[4 + half][:, 0:512], op=ALU.mult)), inc=cV)
            return last

        def emit_proj(wdram, m0, m1, act, wbuf, wbuf_free, evac, act_ready, bank_init):
            bank_free = dict(bank_init)
            for i, m in enumerate(range(m0, m1)):
                par = i % 2
                wload = p.op("gpsimd", (lambda g, m=m, par=par: g.dma_start(
                    out=wbuf[par][:], in_=wdram[m])), waits=[wbuf_free[par]], inc=cW, amt=16)
                tm = [0, 0]
                for half in range(2):
                    bk = par * 2 + half
                    for k in range(KD):
                        waits = []
                        if k == 0:
                            waits = [bank_free[bk]]
                            if half == 0:
                                waits.append((cW, wload))
                            if i < 2:
                                waits = waits + list(act_ready)

                        def mm(t, m=m, par=par, half=half, k=k, bk=bk):
                            return t.matmul(pf[bk][:, 0:512],
                                            wbuf[par][:, k * 128:(k + 1) * 128],
                                            act[:, k * TC + half * 512: k * TC + half * 512 + 512],
                                            start=(k == 0), stop=(k == KD - 1))
                        tmv = p.op("tensor", mm, waits=waits, inc=(cT if k == KD - 1 else None))
                    tm[half] = tmv
                wbuf_free[par] = (cT, tm[1])
                for half in range(2):
                    bk = par * 2 + half
                    bank_free[bk] = evac(i, m, half, bk, [(cT, tm[half])])

        bigA_stk = contextlib.ExitStack()
        sbBA, _ = mk_alloc(bigA_stk)
        bigA = sbBA("bigA", [128, KD * TC], BF16)
        x_ln1 = bigA

        base_stk = contextlib.ExitStack()
        sbB, _ = mk_alloc(base_stk)
        decayq = sbB("decayq_sb", [128, H * CH], F32)
        lamct = sbB("lamc_sb", [128, H * CH], F32)
        S_run = sbB("S_run", [128, H * CH], F32)
        S_bf = sbB("S_bf", [128, H * CH], BF16)
        qd_s = sbB("qd_s", [128, H * CH], BF16)
        s_tmp = sbB("s_tmp", [128, 512], F32)
        qT = sbB("qT", [128, KD * TC], BF16)

        v_tb0 = 0
        for dst_t, src_t in [(decayq, decayq_d), (lamct, lamc_d)]:
            v_tb0 = p.op("sync", (lambda s, dst_t=dst_t, src_t=src_t: s.dma_start(
                out=dst_t[:], in_=src_t)), inc=cS, amt=16)

        ln1_stk = contextlib.ExitStack()
        sbL, _ = mk_alloc(ln1_stk)
        x_f32 = sbL("x_f32", [128, KD * TC], F32)
        xb1 = [sbL("xb1_0", [128, 512], BF16), sbL("xb1_1", [128, 512], BF16)]
        sqb = [sbL("sqb0", [128, 512], BF16), sbL("sqb1", [128, 512], BF16)]
        rows = sbL("rows", [1, 4096], F32)
        nsc = [sbL("nsc0", [128, 512], F32), sbL("nsc1", [128, 512], F32)]
        v_xload = p.op("sync", lambda s: s.dma_start(
            out=x_f32[:], in_=xT.rearrange("(k p) t -> p k t", k=KD, p=128)), inc=cS, amt=16)
        ready_k1 = [([(cS, cS.n)] if k == 0 else []) for k in range(KD)]
        v_ln1 = emit_ln(x_f32, x_ln1, xb1, sqb, rows, nsc, ready_k1)
        if DEBUG:
            p.op("sync", lambda s: s.dma_start(out=dbg["dbg_xln1"], in_=x_ln1[:]),
                 waits=[(cV, v_ln1)], inc=cS, amt=16)
        ln1_t, ln1_v, ln1_a = cT.n, cV.n, cA.n
        ln1_stk.close()
        snap1 = [(cT, ln1_t), (cV, ln1_v), (cA, ln1_a)]

        scan_stk = contextlib.ExitStack()
        sbA, psA = mk_alloc(scan_stk)
        dmaskT = sbA("dmaskT_sb", [128, H * CH], F32)
        decayk = sbA("decayk_sb", [128, H * CH], F32)
        kT = sbA("kT", [128, KD * TC], BF16)
        v_tm = sbA("v_tm", [128, NCH * D], BF16)
        kd_tm = sbA("kd_tm", [128, H * CH], BF16)
        sT_s = sbA("sT_s", [128, H * CH], BF16)
        vtmp = [sbA("vtmp0", [128, TC], BF16), sbA("vtmp1", [128, TC], BF16)]
        wbufA = [sbA("wbufA0", [128, D], BF16), sbA("wbufA1", [128, D], BF16)]
        ptb = [psA("ptb0", [128, 1024], BF16), psA("ptb1", [128, 1024], BF16)]

        v_tabs = 0
        for i, (dst_t, src_t) in enumerate([(dmaskT, dmaskT_d), (decayk, decayk_d)]):
            v_tabs = p.op("sync", (lambda s, dst_t=dst_t, src_t=src_t: s.dma_start(
                out=dst_t[:], in_=src_t)), waits=(snap1 if i == 0 else []), inc=cS, amt=16)

        wbuf_free = [snap1[0], snap1[0]]
        bank_free_p = {i: snap1[1] for i in range(4)}
        tb_free = [snap1[1], snap1[1]]
        vtmp_free = [(cT, 0), (cT, 0)]
        vtm_last = 0
        for h in range(H):
            m = 32 + h
            par = h % 2
            wload = p.op("gpsimd", (lambda g, m=m, par=par: g.dma_start(
                out=wbufA[par][:], in_=qkv_s[m])), waits=[wbuf_free[par]], inc=cW, amt=16)
            tmh = [0, 0]
            for half in range(2):
                bk = par * 2 + half
                for k in range(KD):
                    waits = []
                    if k == 0:
                        waits = [bank_free_p[bk]]
                        if half == 0:
                            waits.append((cW, wload))
                        if h < 2:
                            waits.append((cV, v_ln1))

                    def mm(t, m=m, par=par, half=half, k=k, bk=bk):
                        return t.matmul(pf[bk][:, 0:512],
                                        wbufA[par][:, k * 128:(k + 1) * 128],
                                        x_ln1[:, k * TC + half * 512: k * TC + half * 512 + 512],
                                        start=(k == 0), stop=(k == KD - 1))
                    tmv = p.op("tensor", mm, waits=waits, inc=(cT if k == KD - 1 else None))
                tmh[half] = tmv
            wbuf_free[par] = (cT, tmh[1])
            va = 0
            for half in range(2):
                bk = par * 2 + half
                va = p.op("scalar", (lambda a, m=m, half=half, bk=bk, par=par: a.activation(
                    vtmp[par][:, half * 512:(half + 1) * 512],
                    pf[bk][:, 0:512], AF.Identity, bias=bqkv[:, m:m + 1], scale=1.0)),
                    waits=[(cT, tmh[half])] + ([vtmp_free[par], (cS, v_pre)] if half == 0 else []),
                    inc=cA)
                bank_free_p[bk] = (cA, va)
            tmt = 0
            for j in range(NCH):
                w0 = []
                if j == 0:
                    w0 = [(cA, va), tb_free[par]]
                    if h == 0:
                        w0.append((cW, v_idb))
                tmt = p.op("tensor", (lambda t, j=j, par=par: t.transpose(
                    ptb[par][:, j * 128:(j + 1) * 128],
                    vtmp[par][:, j * 128:(j + 1) * 128], id_b[:, :])),
                    waits=w0, inc=(cT if j == NCH - 1 else None))
            vtmp_free[par] = (cT, tmt)
            vtm_last = p.op("vector", (lambda v, h=h, par=par: v.tensor_copy(
                v_tm.ap().rearrange("p (j he) -> p j he", j=NCH, he=D)[:, :, h * 128:(h + 1) * 128],
                ptb[par].ap().rearrange("p (j e) -> p j e", j=NCH, e=128))),
                waits=[(cT, tmt)], inc=cV)
            tb_free[par] = (cV, vtm_last)

        def evac_kv(i, m, half, bk, waits):
            mrel = m - 16
            va = p.op("scalar", (lambda a, m=m, half=half, bk=bk, mrel=mrel: a.activation(
                kT[:, mrel * TC + half * 512: mrel * TC + half * 512 + 512],
                pf[bk][:, 0:512], AF.Identity, bias=bqkv[:, m:m + 1], scale=1.0)),
                waits=waits, inc=cA)
            return (cA, va)

        def evac_q(i, m, half, bk, waits):
            va = p.op("scalar", (lambda a, m=m, half=half, bk=bk: a.activation(
                qT[:, m * TC + half * 512: m * TC + half * 512 + 512],
                pf[bk][:, 0:512], AF.Identity, bias=bqkv[:, m:m + 1], scale=1.0)),
                waits=waits, inc=cA)
            return (cA, va)

        emit_proj(qkv_s, 16, 32, x_ln1, wbufA, wbuf_free, evac_kv,
                  [(cV, v_ln1)], dict(bank_free_p))
        emit_proj(qkv_s, 0, 16, x_ln1, wbufA, wbuf_free, evac_q,
                  [(cV, v_ln1)], {i: (cA, cA.n) for i in range(4)})
        v_proj_a = cA.n
        v_proj_t = cT.n

        v_s0 = p.op("vector", lambda v: v.memset(S_run[:], 0.0),
                    waits=[(cA, ln1_a), (cT, ln1_t)], inc=cV)
        sc_free = [(cV, 0), (cV, 0)]
        po_free = [(cA, v_proj_a), (cA, v_proj_a)]
        kv_free = [(cV, v_ln1), (cV, v_ln1)]

        def emit_kd(jj, first_waits):
            outs = []
            for g in range(2):
                tmx = 0
                for hh in range(8):
                    hx = g * 8 + hh
                    waits = []
                    if hh == 0:
                        waits = [tb_free[g]]
                        if g == 0:
                            waits = waits + first_waits
                    tmx = p.op("tensor", (lambda t, hx=hx, hh=hh, g=g, jj=jj: t.transpose(
                        ptb[g][:, hh * 128:(hh + 1) * 128],
                        kT[:, hx * TC + jj * 128: hx * TC + jj * 128 + 128],
                        id_b[:, :])), waits=waits, inc=(cT if hh == 7 else None))
                ve = p.op("vector", (lambda v, g=g: v.tensor_tensor(
                    kd_tm[:, g * 1024:(g + 1) * 1024], ptb[g][:, :],
                    decayk[:, g * 1024:(g + 1) * 1024], op=ALU.mult)),
                    waits=[(cT, tmx)], inc=cV)
                tb_free[g] = (cV, ve)
                outs.append(ve)
            return outs

        def emit_kv_s(jj, kd_ready, vtm_ready):
            for g4 in range(4):
                kb = 4 + (g4 % 2)
                tmx = 0
                for hh in range(4):
                    hx = g4 * 4 + hh
                    waits = []
                    if hh == 0:
                        waits = [kv_free[g4 % 2], (cV, kd_ready[0 if g4 < 2 else 1])]
                        if vtm_ready:
                            waits.append((cV, vtm_ready))
                    tmx = p.op("tensor", (lambda t, hx=hx, jj=jj, kb=kb, hh=hh: t.matmul(
                        pf[kb][:, hh * 128:(hh + 1) * 128],
                        kd_tm[:, hx * 128:(hx + 1) * 128],
                        v_tm[:, jj * D + hx * 128: jj * D + (hx + 1) * 128],
                        start=True, stop=True, skip_group_check=True)),
                        waits=waits, inc=(cT if hh == 3 else None))
                p.op("vector", (lambda v, g4=g4: v.tensor_tensor(
                    s_tmp[:, 0:512], S_run[:, g4 * 512:(g4 + 1) * 512],
                    lamct[:, g4 * 512:(g4 + 1) * 512], op=ALU.mult)), inc=cV)
                v2 = p.op("vector", (lambda v, g4=g4, kb=kb: v.tensor_tensor(
                    S_run[:, g4 * 512:(g4 + 1) * 512], s_tmp[:, 0:512],
                    pf[kb][:, 0:512], op=ALU.add)), waits=[(cT, tmx)], inc=cV)
                kv_free[g4 % 2] = (cV, v2)

        attnT = bigA
        t_prev_j = 0
        for j in range(NCH):
            w_sbf = [(cT, t_prev_j)] if j > 0 else [(cV, v_s0)]
            v_sbf = p.op("vector", (lambda v: v.tensor_copy(S_bf[:], S_run[:])),
                         waits=w_sbf, inc=cV)
            v_qd = p.op("vector", (lambda v, j=j: v.tensor_tensor(
                qd_s.ap().rearrange("p (h t) -> p h t", h=H, t=CH),
                qT.ap().rearrange("p (h t) -> p h t", h=H, t=TC)[:, :, j * 128:(j + 1) * 128],
                decayq.ap().rearrange("p (h t) -> p h t", h=H, t=CH), op=ALU.mult)),
                waits=([(cT, t_prev_j)] if j > 0 else [(cA, v_proj_a), (cS, cS.n)]), inc=cV)
            kde = emit_kd(j, [(cS, cS.n), (cT, v_proj_t)] if j == 0 else [])
            sT_ready = [0, 0, 0, 0]
            for g4 in range(4):
                sb_ = g4 % 2
                tmx = 0
                for hh in range(4):
                    hx = g4 * 4 + hh
                    waits = [sc_free[sb_]] if hh == 0 else []
                    tmx = p.op("tensor", (lambda t, hx=hx, j=j, sb_=sb_, hh=hh: t.matmul(
                        pf[sb_][:, hh * 128:(hh + 1) * 128],
                        kT[:, hx * TC + j * 128: hx * TC + j * 128 + 128],
                        qT[:, hx * TC + j * 128: hx * TC + j * 128 + 128],
                        start=True, stop=True, skip_group_check=True)),
                        waits=waits, inc=(cT if hh == 3 else None))
                vm = p.op("vector", (lambda v, g4=g4, sb_=sb_: v.tensor_tensor(
                    sT_s[:, g4 * 512:(g4 + 1) * 512], pf[sb_][:, 0:512],
                    dmaskT[:, g4 * 512:(g4 + 1) * 512], op=ALU.mult)),
                    waits=[(cT, tmx)], inc=cV)
                sc_free[sb_] = (cV, vm)
                sT_ready[g4] = vm
            for g4 in range(4):
                pb_ = 2 + (g4 % 2)
                tmx = 0
                for hh in range(4):
                    hx = g4 * 4 + hh
                    waits = []
                    if hh == 0:
                        waits = [po_free[g4 % 2], (cV, sT_ready[g4])]
                        if g4 == 0:
                            waits.append((cV, v_qd))
                            waits.append((cV, v_sbf))
                    p.op("tensor", (lambda t, hx=hx, j=j, pb_=pb_, hh=hh: t.matmul(
                        pf[pb_][:, hh * 128:(hh + 1) * 128],
                        v_tm[:, j * D + hx * 128: j * D + (hx + 1) * 128],
                        sT_s[:, hx * 128:(hx + 1) * 128],
                        start=True, stop=False, skip_group_check=True)), waits=waits)
                    tmx = p.op("tensor", (lambda t, hx=hx, pb_=pb_, hh=hh: t.matmul(
                        pf[pb_][:, hh * 128:(hh + 1) * 128],
                        S_bf[:, hx * 128:(hx + 1) * 128],
                        qd_s[:, hx * 128:(hx + 1) * 128],
                        start=False, stop=True, skip_group_check=True)),
                        inc=(cT if hh == 3 else None))
                va = p.op("scalar", (lambda a, g4=g4, j=j, pb_=pb_: a.activation(
                    attnT.ap().rearrange("p (h t) -> p h t", h=H, t=TC)[:, g4 * 4:(g4 + 1) * 4, j * 128:(j + 1) * 128],
                    pf[pb_].ap().rearrange("p (h t) -> p h t", h=4, t=128),
                    AF.Identity)), waits=[(cT, tmx)], inc=cA)
                po_free[g4 % 2] = (cA, va)
            emit_kv_s(j, kde, vtm_last if j == 0 else 0)
            t_prev_j = cT.n
        v_scan_t = cT.n
        v_scan_v = cV.n
        v_scan_a = cA.n

        aL = a_loc.ap().rearrange("(h2 dhi) (dlo e) -> h2 dhi dlo e", h2=16, dhi=8, dlo=16, e=128)
        v_exp = 0
        for h in range(H):
            v_exp = p.op("gpsimd", (lambda g, h=h: g.dma_start(
                out=aL[h], in_=S_run[:, h * 128:(h + 1) * 128])),
                waits=([(cV, v_scan_v)] if h == 0 else []), inc=cW, amt=16)
        if DEBUG:
            p.op("sync", lambda s: s.dma_start(out=dbg["dbg_A"], in_=S_run[:]),
                 waits=[(cV, v_scan_v)], inc=cS, amt=16)
            p.op("sync", lambda s: s.dma_start(out=dbg["dbg_kT"], in_=kT[:]), inc=cS, amt=16)
            p.op("sync", lambda s: s.dma_start(out=dbg["dbg_vtm"], in_=v_tm[:]), inc=cS, amt=16)
        v_cc = p.op("gpsimd", lambda g: g.collective_compute(
            "AllGather", ALU.bypass, replica_groups=[list(range(NCORES))],
            ins=[a_loc.ap()], outs=[ag.ap()]), waits=[(cW, v_exp)], inc=cCC)

        scan_stk.close()
        snapS = [(cT, v_scan_t), (cV, v_scan_v), (cA, v_scan_a)]

        corr_stk = contextlib.ExitStack()
        sbC, _ = mk_alloc(corr_stk)
        qd_all = sbC("qd_all", [128, NCH * H * CH], BF16)
        S_bfp = [sbC("S_bfp0", [128, H * CH], BF16), sbC("S_bfp1", [128, H * CH], BF16)]
        exch_stk = contextlib.ExitStack()
        sbE, _ = mk_alloc(exch_stk)
        ar_all = sbE("ar_all", [128, NCORES * H * CH], BF16)
        P_sb = sbE("P_sb", [128, H * CH], F32)

        v_qdall = 0
        for j in range(NCH):
            v_qdall = p.op("vector", (lambda v, j=j: v.tensor_tensor(
                qd_all.ap().rearrange("p (j2 h t) -> p (j2 h) t", j2=NCH, h=H, t=CH)[:, j * H:(j + 1) * H, :],
                qT.ap().rearrange("p (h t) -> p h t", h=H, t=TC)[:, :, j * 128:(j + 1) * 128],
                decayq.ap().rearrange("p (h t) -> p h t", h=H, t=CH), op=ALU.mult)),
                waits=(snapS + [(cA, v_proj_a), (cS, cS.n)] if j == 0 else []), inc=cV)

        v_arl = p.op("gpsimd", lambda g: g.dma_start(
            out=ar_all[:], in_=ag.ap().rearrange("(r p) c -> p r c", r=NCORES, p=128)),
            waits=[(cCC, 1)], inc=cW, amt=16)
        v_cmb = 0
        for r in range(NCORES):
            if r == 0:
                v_cmb = p.op("vector", (lambda v, r=r: v.tensor_scalar(
                    P_sb[:], ar_all[:, r * H * CH:(r + 1) * H * CH],
                    coef[:, r:r + 1], None, op0=ALU.mult)),
                    waits=[(cW, v_arl), (cS, v_pre)], inc=cV)
            else:
                v_cmb = p.op("vector", (lambda v, r=r: v.scalar_tensor_tensor(
                    P_sb[:], ar_all[:, r * H * CH:(r + 1) * H * CH],
                    coef[:, r:r + 1], P_sb[:], op0=ALU.mult, op1=ALU.add)), inc=cV)
        pT = p_tmp.ap().rearrange("(dhi dlo) (h2 e) -> h2 dhi dlo e",
                                  dhi=8, dlo=16, h2=16, e=128)
        v_pexp_s = 0
        v_pexp_x = 0
        for h in range(H):
            eng = "sync" if h % 2 == 0 else "scalar"
            ctr = cS if h % 2 == 0 else cX
            vv = p.op(eng, (lambda s, h=h, pT=pT: s.dma_start(
                out=pT[h],
                in_=P_sb.ap()[h * 8:(h + 1) * 8, :].rearrange(
                    "dhi (dlo e) -> dhi dlo e", dlo=16, e=128))),
                waits=([(cV, v_cmb)] if h < 2 else []), inc=ctr, amt=16)
            if h % 2 == 0:
                v_pexp_s = vv
            else:
                v_pexp_x = vv
        v_pimp = p.op("sync", lambda s: s.dma_start(out=S_run[:], in_=p_tmp.ap()),
                      waits=[(cS, v_pexp_s), (cX, v_pexp_x), (cW, v_exp), (cV, v_scan_v)],
                      inc=cS, amt=16)
        exch_stk.close()
        v_dbgP = 0
        if DEBUG:
            p.op("sync", lambda s: s.dma_start(out=dbg["dbg_ag"], in_=ag.ap()),
                 waits=[(cCC, 1)], inc=cS, amt=16)
            v_dbgP = p.op("sync", lambda s: s.dma_start(out=dbg["dbg_P"], in_=S_run[:]),
                 waits=[(cS, v_pimp)], inc=cS, amt=16)

        t_corr_par = [0, 0]
        v_corr = 0
        for j in range(NCH):
            b = j % 2
            if j > 0:
                p.op("vector", lambda v: v.tensor_tensor(
                    S_run[:], S_run[:], lamct[:], op=ALU.mult), inc=cV)
            v_pbf = p.op("vector", (lambda v, b=b: v.tensor_copy(S_bfp[b][:], S_run[:])),
                         waits=([(cS, max(v_pimp, v_dbgP)), (cT, v_scan_t)] if j == 0 else
                                [(cT, t_corr_par[b])]), inc=cV)
            for g4 in range(4):
                pb_ = 2 + (g4 % 2)
                tmx = 0
                for hh in range(4):
                    hx = g4 * 4 + hh
                    waits = []
                    if hh == 0:
                        waits = [po_free[g4 % 2]]
                        if g4 == 0:
                            waits += [(cV, v_pbf)]
                            if j == 0:
                                waits += [(cV, v_qdall)]
                    tmx = p.op("tensor", (lambda t, hx=hx, j=j, b=b, pb_=pb_, hh=hh: t.matmul(
                        pf[pb_][:, hh * 128:(hh + 1) * 128],
                        S_bfp[b][:, hx * 128:(hx + 1) * 128],
                        qd_all[:, j * H * CH + hx * 128: j * H * CH + (hx + 1) * 128],
                        start=True, stop=True, skip_group_check=True)),
                        waits=waits, inc=(cT if hh == 3 else None))
                v_corr = p.op("vector", (lambda v, g4=g4, j=j, pb_=pb_: v.tensor_tensor(
                    attnT.ap().rearrange("p (h t) -> p h t", h=H, t=TC)[:, g4 * 4:(g4 + 1) * 4, j * 128:(j + 1) * 128],
                    attnT.ap().rearrange("p (h t) -> p h t", h=H, t=TC)[:, g4 * 4:(g4 + 1) * 4, j * 128:(j + 1) * 128],
                    pf[pb_].ap().rearrange("p (h t) -> p h t", h=4, t=128), op=ALU.add)),
                    waits=[(cT, tmx)] + ([(cA, v_scan_a)] if j == 0 and g4 == 0 else []),
                    inc=cV)
                po_free[g4 % 2] = (cV, v_corr)
            t_corr_par[b] = cT.n
        if DEBUG:
            p.op("sync", lambda s: s.dma_start(out=dbg["dbg_attnT"], in_=attnT[:]),
                 waits=[(cV, v_corr)], inc=cS, amt=16)
        corr_stk.close()
        base_stk.close()
        snapC = [(cT, cT.n), (cV, cV.n), (cA, cA.n)]

        tail_stk = contextlib.ExitStack()
        sbT, _ = mk_alloc(tail_stk)
        R = sbT("R", [128, KD * TC], F32)
        w_eT = sbT("w_eT", [8, TC], F32)

        mf_stk = contextlib.ExitStack()
        sbMF, psMF = mk_alloc(mf_stk)
        gbuf = [sbMF("gbuf0", [128, D], BF16), sbMF("gbuf1", [128, D], BF16)]
        ubuf = [sbMF("ubuf0", [128, D], BF16), sbMF("ubuf1", [128, D], BF16)]
        dbuf = [sbMF("dbuf0", [128, 2048], BF16), sbMF("dbuf1", [128, 2048], BF16)]
        sg = [sbMF("sg0", [128, TC], BF16), sbMF("sg1", [128, TC], BF16)]
        hu = [sbMF("hu0", [128, TC], BF16), sbMF("hu1", [128, TC], BF16)]
        Wb = sbMF("Wb", [128, TC], BF16)
        wrow = sbMF("wrow", [1, TC], F32)
        pf.append(psMF("pf6", [128, 512], F32))
        pf.append(psMF("pf7", [128, 512], F32))

        p6_stk = contextlib.ExitStack()
        sbP6, _ = mk_alloc(p6_stk)
        wbufB = [sbP6("wbufB0", [128, D], BF16), sbP6("wbufB1", [128, D], BF16)]
        xrb = [sbP6("xrb0", [128, TC], F32), sbP6("xrb1", [128, TC], F32)]
        xr_ready = {}
        for m in range(2):
            xr_ready[m] = p.op("sync", (lambda s, m=m: s.dma_start(
                out=xrb[m][:], in_=xT[m * 128:(m + 1) * 128, :])),
                waits=(snapC if m == 0 else []), inc=cS, amt=16)

        r_ready = {}

        def evac_proj(i, m, half, bk, waits):
            par = m % 2
            vv = p.op("vector", (lambda v, m=m, half=half, bk=bk, par=par: v.scalar_tensor_tensor(
                R[:, m * TC + half * 512: m * TC + half * 512 + 512],
                pf[bk][:, 0:512], bout[:, m:m + 1],
                xrb[par][:, half * 512: half * 512 + 512],
                op0=ALU.add, op1=ALU.add)),
                waits=waits + [(cS, xr_ready[m])], inc=cV)
            if half == 1:
                r_ready[m] = vv
                if m + 2 < NM:
                    xr_ready[m + 2] = p.op("sync", (lambda s, m=m, par=par: s.dma_start(
                        out=xrb[par][:], in_=xT[(m + 2) * 128:(m + 3) * 128, :])),
                        waits=[(cV, vv)], inc=cS, amt=16)
            return (cV, vv)

        wbufB_free = [snapC[0], snapC[0]]
        emit_proj(wout_s, 0, NM, attnT, wbufB, wbufB_free, evac_proj,
                  [(cV, v_corr)] + snapC, {i: snapC[1] for i in range(4)})
        v_R1 = cV.n
        if DEBUG:
            p.op("sync", lambda s: s.dma_start(out=dbg["dbg_R1"], in_=R[:]),
                 waits=[(cV, v_R1)], inc=cS, amt=16)

        x_ln2 = bigA
        ln2_stk = contextlib.ExitStack()
        sbL2, _ = mk_alloc(ln2_stk)
        xb2 = [sbL2("xb2_0", [128, 512], BF16), sbL2("xb2_1", [128, 512], BF16)]
        sqb2 = [sbL2("sqb2_0", [128, 512], BF16), sbL2("sqb2_1", [128, 512], BF16)]
        rows2 = sbL2("rows2", [1, 4096], F32)
        nsc2 = [sbL2("nsc2_0", [128, 512], F32), sbL2("nsc2_1", [128, 512], F32)]
        ready_k2 = [[(cV, r_ready[k])] for k in range(KD)]
        v_ln2 = emit_ln(R, x_ln2, xb2, sqb2, rows2, nsc2, ready_k2, extra0=snapC)
        if DEBUG:
            p.op("sync", lambda s: s.dma_start(out=dbg["dbg_xln2"], in_=x_ln2[:]),
                 waits=[(cV, v_ln2)], inc=cS, amt=16)

        rt_stk = contextlib.ExitStack()
        sbR, _ = mk_alloc(rt_stk)
        lgT = sbR("lgT", [8, TC], F32)
        lg_tm = sbR("lg_tm", [128, 64], F32)
        probs = sbR("probs", [128, 64], F32)
        g1 = sbR("g1", [128, 64], F32)
        g2 = sbR("g2", [128, 64], F32)
        se = sbR("se", [128, 8], F32)
        ke = sbR("ke", [128, 8], F32)
        w_all = sbR("w_all", [128, 64], F32)

        r3 = lambda apx: apx.rearrange("p (tt e) -> p tt e", tt=8, e=8)
        o3 = lambda apx: apx.rearrange("p (tt o) -> p tt o", tt=8, o=1)
        for half in range(2):
            for k in range(KD):
                waits = []
                if k == 0:
                    waits = [(cV, v_ln2), (cW, v_rtr)] if half == 0 else []

                def mm(t, half=half, k=k):
                    return t.matmul(pf[half][0:8, 0:512],
                                    router[:, k * 8:(k + 1) * 8],
                                    x_ln2[:, k * TC + half * 512: k * TC + half * 512 + 512],
                                    start=(k == 0), stop=(k == KD - 1))
                tm = p.op("tensor", mm, waits=waits, inc=(cT if k == KD - 1 else None))
            p.op("vector", (lambda v, half=half: v.tensor_copy(
                lgT[0:8, half * 512:(half + 1) * 512], pf[half][0:8, 0:512])),
                waits=[(cT, tm)], inc=cV)
        v_lgT = cV.n
        for tt in range(8):
            tm = p.op("tensor", (lambda t, tt=tt: t.transpose(
                pf[4][:, tt * 8:(tt + 1) * 8], lgT[0:8, tt * 128:(tt + 1) * 128],
                id_f[0:8, 0:8])), waits=([(cV, v_lgT)] if tt == 0 else []),
                inc=(cT if tt == 7 else None))
        v_lgtm = p.op("vector", lambda v: v.tensor_copy(lg_tm[:], pf[4][:, 0:64]),
                      waits=[(cT, tm)], inc=cV)
        vrm = p.op("vector", lambda v: v.tensor_reduce(
            o3(se.ap()), r3(lg_tm.ap()), axis=AX.X, op=ALU.max), inc=cV)
        p.op("vector", lambda v: v.tensor_tensor(
            r3(g1.ap()), r3(lg_tm.ap()), o3(se.ap()).broadcast_to((128, 8, 8)),
            op=ALU.subtract), waits=[(cV, vrm)], inc=cV)
        v_exp8 = p.op("scalar", lambda a: a.activation(probs[:], g1[:], AF.Exp),
                      waits=[(cV, cV.n)], inc=cA)
        vred = p.op("vector", lambda v: v.tensor_reduce(
            o3(se.ap()), r3(probs.ap()), axis=AX.X, op=ALU.add),
            waits=[(cA, v_exp8)], inc=cV)
        vrec = p.op("vector", lambda v: v.reciprocal(ke[:], se[:]),
                    waits=[(cV, vred)], inc=cV)
        v_pr = p.op("vector", lambda v: v.tensor_tensor(
            r3(probs.ap()), r3(probs.ap()), o3(ke.ap()).broadcast_to((128, 8, 8)),
            op=ALU.mult), waits=[(cV, vrec)], inc=cV)
        if DEBUG:
            p.op("sync", lambda s: s.dma_start(out=dbg["dbg_probs"], in_=probs[:]),
                 waits=[(cV, v_pr)], inc=cS, amt=16)
        v_wcol = 0
        for e in range(NE):
            colb = r3(probs.ap())[:, :, e:e + 1].broadcast_to((128, 8, 8))
            p.op("vector", (lambda v, colb=colb: v.tensor_tensor(
                r3(g1.ap()), r3(probs.ap()), colb, op=ALU.is_gt)), inc=cV)
            p.op("vector", (lambda v, colb=colb: v.tensor_tensor(
                r3(g2.ap()), r3(probs.ap()), colb, op=ALU.is_equal)), inc=cV)
            p.op("vector", (lambda v, e=e: v.tensor_tensor(
                r3(g2.ap()), r3(g2.ap()),
                r3(tlt.ap()[:, e * 64:(e + 1) * 64]), op=ALU.mult)),
                waits=([(cS, v_pre)] if e == 0 else []), inc=cV)
            p.op("vector", lambda v: v.tensor_tensor(
                r3(g1.ap()), r3(g1.ap()), r3(g2.ap()), op=ALU.add), inc=cV)
            p.op("vector", lambda v: v.tensor_tensor(
                r3(g1.ap()), r3(g1.ap()), r3(probs.ap()), op=ALU.mult), inc=cV)
            vr2 = p.op("vector", lambda v: v.tensor_reduce(
                o3(se.ap()), r3(g1.ap()), axis=AX.X, op=ALU.add), inc=cV)
            vk2 = p.op("vector", lambda v: v.tensor_scalar(
                ke[:], se[:], TOP_P, None, op0=ALU.is_le), waits=[(cV, vr2)], inc=cV)
            v_wcol = p.op("vector", (lambda v, e=e: v.tensor_tensor(
                r3(w_all.ap())[:, :, e:e + 1], o3(ke.ap()),
                r3(probs.ap())[:, :, e:e + 1], op=ALU.mult)), waits=[(cV, vk2)], inc=cV)
        if DEBUG:
            p.op("sync", lambda s: s.dma_start(out=dbg["dbg_w"], in_=w_all[:]),
                 waits=[(cV, v_wcol)], inc=cS, amt=16)
        v_weT = 0
        for rnd in range(2):
            tm = 0
            for i4 in range(4):
                tt = rnd * 4 + i4
                waits = []
                if i4 == 0:
                    waits = [(cV, v_wcol)] if rnd == 0 else [(cV, v_weT)]
                tm = p.op("tensor", (lambda t, tt=tt, i4=i4: t.transpose(
                    pf[5][0:8, i4 * 128:(i4 + 1) * 128], w_all[:, tt * 8:(tt + 1) * 8],
                    id_f[:, :])), waits=waits, inc=(cT if i4 == 3 else None))
            v_weT = p.op("vector", (lambda v, rnd=rnd: v.tensor_copy(
                w_eT[0:8, rnd * 512:(rnd + 1) * 512], pf[5][0:8, 0:512])),
                waits=[(cT, tm)], inc=cV)
        v_lgT_v = v_lgT
        rt_stk.close()
        ln2_stk.close()
        p6_t, p6_v, p6_a = cT.n, cV.n, cA.n
        p6_stk.close()
        snapR = [(cT, p6_t), (cV, p6_v), (cA, p6_a)]

        moe_stk = contextlib.ExitStack()
        sbM, _ = mk_alloc(moe_stk)
        hT = sbM("hT", [128, I_T * TC], BF16)

        free_g = [snapC[2], snapC[2]]
        free_u = [snapC[1], snapC[1]]
        free_d = [snapC[0], snapC[0]]
        bank_free = {0: (cV, v_lgT_v), 1: (cV, v_lgT_v), 2: (cV, v_ln2), 3: (cV, v_ln2),
                     4: (cV, v_lgtm), 5: (cV, v_weT), 6: snapC[1], 7: snapC[1]}
        wb_free = (cV, 0)
        sg_free = [snapC[1], snapC[1]]
        hu_free = [snapC[1], snapC[1]]
        hu_guard = [[snapC[1], snapC[1]], [snapC[1], snapC[1]]]
        wrow_free = (cT, 0)
        v2 = 0

        for e in range(NE):
            v_wr = p.op("sync", (lambda s, e=e: s.dma_start(
                out=wrow[:], in_=w_eT[e:e + 1, :])),
                waits=[(cV, v_weT), wrow_free] + (snapC if e == 0 else []), inc=cS, amt=16)
            wb_vpost = [0, 0]
            for hf in range(2):
                bk = hf
                tpost = p.op("tensor", (lambda t, hf=hf, bk=bk: t.matmul(
                    pf[bk][:, 0:512], ones_row[0:1, :],
                    wrow[0:1, hf * 512:(hf + 1) * 512], start=True, stop=True)),
                    waits=[(cS, v_wr), bank_free[bk]], inc=cT)
                wrow_free = (cT, tpost)
                vpost = p.op("vector", (lambda v, hf=hf, bk=bk: v.tensor_copy(
                    Wb[:, hf * 512:(hf + 1) * 512], pf[bk][:, 0:512])),
                    waits=[(cT, tpost), wb_free], inc=cV)
                bank_free[bk] = (cV, vpost)
                wb_vpost[hf] = vpost
            for it in range(I_T):
                par = it % 2
                p.op("gpsimd", (lambda g, e=e, it=it, par=par: g.dma_start(
                    out=gbuf[par][:], in_=gate_s[e, it])),
                    waits=[free_g[par]] + (snapC if (e == 0 and it < 2) else []),
                    inc=cW, amt=16)
                wu = p.op("gpsimd", (lambda g, e=e, it=it, par=par: g.dma_start(
                    out=ubuf[par][:], in_=up_s[e, it])), waits=[free_u[par]], inc=cW, amt=16)
                tg = [0, 0]
                tu = [0, 0]
                for hf in range(2):
                    bkg = par * 4 + hf
                    bku = par * 4 + 2 + hf
                    for k in range(KD):
                        waits = []
                        if k == 0:
                            waits = [hu_guard[par][hf]]
                            if hf == 0:
                                waits.append((cW, wu))
                            if it == 0:
                                waits.append((cV, wb_vpost[hf]))
                            if it < 2:
                                waits.append(bank_free[bkg])
                                waits.append(bank_free[bku])
                            if e == 0 and it < 2:
                                waits.append((cV, v_ln2))

                        def g_mm(t, par=par, hf=hf, k=k, bkg=bkg):
                            return t.matmul(pf[bkg][:, 0:512],
                                            gbuf[par][:, k * 128:(k + 1) * 128],
                                            x_ln2[:, k * TC + hf * 512: k * TC + hf * 512 + 512],
                                            start=(k == 0), stop=(k == KD - 1))
                        tg[hf] = p.op("tensor", g_mm, waits=waits,
                                      inc=(cT if k == KD - 1 else None))
                    for k in range(KD):

                        def u_mm(t, par=par, hf=hf, k=k, bku=bku):
                            return t.matmul(pf[bku][:, 0:512],
                                            ubuf[par][:, k * 128:(k + 1) * 128],
                                            x_ln2[:, k * TC + hf * 512: k * TC + hf * 512 + 512],
                                            start=(k == 0), stop=(k == KD - 1))
                        tu[hf] = p.op("tensor", u_mm, waits=[],
                                      inc=(cT if k == KD - 1 else None))
                for hf in range(2):
                    bkg = par * 4 + hf
                    bku = par * 4 + 2 + hf
                    apost = p.op("scalar", (lambda a, par=par, hf=hf, bkg=bkg: a.activation(
                        sg[par][:, hf * 512:(hf + 1) * 512], pf[bkg][:, 0:512], AF.Silu)),
                        waits=[(cT, tg[hf]), sg_free[par]], inc=cA)
                    free_g[par] = (cA, apost)
                    bank_free[bkg] = (cA, apost)
                    v1 = p.op("vector", (lambda v, par=par, hf=hf, bku=bku: v.tensor_tensor(
                        hu[par][:, hf * 512:(hf + 1) * 512],
                        sg[par][:, hf * 512:(hf + 1) * 512],
                        pf[bku][:, 0:512], op=ALU.mult)),
                        waits=[(cA, apost), (cT, tu[hf]), hu_free[par]], inc=cV)
                    free_u[par] = (cV, v1)
                    bank_free[bku] = (cV, v1)
                    hu_guard[par][hf] = (cV, v1)
                    v2 = p.op("vector", (lambda v, it=it, par=par, hf=hf: v.tensor_tensor(
                        hT[:, it * TC + hf * 512: it * TC + hf * 512 + 512],
                        hu[par][:, hf * 512:(hf + 1) * 512],
                        Wb[:, hf * 512:(hf + 1) * 512], op=ALU.mult)),
                        waits=(([(cV, wb_vpost[1])] + (snapR if e == 0 else []))
                               if (it == 0 and hf == 0) else []), inc=cV)
                    hu_free[par] = (cV, v2)
                    if hf == 1:
                        sg_free[par] = (cV, v2)
            hT_done = v2
            wb_free = (cV, hT_done)
            for m in range(NM):
                bks = ((m % 2) * 2, (m % 2) * 2 + 1)
                td = [0, 0]
                for kg in range(2):
                    par = (2 * m + kg) % 2
                    wd = p.op("gpsimd", (lambda g, e=e, m=m, kg=kg, par=par: g.dma_start(
                        out=dbuf[par][:], in_=down_s[e, m][:, kg * 2048:(kg + 1) * 2048])),
                        waits=[free_d[par]], inc=cW, amt=16)
                    first_waits = [(cW, wd)]
                    if kg == 0:
                        first_waits += [bank_free[bks[1]]]
                        if m < 2:
                            first_waits += [(cV, hT_done), bank_free[bks[0]]]
                    for kk in range(16):
                        k = kg * 16 + kk
                        for hf in range(2):
                            waits = first_waits if (kk == 0 and hf == 0) else []

                            def d_mm(t, par=par, k=k, kk=kk, hf=hf, bk=bks[hf]):
                                return t.matmul(pf[bk][:, 0:512],
                                                dbuf[par][:, kk * 128:(kk + 1) * 128],
                                                hT[:, k * TC + hf * 512: k * TC + hf * 512 + 512],
                                                start=(k == 0), stop=(k == I_T - 1))
                            td[hf] = p.op("tensor", d_mm, waits=waits,
                                          inc=(cT if k == I_T - 1 else None))
                    if kg == 0:
                        tmk = p.op("tensor", (lambda t: t.nop()), inc=cT)
                        free_d[par] = (cT, tmk)
                    else:
                        free_d[par] = (cT, td[1])
                for hf in range(2):
                    vr = p.op("vector", (lambda v, m=m, hf=hf, bk=bks[hf]: v.tensor_tensor(
                        R[:, m * TC + hf * 512: m * TC + hf * 512 + 512],
                        R[:, m * TC + hf * 512: m * TC + hf * 512 + 512],
                        pf[bk][:, 0:512], op=ALU.add)), waits=[(cT, td[hf])], inc=cV)
                    bank_free[bks[hf]] = (cV, vr)

        p.op("sync", lambda s: s.dma_start(
            out=outT.rearrange("(k p) t -> p k t", k=KD, p=128), in_=R[:]),
            waits=[(cV, cV.n)], inc=cS, amt=16)
        p.op("sync", None, waits=[(cS, cS.n)])

        moe_stk.close()
        mf_stk.close()
        tail_stk.close()
        bigA_stk.close()
        block = L0.enter_context(nc.Block())
        p.emit(block)
    return nc


def _pack(inputs):
    x = np.asarray(inputs["L0_attn_x"], dtype=np.float32)
    w_qkv = np.asarray(inputs["L0_attn_w_qkv"], dtype=np.float32)
    b_qkv = np.asarray(inputs["L0_attn_b_qkv"], dtype=np.float32)
    w_out = np.asarray(inputs["L0_attn_w_out"], dtype=np.float32)
    b_out = np.asarray(inputs["L0_attn_b_out"], dtype=np.float32)
    router_w = np.asarray(inputs["L0_ffn_router_w"], dtype=np.float32)
    gate_w = np.asarray(inputs["L0_ffn_gate_w"], dtype=np.float32)
    up_w = np.asarray(inputs["L0_ffn_up_w"], dtype=np.float32)
    down_w = np.asarray(inputs["L0_ffn_down_w"], dtype=np.float32)

    xT = np.ascontiguousarray(x.reshape(T, D).T)
    qkv_sv = np.ascontiguousarray(
        w_qkv.reshape(QKV_M, 128, KD, 128).transpose(0, 3, 2, 1)).reshape(QKV_M, 128, D)
    wout_sv = np.ascontiguousarray(
        w_out.reshape(NM, 128, KD, 128).transpose(0, 3, 2, 1)).reshape(NM, 128, D)
    bqkv = np.ascontiguousarray(b_qkv.reshape(QKV_M, 128).T)
    boutp = np.ascontiguousarray(b_out.reshape(NM, 128).T)
    router_p = np.ascontiguousarray(
        router_w.T.reshape(KD, 128, NE).transpose(1, 0, 2)).reshape(128, 128)
    gate_sv = np.ascontiguousarray(
        gate_w.reshape(NE, I_T, 128, KD, 128).transpose(0, 1, 4, 3, 2)).reshape(NE, I_T, 128, D)
    up_sv = np.ascontiguousarray(
        up_w.reshape(NE, I_T, 128, KD, 128).transpose(0, 1, 4, 3, 2)).reshape(NE, I_T, 128, D)
    down_sv = np.ascontiguousarray(
        down_w.reshape(NE, NM, 128, I_T, 128).transpose(0, 1, 4, 3, 2)).reshape(NE, NM, 128, I_T * 128)

    lam, decayq, dmaskT, decayk, lamc = _host_consts()
    shared = {
        "qkv_s": qkv_sv, "bqkv": bqkv, "wout_s": wout_sv, "bout": boutp,
        "router_p": router_p, "gate_s": gate_sv, "up_s": up_sv, "down_s": down_sv,
        "dmaskT": dmaskT, "decayq": decayq, "decayk": decayk, "lamc": lamc,
        "tlt": _tlt(), "ident": np.eye(128, dtype=np.float32),
    }
    in_maps = []
    for c in range(NCORES):
        m = dict(shared)
        m["xT"] = np.ascontiguousarray(xT[:, c * TC:(c + 1) * TC])
        m["coef"] = _host_coef(c, lam)
        in_maps.append(m)
    return in_maps


def kernel(**inputs):
    in_maps = _pack(inputs)
    nc = build()
    res = run_bass_kernel_spmd(nc, in_maps, core_ids=list(range(NCORES)))
    if DEBUG:
        _LAST["res"] = res
        _LAST["in_maps"] = in_maps
    parts = [np.asarray(res.results[c]["outT"], dtype=np.float32) for c in range(NCORES)]
    full = np.concatenate(parts, axis=1)
    return np.ascontiguousarray(full.T).reshape(1, T, D)


# version 06
# speedup vs baseline: 1.0223x; 1.0047x over previous
import contextlib

import numpy as np

import concourse.bass as bass
import concourse.mybir as mybir
from concourse.bass_utils import run_bass_kernel_spmd

F32 = mybir.dt.float32
BF16 = mybir.dt.bfloat16
AF = mybir.ActivationFunctionType
ALU = mybir.AluOpType
AX = mybir.AxisListType

NCORES = 8
D = 2048
T = 8192
TC = T // NCORES
KD = 16
H = 16
CH = 128
NCH = TC // CH
NE = 8
I_T = 32
NM = 16
QKV_M = 48
TOP_P = 0.9
EPS = 1e-5

DEBUG = False
_LAST = {}


class Ctr:
    __slots__ = ("h", "n")

    def __init__(self, h):
        self.h = h
        self.n = 0


class Prog:
    ENGINES = ("sync", "gpsimd", "tensor", "vector", "scalar")

    def __init__(self):
        self.ops = {e: [] for e in self.ENGINES}

    def op(self, eng, fn, waits=(), inc=None, amt=1):
        post = None
        if inc is not None:
            inc.n += amt
            post = inc.n
        ws = tuple((c, v) for (c, v) in waits if v > 0)
        self.ops[eng].append((fn, ws, inc, amt if inc is not None else 0))
        return post

    def emit(self, block):
        for eng in self.ENGINES:
            def mk(e):
                def body(h):
                    for fn, waits, inc, amt in self.ops[e]:
                        for (c, v) in waits:
                            h.wait_ge(c.h, v)
                        if fn is not None:
                            inst = fn(h)
                            if inc is not None:
                                inst.then_inc(inc.h, amt)
                return body
            getattr(block, eng)(mk(eng))


def _host_consts():
    hh = np.arange(H, dtype=np.float64)
    slopes = (2.0 ** -0.5) ** (hh + 1.0)
    lam = np.exp(-slopes)
    pos = np.arange(CH, dtype=np.float64)
    dq = lam[:, None] ** (pos[None, :] + 1.0)
    decayq = np.broadcast_to(dq.reshape(1, H * CH), (128, H * CH)).astype(np.float32)
    diff = pos[None, None, :] - pos[None, :, None]
    dm = np.where(diff >= 0, lam[:, None, None] ** np.maximum(diff, 0.0), 0.0)
    dmaskT = np.ascontiguousarray(dm.transpose(1, 0, 2).reshape(CH, H * CH)).astype(np.float32)
    dk = lam[:, None] ** (CH - 1.0 - pos[None, :])
    decayk = np.ascontiguousarray(
        np.broadcast_to(dk.T[:, :, None], (CH, H, CH)).reshape(CH, H * CH)).astype(np.float32)
    lc = lam ** CH
    lamc = np.ascontiguousarray(
        np.broadcast_to(lc[None, :, None], (CH, H, CH)).reshape(CH, H * CH)).astype(np.float32)
    return lam, decayq, dmaskT, decayk, lamc


def _host_coef(core, lam):
    out = np.zeros((128, NCORES), dtype=np.float64)
    for pp in range(128):
        h = pp // 8
        for r in range(core):
            out[pp, r] = lam[h] ** (TC * (core - 1 - r))
    return out.astype(np.float32)


def _tlt():
    out = np.zeros((128, NE * 64), dtype=np.float32)
    for e in range(NE):
        for tt in range(8):
            out[:, e * 64 + tt * 8: e * 64 + tt * 8 + e] = 1.0
    return out


def build():
    nc = bass.Bass("TRN2", target_bir_lowering=False, debug=False)

    xT = nc.dram_tensor("xT", [D, TC], F32, kind="ExternalInput").ap()
    qkv_s = nc.dram_tensor("qkv_s", [QKV_M, 128, D], F32, kind="ExternalInput").ap()
    bqkv_d = nc.dram_tensor("bqkv", [128, QKV_M], F32, kind="ExternalInput").ap()
    wout_s = nc.dram_tensor("wout_s", [NM, 128, D], F32, kind="ExternalInput").ap()
    bout_d = nc.dram_tensor("bout", [128, NM], F32, kind="ExternalInput").ap()
    router_d = nc.dram_tensor("router_p", [128, 128], F32, kind="ExternalInput").ap()
    gate_s = nc.dram_tensor("gate_s", [NE, I_T, 128, D], F32, kind="ExternalInput").ap()
    up_s = nc.dram_tensor("up_s", [NE, I_T, 128, D], F32, kind="ExternalInput").ap()
    down_s = nc.dram_tensor("down_s", [NE, NM, 128, I_T * 128], F32, kind="ExternalInput").ap()
    dmaskT_d = nc.dram_tensor("dmaskT", [128, H * CH], F32, kind="ExternalInput").ap()
    decayq_d = nc.dram_tensor("decayq", [128, H * CH], F32, kind="ExternalInput").ap()
    decayk_d = nc.dram_tensor("decayk", [128, H * CH], F32, kind="ExternalInput").ap()
    lamc_d = nc.dram_tensor("lamc", [128, H * CH], F32, kind="ExternalInput").ap()
    tlt_d = nc.dram_tensor("tlt", [128, NE * 64], F32, kind="ExternalInput").ap()
    coef_d = nc.dram_tensor("coef", [128, NCORES], F32, kind="ExternalInput").ap()
    ident_d = nc.dram_tensor("ident", [128, 128], F32, kind="ExternalInput").ap()
    outT = nc.dram_tensor("outT", [D, TC], F32, kind="ExternalOutput").ap()
    dbg = {}
    if DEBUG:
        for nm_, shape, dt in [
            ("dbg_xln1", [128, KD * TC], BF16), ("dbg_A", [128, H * CH], F32),
            ("dbg_P", [128, H * CH], F32), ("dbg_attnT", [128, H * TC], BF16),
            ("dbg_R1", [128, KD * TC], F32), ("dbg_probs", [128, 64], F32),
            ("dbg_w", [128, 64], F32), ("dbg_kT", [128, KD * TC], BF16),
            ("dbg_vtm", [128, NCH * D], BF16), ("dbg_xln2", [128, KD * TC], BF16),
            ("dbg_ag", [NCORES * 128, H * CH], BF16),
        ]:
            dbg[nm_] = nc.dram_tensor(nm_, shape, dt, kind="ExternalOutput").ap()

    a_loc = nc.dram_tensor("a_loc", [128, H * CH], BF16, kind="Internal")
    ag = nc.dram_tensor("ag", [NCORES * 128, H * CH], BF16, kind="Internal",
                        addr_space="Shared")
    p_tmp = nc.dram_tensor("p_tmp", [128, H * CH], F32, kind="Internal")

    p = Prog()
    L0 = contextlib.ExitStack()

    def mk_alloc(stk):
        return (lambda name, shape, dt: stk.enter_context(nc.sbuf_tensor(name, shape, dt)),
                lambda name, shape, dt: stk.enter_context(nc.psum_tensor(name, shape, dt)))

    with L0:
        sb0, ps0 = mk_alloc(L0)
        sem = lambda name: Ctr(L0.enter_context(nc.semaphore(name)))
        cS = sem("cS")
        cW = sem("cW")
        cT = sem("cT")
        cV = sem("cV")
        cA = sem("cA")
        cCC = sem("cCC")
        cX = sem("cX")

        ones_b = sb0("ones_b", [128, 1], BF16)
        ones_row = sb0("ones_row", [1, 128], F32)
        id_f = sb0("id_f", [128, 128], F32)
        id_b = sb0("id_b", [128, 128], BF16)
        bqkv = sb0("bqkv_sb", [128, QKV_M], F32)
        bout = sb0("bout_sb", [128, NM], F32)
        router = sb0("router_sb", [128, 128], BF16)
        tlt = sb0("tlt_sb", [128, NE * 64], F32)
        coef = sb0("coef_sb", [128, NCORES], F32)
        pf = [ps0("pf%d" % i, [128, 512], F32) for i in range(6)]

        v_pre = 0
        for fn in [
            lambda s: s.dma_start(out=bqkv[:], in_=bqkv_d),
            lambda s: s.dma_start(out=bout[:], in_=bout_d),
            lambda s: s.dma_start(out=tlt[:], in_=tlt_d),
            lambda s: s.dma_start(out=coef[:], in_=coef_d),
            lambda s: s.dma_start(out=id_f[:], in_=ident_d),
        ]:
            v_pre = p.op("sync", fn, inc=cS, amt=16)
        v_idb = p.op("gpsimd", lambda g: g.dma_start(out=id_b[:], in_=ident_d), inc=cW, amt=16)
        v_rtr = p.op("gpsimd", lambda g: g.dma_start(out=router[:], in_=router_d), inc=cW, amt=16)
        p.op("vector", lambda v: v.memset(ones_b[:], 1.0), inc=cV)
        p.op("vector", lambda v: v.memset(ones_row[:], 1.0), inc=cV)
        v_ones = cV.n

        def emit_ln(src, dst, xb, sqb, rows, nsc, ready_k, extra0=()):
            xb_free = [(cT, 0), (cT, 0)]
            sq_free = [(cT, 0), (cT, 0)]
            for half in range(2):
                for k in range(KD):
                    par = k % 2
                    vw = [xb_free[par]]
                    aw = [sq_free[par]]
                    if half == 0:
                        vw = vw + list(ready_k[k])
                        aw = aw + list(ready_k[k])
                        if k == 0:
                            vw = vw + [(cV, v_ones)] + list(extra0)
                            aw = aw + list(extra0)
                    vc = p.op("vector", (lambda v, k=k, half=half, par=par: v.tensor_copy(
                        xb[par][:, 0:512],
                        src[:, k * TC + half * 512: k * TC + half * 512 + 512])),
                        waits=vw, inc=cV)
                    va = p.op("scalar", (lambda a, k=k, half=half, par=par: a.activation(
                        sqb[par][:, 0:512],
                        src[:, k * TC + half * 512: k * TC + half * 512 + 512],
                        AF.Square)), waits=aw, inc=cA)
                    tm_s = p.op("tensor", (lambda t, k=k, par=par: t.matmul(
                        pf[0][0:1, 0:512], ones_b[:, 0:1], xb[par][:, 0:512],
                        start=(k == 0), stop=(k == KD - 1))), waits=[(cV, vc)], inc=cT)
                    xb_free[par] = (cT, tm_s)
                    tm_q = p.op("tensor", (lambda t, k=k, par=par: t.matmul(
                        pf[1][0:1, 0:512], ones_b[:, 0:1], sqb[par][:, 0:512],
                        start=(k == 0), stop=(k == KD - 1))), waits=[(cA, va)], inc=cT)
                    sq_free[par] = (cT, tm_q)
                ve = p.op("vector", (lambda v, half=half: v.tensor_scalar(
                    rows[0:1, half * 512: half * 512 + 512], pf[0][0:1, 0:512],
                    1.0 / D, None, op0=ALU.mult)), waits=[(cT, tm_s)], inc=cV)
                ve2 = p.op("vector", (lambda v, half=half: v.tensor_scalar(
                    rows[0:1, 1024 + half * 512: 1024 + half * 512 + 512], pf[1][0:1, 0:512],
                    1.0 / D, None, op0=ALU.mult)), waits=[(cT, tm_q)], inc=cV)
                if half == 0:
                    p.op("tensor", None, waits=[(cV, ve2)])
            p.op("vector", lambda v: v.tensor_tensor(
                rows[0:1, 2048:3072], rows[0:1, 0:1024], rows[0:1, 0:1024], op=ALU.mult), inc=cV)
            v_var = p.op("vector", lambda v: v.scalar_tensor_tensor(
                rows[0:1, 3072:4096], rows[0:1, 1024:2048], EPS, rows[0:1, 2048:3072],
                op0=ALU.add, op1=ALU.subtract), inc=cV)
            a_sd = p.op("scalar", lambda a: a.activation(
                rows[0:1, 2048:3072], rows[0:1, 3072:4096], AF.Sqrt),
                waits=[(cV, v_var)], inc=cA)
            v_rstd = p.op("vector", lambda v: v.reciprocal(
                rows[0:1, 3072:4096], rows[0:1, 2048:3072]), waits=[(cA, a_sd)], inc=cV)
            for half in range(2):
                p.op("tensor", (lambda t, half=half: t.matmul(
                    pf[2 + half][:, 0:512], ones_row[0:1, :],
                    rows[0:1, half * 512: half * 512 + 512], start=True, stop=True)),
                    waits=[(cV, v_rstd)] if half == 0 else [], inc=cT)
                t_bc = p.op("tensor", (lambda t, half=half: t.matmul(
                    pf[4 + half][:, 0:512], ones_row[0:1, :],
                    rows[0:1, 3072 + half * 512: 3072 + half * 512 + 512],
                    start=True, stop=True)), inc=cT)
            last = 0
            lnk = [0] * KD
            for k in range(KD):
                for half in range(2):
                    par = k % 2
                    p.op("vector", (lambda v, k=k, half=half, par=par: v.tensor_tensor(
                        nsc[par][:, 0:512],
                        src[:, k * TC + half * 512: k * TC + half * 512 + 512],
                        pf[2 + half][:, 0:512], op=ALU.subtract)),
                        waits=([(cT, t_bc)] if (k == 0 and half == 0) else []), inc=cV)
                    last = p.op("vector", (lambda v, k=k, half=half, par=par: v.tensor_tensor(
                        dst[:, k * TC + half * 512: k * TC + half * 512 + 512],
                        nsc[par][:, 0:512],
                        pf[4 + half][:, 0:512], op=ALU.mult)), inc=cV)
                lnk[k] = last
            return last, lnk, ve2

        def emit_proj(wdram, m0, m1, act, wbuf, wbuf_free, evac, act_ready, bank_init):
            bank_free = dict(bank_init)
            for i, m in enumerate(range(m0, m1)):
                par = i % 2
                wload = p.op("gpsimd", (lambda g, m=m, par=par: g.dma_start(
                    out=wbuf[par][:], in_=wdram[m])), waits=[wbuf_free[par]], inc=cW, amt=16)
                tm = [0, 0]
                for half in range(2):
                    bk = par * 2 + half
                    for k in range(KD):
                        waits = []
                        if k == 0:
                            waits = [bank_free[bk]]
                            if half == 0:
                                waits.append((cW, wload))
                            if i < 2:
                                waits = waits + list(act_ready)

                        def mm(t, m=m, par=par, half=half, k=k, bk=bk):
                            return t.matmul(pf[bk][:, 0:512],
                                            wbuf[par][:, k * 128:(k + 1) * 128],
                                            act[:, k * TC + half * 512: k * TC + half * 512 + 512],
                                            start=(k == 0), stop=(k == KD - 1))
                        tmv = p.op("tensor", mm, waits=waits, inc=(cT if k == KD - 1 else None))
                    tm[half] = tmv
                wbuf_free[par] = (cT, tm[1])
                for half in range(2):
                    bk = par * 2 + half
                    bank_free[bk] = evac(i, m, half, bk, [(cT, tm[half])])

        bigA_stk = contextlib.ExitStack()
        sbBA, _ = mk_alloc(bigA_stk)
        bigA = sbBA("bigA", [128, KD * TC], BF16)
        x_ln1 = bigA

        base_stk = contextlib.ExitStack()
        sbB, _ = mk_alloc(base_stk)
        decayq = sbB("decayq_sb", [128, H * CH], F32)
        lamct = sbB("lamc_sb", [128, H * CH], F32)
        S_run = sbB("S_run", [128, H * CH], F32)
        S_bf = sbB("S_bf", [128, H * CH], BF16)
        qd_s = sbB("qd_s", [128, H * CH], BF16)
        s_tmp = sbB("s_tmp", [128, 512], F32)
        qT = sbB("qT", [128, KD * TC], BF16)
        wbufA = [sbB("wbufA0", [128, D], BF16), sbB("wbufA1", [128, D], BF16)]

        v_tb0 = 0
        for dst_t, src_t in [(decayq, decayq_d), (lamct, lamc_d)]:
            v_tb0 = p.op("sync", (lambda s, dst_t=dst_t, src_t=src_t: s.dma_start(
                out=dst_t[:], in_=src_t)), inc=cS, amt=16)

        ln1_stk = contextlib.ExitStack()
        sbL, _ = mk_alloc(ln1_stk)
        x_f32 = sbL("x_f32", [128, KD * TC], F32)
        xb1 = [sbL("xb1_0", [128, 512], BF16), sbL("xb1_1", [128, 512], BF16)]
        sqb = [sbL("sqb0", [128, 512], BF16), sbL("sqb1", [128, 512], BF16)]
        rows = sbL("rows", [1, 4096], F32)
        nsc = [sbL("nsc0", [128, 512], F32), sbL("nsc1", [128, 512], F32)]
        xfv = x_f32.ap().rearrange("p (k t) -> p k t", k=KD, t=TC)
        xsv = xT.rearrange("(k p) t -> p k t", k=KD, p=128)
        v_xl_s = p.op("sync", lambda s: s.dma_start(
            out=xfv[:, 0:KD // 2], in_=xsv[:, 0:KD // 2]), inc=cS, amt=16)
        v_xl_x = p.op("scalar", lambda s: s.dma_start(
            out=xfv[:, KD // 2:KD], in_=xsv[:, KD // 2:KD]), inc=cX, amt=16)
        ready_k1 = [([(cS, v_xl_s)] if k == 0 else
                     ([(cX, v_xl_x)] if k == KD // 2 else [])) for k in range(KD)]
        v_ln1, lnk1, v_sfree1 = emit_ln(x_f32, x_ln1, xb1, sqb, rows, nsc, ready_k1)
        if DEBUG:
            p.op("sync", lambda s: s.dma_start(out=dbg["dbg_xln1"], in_=x_ln1[:]),
                 waits=[(cV, v_ln1)], inc=cS, amt=16)
        ln1_t, ln1_v, ln1_a = cT.n, cV.n, cA.n
        ln1_stk.close()
        snap1 = [(cT, ln1_t), (cV, ln1_v), (cA, ln1_a)]

        scan_stk = contextlib.ExitStack()
        sbA, psA = mk_alloc(scan_stk)
        dmaskT = sbA("dmaskT_sb", [128, H * CH], F32)
        decayk = sbA("decayk_sb", [128, H * CH], F32)
        kT = sbA("kT", [128, KD * TC], BF16)
        v_tm = sbA("v_tm", [128, NCH * D], BF16)
        kd_tm = sbA("kd_tm", [128, H * CH], BF16)
        sT_s = sbA("sT_s", [128, H * CH], BF16)
        vtmp = [sbA("vtmp0", [128, TC], BF16), sbA("vtmp1", [128, TC], BF16)]
        ptb = [psA("ptb0", [128, 1024], BF16), psA("ptb1", [128, 1024], BF16)]

        v_tabs = 0
        for i, (dst_t, src_t) in enumerate([(dmaskT, dmaskT_d), (decayk, decayk_d)]):
            v_tabs = p.op("sync", (lambda s, dst_t=dst_t, src_t=src_t: s.dma_start(
                out=dst_t[:], in_=src_t)), waits=(snap1 if i == 0 else []), inc=cS, amt=16)

        wbuf_free = [(cT, 0), (cT, 0)]
        bank_free_p = {0: (cV, v_sfree1), 1: (cV, v_sfree1), 2: snap1[1], 3: snap1[1]}
        tb_free = [snap1[1], snap1[1]]
        vtmp_free = [(cT, 0), (cT, 0)]
        vtm_last = 0
        for h in range(H):
            m = 32 + h
            par = h % 2
            wload = p.op("gpsimd", (lambda g, m=m, par=par: g.dma_start(
                out=wbufA[par][:], in_=qkv_s[m])), waits=[wbuf_free[par]], inc=cW, amt=16)
            tmh = [0, 0]
            for half in range(2):
                bk = par * 2 + half
                for k in range(KD):
                    waits = []
                    if k == 0:
                        waits = [bank_free_p[bk]]
                        if half == 0:
                            waits.append((cW, wload))
                    if h < 2 and half == 0:
                        waits.append((cV, lnk1[k]))

                    def mm(t, m=m, par=par, half=half, k=k, bk=bk):
                        return t.matmul(pf[bk][:, 0:512],
                                        wbufA[par][:, k * 128:(k + 1) * 128],
                                        x_ln1[:, k * TC + half * 512: k * TC + half * 512 + 512],
                                        start=(k == 0), stop=(k == KD - 1))
                    tmv = p.op("tensor", mm, waits=waits, inc=(cT if k == KD - 1 else None))
                tmh[half] = tmv
            wbuf_free[par] = (cT, tmh[1])
            va = 0
            for half in range(2):
                bk = par * 2 + half
                va = p.op("scalar", (lambda a, m=m, half=half, bk=bk, par=par: a.activation(
                    vtmp[par][:, half * 512:(half + 1) * 512],
                    pf[bk][:, 0:512], AF.Identity, bias=bqkv[:, m:m + 1], scale=1.0)),
                    waits=[(cT, tmh[half])] + ([vtmp_free[par], (cS, v_pre)] if half == 0 else []),
                    inc=cA)
                bank_free_p[bk] = (cA, va)
            tmt = 0
            for j in range(NCH):
                w0 = []
                if j == 0:
                    w0 = [(cA, va), tb_free[par]]
                    if h == 0:
                        w0.append((cW, v_idb))
                tmt = p.op("tensor", (lambda t, j=j, par=par: t.transpose(
                    ptb[par][:, j * 128:(j + 1) * 128],
                    vtmp[par][:, j * 128:(j + 1) * 128], id_b[:, :])),
                    waits=w0, inc=(cT if j == NCH - 1 else None))
            vtmp_free[par] = (cT, tmt)
            vtm_last = p.op("vector", (lambda v, h=h, par=par: v.tensor_copy(
                v_tm.ap().rearrange("p (j he) -> p j he", j=NCH, he=D)[:, :, h * 128:(h + 1) * 128],
                ptb[par].ap().rearrange("p (j e) -> p j e", j=NCH, e=128))),
                waits=[(cT, tmt)], inc=cV)
            tb_free[par] = (cV, vtm_last)

        def evac_kv(i, m, half, bk, waits):
            mrel = m - 16
            va = p.op("scalar", (lambda a, m=m, half=half, bk=bk, mrel=mrel: a.activation(
                kT[:, mrel * TC + half * 512: mrel * TC + half * 512 + 512],
                pf[bk][:, 0:512], AF.Identity, bias=bqkv[:, m:m + 1], scale=1.0)),
                waits=waits, inc=cA)
            return (cA, va)

        def evac_q(i, m, half, bk, waits):
            va = p.op("scalar", (lambda a, m=m, half=half, bk=bk: a.activation(
                qT[:, m * TC + half * 512: m * TC + half * 512 + 512],
                pf[bk][:, 0:512], AF.Identity, bias=bqkv[:, m:m + 1], scale=1.0)),
                waits=waits, inc=cA)
            return (cA, va)

        emit_proj(qkv_s, 16, 32, x_ln1, wbufA, wbuf_free, evac_kv,
                  [(cV, v_ln1)], dict(bank_free_p))
        emit_proj(qkv_s, 0, 16, x_ln1, wbufA, wbuf_free, evac_q,
                  [(cV, v_ln1)], {i: (cA, cA.n) for i in range(4)})
        v_proj_a = cA.n
        v_proj_t = cT.n

        v_s0 = p.op("vector", lambda v: v.memset(S_run[:], 0.0),
                    waits=[(cA, ln1_a), (cT, ln1_t)], inc=cV)
        sc_free = [(cV, 0), (cV, 0)]
        po_free = [(cA, v_proj_a), (cA, v_proj_a)]
        kv_free = [(cV, v_ln1), (cV, v_ln1)]

        def emit_kd(jj, first_waits):
            outs = []
            for g in range(2):
                tmx = 0
                for hh in range(8):
                    hx = g * 8 + hh
                    waits = []
                    if hh == 0:
                        waits = [tb_free[g]]
                        if g == 0:
                            waits = waits + first_waits
                    tmx = p.op("tensor", (lambda t, hx=hx, hh=hh, g=g, jj=jj: t.transpose(
                        ptb[g][:, hh * 128:(hh + 1) * 128],
                        kT[:, hx * TC + jj * 128: hx * TC + jj * 128 + 128],
                        id_b[:, :])), waits=waits, inc=(cT if hh == 7 else None))
                ve = p.op("vector", (lambda v, g=g: v.tensor_tensor(
                    kd_tm[:, g * 1024:(g + 1) * 1024], ptb[g][:, :],
                    decayk[:, g * 1024:(g + 1) * 1024], op=ALU.mult)),
                    waits=[(cT, tmx)], inc=cV)
                tb_free[g] = (cV, ve)
                outs.append(ve)
            return outs

        def emit_kv_s(jj, kd_ready, vtm_ready):
            for g4 in range(4):
                kb = 4 + (g4 % 2)
                tmx = 0
                for hh in range(4):
                    hx = g4 * 4 + hh
                    waits = []
                    if hh == 0:
                        waits = [kv_free[g4 % 2], (cV, kd_ready[0 if g4 < 2 else 1])]
                        if vtm_ready:
                            waits.append((cV, vtm_ready))
                    tmx = p.op("tensor", (lambda t, hx=hx, jj=jj, kb=kb, hh=hh: t.matmul(
                        pf[kb][:, hh * 128:(hh + 1) * 128],
                        kd_tm[:, hx * 128:(hx + 1) * 128],
                        v_tm[:, jj * D + hx * 128: jj * D + (hx + 1) * 128],
                        start=True, stop=True, skip_group_check=True)),
                        waits=waits, inc=(cT if hh == 3 else None))
                p.op("vector", (lambda v, g4=g4: v.tensor_tensor(
                    s_tmp[:, 0:512], S_run[:, g4 * 512:(g4 + 1) * 512],
                    lamct[:, g4 * 512:(g4 + 1) * 512], op=ALU.mult)), inc=cV)
                v2 = p.op("vector", (lambda v, g4=g4, kb=kb: v.tensor_tensor(
                    S_run[:, g4 * 512:(g4 + 1) * 512], s_tmp[:, 0:512],
                    pf[kb][:, 0:512], op=ALU.add)), waits=[(cT, tmx)], inc=cV)
                kv_free[g4 % 2] = (cV, v2)

        attnT = bigA
        t_prev_j = 0
        for j in range(NCH):
            w_sbf = [(cT, t_prev_j)] if j > 0 else [(cV, v_s0)]
            v_sbf = p.op("vector", (lambda v: v.tensor_copy(S_bf[:], S_run[:])),
                         waits=w_sbf, inc=cV)
            v_qd = p.op("vector", (lambda v, j=j: v.tensor_tensor(
                qd_s.ap().rearrange("p (h t) -> p h t", h=H, t=CH),
                qT.ap().rearrange("p (h t) -> p h t", h=H, t=TC)[:, :, j * 128:(j + 1) * 128],
                decayq.ap().rearrange("p (h t) -> p h t", h=H, t=CH), op=ALU.mult)),
                waits=([(cT, t_prev_j)] if j > 0 else [(cA, v_proj_a), (cS, cS.n)]), inc=cV)
            kde = emit_kd(j, [(cS, cS.n), (cT, v_proj_t)] if j == 0 else [])
            sT_ready = [0, 0, 0, 0]
            for g4 in range(4):
                sb_ = g4 % 2
                tmx = 0
                for hh in range(4):
                    hx = g4 * 4 + hh
                    waits = [sc_free[sb_]] if hh == 0 else []
                    tmx = p.op("tensor", (lambda t, hx=hx, j=j, sb_=sb_, hh=hh: t.matmul(
                        pf[sb_][:, hh * 128:(hh + 1) * 128],
                        kT[:, hx * TC + j * 128: hx * TC + j * 128 + 128],
                        qT[:, hx * TC + j * 128: hx * TC + j * 128 + 128],
                        start=True, stop=True, skip_group_check=True)),
                        waits=waits, inc=(cT if hh == 3 else None))
                vm = p.op("vector", (lambda v, g4=g4, sb_=sb_: v.tensor_tensor(
                    sT_s[:, g4 * 512:(g4 + 1) * 512], pf[sb_][:, 0:512],
                    dmaskT[:, g4 * 512:(g4 + 1) * 512], op=ALU.mult)),
                    waits=[(cT, tmx)], inc=cV)
                sc_free[sb_] = (cV, vm)
                sT_ready[g4] = vm
            for g4 in range(4):
                pb_ = 2 + (g4 % 2)
                tmx = 0
                for hh in range(4):
                    hx = g4 * 4 + hh
                    waits = []
                    if hh == 0:
                        waits = [po_free[g4 % 2], (cV, sT_ready[g4])]
                        if g4 == 0:
                            waits.append((cV, v_qd))
                            waits.append((cV, v_sbf))
                    p.op("tensor", (lambda t, hx=hx, j=j, pb_=pb_, hh=hh: t.matmul(
                        pf[pb_][:, hh * 128:(hh + 1) * 128],
                        v_tm[:, j * D + hx * 128: j * D + (hx + 1) * 128],
                        sT_s[:, hx * 128:(hx + 1) * 128],
                        start=True, stop=False, skip_group_check=True)), waits=waits)
                    tmx = p.op("tensor", (lambda t, hx=hx, pb_=pb_, hh=hh: t.matmul(
                        pf[pb_][:, hh * 128:(hh + 1) * 128],
                        S_bf[:, hx * 128:(hx + 1) * 128],
                        qd_s[:, hx * 128:(hx + 1) * 128],
                        start=False, stop=True, skip_group_check=True)),
                        inc=(cT if hh == 3 else None))
                va = p.op("scalar", (lambda a, g4=g4, j=j, pb_=pb_: a.activation(
                    attnT.ap().rearrange("p (h t) -> p h t", h=H, t=TC)[:, g4 * 4:(g4 + 1) * 4, j * 128:(j + 1) * 128],
                    pf[pb_].ap().rearrange("p (h t) -> p h t", h=4, t=128),
                    AF.Identity)), waits=[(cT, tmx)], inc=cA)
                po_free[g4 % 2] = (cA, va)
            emit_kv_s(j, kde, vtm_last if j == 0 else 0)
            t_prev_j = cT.n
        v_scan_t = cT.n
        v_scan_v = cV.n
        v_scan_a = cA.n

        aL = a_loc.ap().rearrange("(h2 dhi) (dlo e) -> h2 dhi dlo e", h2=16, dhi=8, dlo=16, e=128)
        v_exp = 0
        for h in range(H):
            v_exp = p.op("gpsimd", (lambda g, h=h: g.dma_start(
                out=aL[h], in_=S_run[:, h * 128:(h + 1) * 128])),
                waits=([(cV, v_scan_v)] if h == 0 else []), inc=cW, amt=16)
        if DEBUG:
            p.op("sync", lambda s: s.dma_start(out=dbg["dbg_A"], in_=S_run[:]),
                 waits=[(cV, v_scan_v)], inc=cS, amt=16)
            p.op("sync", lambda s: s.dma_start(out=dbg["dbg_kT"], in_=kT[:]), inc=cS, amt=16)
            p.op("sync", lambda s: s.dma_start(out=dbg["dbg_vtm"], in_=v_tm[:]), inc=cS, amt=16)
        v_cc = p.op("gpsimd", lambda g: g.collective_compute(
            "AllGather", ALU.bypass, replica_groups=[list(range(NCORES))],
            ins=[a_loc.ap()], outs=[ag.ap()]),
            waits=[(cW, v_exp)], inc=cCC)

        scan_stk.close()
        snapS = [(cT, v_scan_t), (cV, v_scan_v), (cA, v_scan_a)]

        TOPS = 229344
        xrb = [nc.alloc_sbuf_tensor_at("xrb0", [128, TC], F32, offset=TOPS - 4096),
               nc.alloc_sbuf_tensor_at("xrb1", [128, TC], F32, offset=TOPS - 8192)]
        wbufB = [nc.alloc_sbuf_tensor_at("wbufB0", [128, D], BF16, offset=TOPS - 12288),
                 nc.alloc_sbuf_tensor_at("wbufB1", [128, D], BF16, offset=TOPS - 16384)]
        xr_ready = {}
        for m in range(2):
            xr_ready[m] = p.op("sync", (lambda s, m=m: s.dma_start(
                out=xrb[m][:], in_=xT[m * 128:(m + 1) * 128, :])),
                waits=(snapS if m == 0 else []), inc=cS, amt=16)

        corr_stk = contextlib.ExitStack()
        sbC, _ = mk_alloc(corr_stk)
        qd_all = sbC("qd_all", [128, NCH * H * CH], BF16)
        S_bfp = [sbC("S_bfp0", [128, H * CH], BF16), sbC("S_bfp1", [128, H * CH], BF16)]
        exch_stk = contextlib.ExitStack()
        sbE, _ = mk_alloc(exch_stk)
        ar_all = sbE("ar_all", [128, NCORES * H * CH], BF16)
        P_sb = sbE("P_sb", [128, H * CH], F32)
        assert nc.sbuf_bytes_remaining >= 16384, nc.sbuf_bytes_remaining

        v_qdall = 0
        for j in range(NCH):
            v_qdall = p.op("vector", (lambda v, j=j: v.tensor_tensor(
                qd_all.ap().rearrange("p (j2 h t) -> p (j2 h) t", j2=NCH, h=H, t=CH)[:, j * H:(j + 1) * H, :],
                qT.ap().rearrange("p (h t) -> p h t", h=H, t=TC)[:, :, j * 128:(j + 1) * 128],
                decayq.ap().rearrange("p (h t) -> p h t", h=H, t=CH), op=ALU.mult)),
                waits=(snapS + [(cA, v_proj_a), (cS, cS.n)] if j == 0 else []), inc=cV)

        agv = ag.ap().rearrange("(r p) c -> p r c", r=NCORES, p=128)
        arv = ar_all.ap().rearrange("p (r c) -> p r c", r=NCORES, c=H * CH)
        v_arl_g = p.op("gpsimd", lambda g: g.dma_start(
            out=arv[:, 0:NCORES // 2], in_=agv[:, 0:NCORES // 2]),
            waits=[(cCC, 1)], inc=cW, amt=16)
        v_arl_s = p.op("sync", lambda s: s.dma_start(
            out=arv[:, NCORES // 2:NCORES], in_=agv[:, NCORES // 2:NCORES]),
            waits=[(cCC, 1)], inc=cS, amt=16)
        v_cmb = 0
        for r in range(NCORES):
            if r == 0:
                v_cmb = p.op("vector", (lambda v, r=r: v.tensor_scalar(
                    P_sb[:], ar_all[:, r * H * CH:(r + 1) * H * CH],
                    coef[:, r:r + 1], None, op0=ALU.mult)),
                    waits=[(cW, v_arl_g), (cS, v_pre)], inc=cV)
            else:
                v_cmb = p.op("vector", (lambda v, r=r: v.scalar_tensor_tensor(
                    P_sb[:], ar_all[:, r * H * CH:(r + 1) * H * CH],
                    coef[:, r:r + 1], P_sb[:], op0=ALU.mult, op1=ALU.add)),
                    waits=([(cS, v_arl_s)] if r == NCORES // 2 else []), inc=cV)
        pT = p_tmp.ap().rearrange("(dhi dlo) (h2 e) -> h2 dhi dlo e",
                                  dhi=8, dlo=16, h2=16, e=128)
        v_pexp_s = 0
        v_pexp_x = 0
        for h in range(H):
            eng = "sync" if h % 2 == 0 else "scalar"
            ctr = cS if h % 2 == 0 else cX
            vv = p.op(eng, (lambda s, h=h, pT=pT: s.dma_start(
                out=pT[h],
                in_=P_sb.ap()[h * 8:(h + 1) * 8, :].rearrange(
                    "dhi (dlo e) -> dhi dlo e", dlo=16, e=128))),
                waits=([(cV, v_cmb)] if h < 2 else []), inc=ctr, amt=16)
            if h % 2 == 0:
                v_pexp_s = vv
            else:
                v_pexp_x = vv
        pimp_w = [(cS, v_pexp_s), (cX, v_pexp_x), (cW, v_exp), (cV, v_scan_v)]
        v_pimp_s = p.op("sync", lambda s: s.dma_start(
            out=S_run[:, 0:H * CH // 2], in_=p_tmp.ap()[:, 0:H * CH // 2]),
            waits=pimp_w, inc=cS, amt=16)
        v_pimp_x = p.op("scalar", lambda s: s.dma_start(
            out=S_run[:, H * CH // 2:H * CH], in_=p_tmp.ap()[:, H * CH // 2:H * CH]),
            waits=pimp_w, inc=cX, amt=16)
        exch_stk.close()
        v_dbgP = 0
        if DEBUG:
            p.op("sync", lambda s: s.dma_start(out=dbg["dbg_ag"], in_=ag.ap()),
                 waits=[(cCC, 1)], inc=cS, amt=16)
            v_dbgP = p.op("sync", lambda s: s.dma_start(out=dbg["dbg_P"], in_=S_run[:]),
                 waits=[(cS, v_pimp_s), (cX, v_pimp_x)], inc=cS, amt=16)

        t_corr_par = [0, 0]
        v_corr = 0
        for j in range(NCH):
            b = j % 2
            if j > 0:
                p.op("vector", lambda v: v.tensor_tensor(
                    S_run[:], S_run[:], lamct[:], op=ALU.mult), inc=cV)
            v_pbf = p.op("vector", (lambda v, b=b: v.tensor_copy(S_bfp[b][:], S_run[:])),
                         waits=([(cS, max(v_pimp_s, v_dbgP)), (cX, v_pimp_x),
                                 (cT, v_scan_t)] if j == 0 else
                                [(cT, t_corr_par[b])]), inc=cV)
            for g4 in range(4):
                pb_ = 2 + (g4 % 2)
                tmx = 0
                for hh in range(4):
                    hx = g4 * 4 + hh
                    waits = []
                    if hh == 0:
                        waits = [po_free[g4 % 2]]
                        if g4 == 0:
                            waits += [(cV, v_pbf)]
                            if j == 0:
                                waits += [(cV, v_qdall)]
                    tmx = p.op("tensor", (lambda t, hx=hx, j=j, b=b, pb_=pb_, hh=hh: t.matmul(
                        pf[pb_][:, hh * 128:(hh + 1) * 128],
                        S_bfp[b][:, hx * 128:(hx + 1) * 128],
                        qd_all[:, j * H * CH + hx * 128: j * H * CH + (hx + 1) * 128],
                        start=True, stop=True, skip_group_check=True)),
                        waits=waits, inc=(cT if hh == 3 else None))
                v_corr = p.op("vector", (lambda v, g4=g4, j=j, pb_=pb_: v.tensor_tensor(
                    attnT.ap().rearrange("p (h t) -> p h t", h=H, t=TC)[:, g4 * 4:(g4 + 1) * 4, j * 128:(j + 1) * 128],
                    attnT.ap().rearrange("p (h t) -> p h t", h=H, t=TC)[:, g4 * 4:(g4 + 1) * 4, j * 128:(j + 1) * 128],
                    pf[pb_].ap().rearrange("p (h t) -> p h t", h=4, t=128), op=ALU.add)),
                    waits=[(cT, tmx)] + ([(cA, v_scan_a)] if j == 0 and g4 == 0 else []),
                    inc=cV)
                po_free[g4 % 2] = (cV, v_corr)
            t_corr_par[b] = cT.n
        if DEBUG:
            p.op("sync", lambda s: s.dma_start(out=dbg["dbg_attnT"], in_=attnT[:]),
                 waits=[(cV, v_corr)], inc=cS, amt=16)
        corr_stk.close()
        base_stk.close()
        snapC = [(cT, cT.n), (cV, cV.n), (cA, cA.n)]

        tail_stk = contextlib.ExitStack()
        sbT, _ = mk_alloc(tail_stk)
        R = sbT("R", [128, KD * TC], F32)
        w_eT = sbT("w_eT", [8, TC], F32)

        mf_stk = contextlib.ExitStack()
        sbMF, psMF = mk_alloc(mf_stk)
        gbuf = [sbMF("gbuf0", [128, D], BF16), sbMF("gbuf1", [128, D], BF16)]
        ubuf = [sbMF("ubuf0", [128, D], BF16), sbMF("ubuf1", [128, D], BF16)]
        dbuf = [sbMF("dbuf0", [128, 2048], BF16), sbMF("dbuf1", [128, 2048], BF16)]
        sg = [sbMF("sg0", [128, TC], BF16), sbMF("sg1", [128, TC], BF16)]
        hu = [sbMF("hu0", [128, TC], BF16), sbMF("hu1", [128, TC], BF16)]
        Wb = sbMF("Wb", [128, TC], BF16)
        wrow = sbMF("wrow", [1, TC], F32)
        pf.append(psMF("pf6", [128, 512], F32))
        pf.append(psMF("pf7", [128, 512], F32))

        r_ready = {}

        def evac_proj(i, m, half, bk, waits):
            par = m % 2
            vv = p.op("vector", (lambda v, m=m, half=half, bk=bk, par=par: v.scalar_tensor_tensor(
                R[:, m * TC + half * 512: m * TC + half * 512 + 512],
                pf[bk][:, 0:512], bout[:, m:m + 1],
                xrb[par][:, half * 512: half * 512 + 512],
                op0=ALU.add, op1=ALU.add)),
                waits=waits + [(cS, xr_ready[m])], inc=cV)
            if half == 1:
                r_ready[m] = vv
                if m + 2 < NM:
                    xr_ready[m + 2] = p.op("sync", (lambda s, m=m, par=par: s.dma_start(
                        out=xrb[par][:], in_=xT[(m + 2) * 128:(m + 3) * 128, :])),
                        waits=[(cV, vv)], inc=cS, amt=16)
            return (cV, vv)

        wbufB_free = [(cT, v_scan_t), (cT, v_scan_t)]
        emit_proj(wout_s, 0, NM, attnT, wbufB, wbufB_free, evac_proj,
                  [(cV, v_corr)] + snapC, {i: snapC[1] for i in range(4)})
        v_R1 = cV.n
        if DEBUG:
            p.op("sync", lambda s: s.dma_start(out=dbg["dbg_R1"], in_=R[:]),
                 waits=[(cV, v_R1)], inc=cS, amt=16)

        x_ln2 = bigA
        ln2_stk = contextlib.ExitStack()
        sbL2, _ = mk_alloc(ln2_stk)
        xb2 = [sbL2("xb2_0", [128, 512], BF16), sbL2("xb2_1", [128, 512], BF16)]
        sqb2 = [sbL2("sqb2_0", [128, 512], BF16), sbL2("sqb2_1", [128, 512], BF16)]
        rows2 = sbL2("rows2", [1, 4096], F32)
        nsc2 = [sbL2("nsc2_0", [128, 512], F32), sbL2("nsc2_1", [128, 512], F32)]
        ready_k2 = [[(cV, r_ready[k])] for k in range(KD)]
        v_ln2, lnk2, _ = emit_ln(R, x_ln2, xb2, sqb2, rows2, nsc2, ready_k2, extra0=snapC)
        if DEBUG:
            p.op("sync", lambda s: s.dma_start(out=dbg["dbg_xln2"], in_=x_ln2[:]),
                 waits=[(cV, v_ln2)], inc=cS, amt=16)

        rt_stk = contextlib.ExitStack()
        sbR, _ = mk_alloc(rt_stk)
        lgT = sbR("lgT", [8, TC], F32)
        lg_tm = sbR("lg_tm", [128, 64], F32)
        probs = sbR("probs", [128, 64], F32)
        g1 = sbR("g1", [128, 64], F32)
        g2 = sbR("g2", [128, 64], F32)
        se = sbR("se", [128, 8], F32)
        ke = sbR("ke", [128, 8], F32)
        w_all = sbR("w_all", [128, 64], F32)
        assert nc.sbuf_bytes_remaining >= 16384, nc.sbuf_bytes_remaining

        r3 = lambda apx: apx.rearrange("p (tt e) -> p tt e", tt=8, e=8)
        o3 = lambda apx: apx.rearrange("p (tt o) -> p tt o", tt=8, o=1)
        for half in range(2):
            for k in range(KD):
                waits = []
                if half == 0:
                    waits = [(cV, lnk2[k])]
                    if k == 0:
                        waits.append((cW, v_rtr))

                def mm(t, half=half, k=k):
                    return t.matmul(pf[6 + half][0:8, 0:512],
                                    router[:, k * 8:(k + 1) * 8],
                                    x_ln2[:, k * TC + half * 512: k * TC + half * 512 + 512],
                                    start=(k == 0), stop=(k == KD - 1))
                tm = p.op("tensor", mm, waits=waits, inc=(cT if k == KD - 1 else None))
            p.op("vector", (lambda v, half=half: v.tensor_copy(
                lgT[0:8, half * 512:(half + 1) * 512], pf[6 + half][0:8, 0:512])),
                waits=[(cT, tm)], inc=cV)
        v_lgT = cV.n
        for tt in range(8):
            tm = p.op("tensor", (lambda t, tt=tt: t.transpose(
                pf[4][:, tt * 8:(tt + 1) * 8], lgT[0:8, tt * 128:(tt + 1) * 128],
                id_f[0:8, 0:8])), waits=([(cV, v_lgT)] if tt == 0 else []),
                inc=(cT if tt == 7 else None))
        v_lgtm = p.op("vector", lambda v: v.tensor_copy(lg_tm[:], pf[4][:, 0:64]),
                      waits=[(cT, tm)], inc=cV)
        vrm = p.op("vector", lambda v: v.tensor_reduce(
            o3(se.ap()), r3(lg_tm.ap()), axis=AX.X, op=ALU.max), inc=cV)
        p.op("vector", lambda v: v.tensor_tensor(
            r3(g1.ap()), r3(lg_tm.ap()), o3(se.ap()).broadcast_to((128, 8, 8)),
            op=ALU.subtract), waits=[(cV, vrm)], inc=cV)
        v_exp8 = p.op("scalar", lambda a: a.activation(probs[:], g1[:], AF.Exp),
                      waits=[(cV, cV.n)], inc=cA)
        vred = p.op("vector", lambda v: v.tensor_reduce(
            o3(se.ap()), r3(probs.ap()), axis=AX.X, op=ALU.add),
            waits=[(cA, v_exp8)], inc=cV)
        vrec = p.op("vector", lambda v: v.reciprocal(ke[:], se[:]),
                    waits=[(cV, vred)], inc=cV)
        v_pr = p.op("vector", lambda v: v.tensor_tensor(
            r3(probs.ap()), r3(probs.ap()), o3(ke.ap()).broadcast_to((128, 8, 8)),
            op=ALU.mult), waits=[(cV, vrec)], inc=cV)
        if DEBUG:
            p.op("sync", lambda s: s.dma_start(out=dbg["dbg_probs"], in_=probs[:]),
                 waits=[(cV, v_pr)], inc=cS, amt=16)
        v_wcol = 0
        for e in range(NE):
            colb = r3(probs.ap())[:, :, e:e + 1].broadcast_to((128, 8, 8))
            p.op("vector", (lambda v, colb=colb: v.tensor_tensor(
                r3(g1.ap()), r3(probs.ap()), colb, op=ALU.is_gt)), inc=cV)
            p.op("vector", (lambda v, colb=colb: v.tensor_tensor(
                r3(g2.ap()), r3(probs.ap()), colb, op=ALU.is_equal)), inc=cV)
            p.op("vector", (lambda v, e=e: v.tensor_tensor(
                r3(g2.ap()), r3(g2.ap()),
                r3(tlt.ap()[:, e * 64:(e + 1) * 64]), op=ALU.mult)),
                waits=([(cS, v_pre)] if e == 0 else []), inc=cV)
            p.op("vector", lambda v: v.tensor_tensor(
                r3(g1.ap()), r3(g1.ap()), r3(g2.ap()), op=ALU.add), inc=cV)
            p.op("vector", lambda v: v.tensor_tensor(
                r3(g1.ap()), r3(g1.ap()), r3(probs.ap()), op=ALU.mult), inc=cV)
            vr2 = p.op("vector", lambda v: v.tensor_reduce(
                o3(se.ap()), r3(g1.ap()), axis=AX.X, op=ALU.add), inc=cV)
            vk2 = p.op("vector", lambda v: v.tensor_scalar(
                ke[:], se[:], TOP_P, None, op0=ALU.is_le), waits=[(cV, vr2)], inc=cV)
            v_wcol = p.op("vector", (lambda v, e=e: v.tensor_tensor(
                r3(w_all.ap())[:, :, e:e + 1], o3(ke.ap()),
                r3(probs.ap())[:, :, e:e + 1], op=ALU.mult)), waits=[(cV, vk2)], inc=cV)
        if DEBUG:
            p.op("sync", lambda s: s.dma_start(out=dbg["dbg_w"], in_=w_all[:]),
                 waits=[(cV, v_wcol)], inc=cS, amt=16)
        v_weT = 0
        for rnd in range(2):
            tm = 0
            for i4 in range(4):
                tt = rnd * 4 + i4
                waits = []
                if i4 == 0:
                    waits = [(cV, v_wcol)] if rnd == 0 else [(cV, v_weT)]
                tm = p.op("tensor", (lambda t, tt=tt, i4=i4: t.transpose(
                    pf[5][0:8, i4 * 128:(i4 + 1) * 128], w_all[:, tt * 8:(tt + 1) * 8],
                    id_f[:, :])), waits=waits, inc=(cT if i4 == 3 else None))
            v_weT = p.op("vector", (lambda v, rnd=rnd: v.tensor_copy(
                w_eT[0:8, rnd * 512:(rnd + 1) * 512], pf[5][0:8, 0:512])),
                waits=[(cT, tm)], inc=cV)
        v_lgT_v = v_lgT
        rt_stk.close()
        ln2_stk.close()
        snapR = [(cT, cT.n), (cV, cV.n), (cA, cA.n)]

        moe_stk = contextlib.ExitStack()
        sbM, _ = mk_alloc(moe_stk)
        hT = sbM("hT", [128, I_T * TC], BF16)

        free_g = [snapC[2], snapC[2]]
        free_u = [snapC[1], snapC[1]]
        free_d = [snapC[0], snapC[0]]
        bank_free = {0: (cV, v_ln2), 1: (cV, v_ln2), 2: (cV, v_ln2), 3: (cV, v_ln2),
                     4: (cV, v_lgtm), 5: (cV, v_weT), 6: (cV, v_lgT_v), 7: (cV, v_lgT_v)}
        wb_free = (cV, 0)
        sg_free = [snapC[1], snapC[1]]
        hu_free = [snapC[1], snapC[1]]
        hu_guard = [[snapC[1], snapC[1]], [snapC[1], snapC[1]]]
        wrow_free = (cT, 0)
        v2 = 0

        for e in range(NE):
            v_wr = p.op("sync", (lambda s, e=e: s.dma_start(
                out=wrow[:], in_=w_eT[e:e + 1, :])),
                waits=[(cV, v_weT), wrow_free] + (snapC if e == 0 else []), inc=cS, amt=16)
            wb_vpost = [0, 0]
            for hf in range(2):
                bk = hf
                tpost = p.op("tensor", (lambda t, hf=hf, bk=bk: t.matmul(
                    pf[bk][:, 0:512], ones_row[0:1, :],
                    wrow[0:1, hf * 512:(hf + 1) * 512], start=True, stop=True)),
                    waits=[(cS, v_wr), bank_free[bk]], inc=cT)
                wrow_free = (cT, tpost)
                vpost = p.op("vector", (lambda v, hf=hf, bk=bk: v.tensor_copy(
                    Wb[:, hf * 512:(hf + 1) * 512], pf[bk][:, 0:512])),
                    waits=[(cT, tpost), wb_free], inc=cV)
                bank_free[bk] = (cV, vpost)
                wb_vpost[hf] = vpost
            for it in range(I_T):
                par = it % 2
                p.op("gpsimd", (lambda g, e=e, it=it, par=par: g.dma_start(
                    out=gbuf[par][:], in_=gate_s[e, it])),
                    waits=[free_g[par]] + (snapC if (e == 0 and it < 2) else []),
                    inc=cW, amt=16)
                wu = p.op("gpsimd", (lambda g, e=e, it=it, par=par: g.dma_start(
                    out=ubuf[par][:], in_=up_s[e, it])), waits=[free_u[par]], inc=cW, amt=16)
                tg = [0, 0]
                tu = [0, 0]
                for hf in range(2):
                    bkg = par * 4 + hf
                    bku = par * 4 + 2 + hf
                    for k in range(KD):
                        waits = []
                        if k == 0:
                            waits = [hu_guard[par][hf]]
                            if hf == 0:
                                waits.append((cW, wu))
                            if it == 0:
                                waits.append((cV, wb_vpost[hf]))
                            if it < 2:
                                waits.append(bank_free[bkg])
                                waits.append(bank_free[bku])
                            if e == 0 and it < 2:
                                waits.append((cV, v_ln2))

                        def g_mm(t, par=par, hf=hf, k=k, bkg=bkg):
                            return t.matmul(pf[bkg][:, 0:512],
                                            gbuf[par][:, k * 128:(k + 1) * 128],
                                            x_ln2[:, k * TC + hf * 512: k * TC + hf * 512 + 512],
                                            start=(k == 0), stop=(k == KD - 1))
                        tg[hf] = p.op("tensor", g_mm, waits=waits,
                                      inc=(cT if k == KD - 1 else None))
                    for k in range(KD):

                        def u_mm(t, par=par, hf=hf, k=k, bku=bku):
                            return t.matmul(pf[bku][:, 0:512],
                                            ubuf[par][:, k * 128:(k + 1) * 128],
                                            x_ln2[:, k * TC + hf * 512: k * TC + hf * 512 + 512],
                                            start=(k == 0), stop=(k == KD - 1))
                        tu[hf] = p.op("tensor", u_mm, waits=[],
                                      inc=(cT if k == KD - 1 else None))
                for hf in range(2):
                    bkg = par * 4 + hf
                    bku = par * 4 + 2 + hf
                    apost = p.op("scalar", (lambda a, par=par, hf=hf, bkg=bkg: a.activation(
                        sg[par][:, hf * 512:(hf + 1) * 512], pf[bkg][:, 0:512], AF.Silu)),
                        waits=[(cT, tg[hf]), sg_free[par]], inc=cA)
                    free_g[par] = (cA, apost)
                    bank_free[bkg] = (cA, apost)
                    v1 = p.op("vector", (lambda v, par=par, hf=hf, bku=bku: v.tensor_tensor(
                        hu[par][:, hf * 512:(hf + 1) * 512],
                        sg[par][:, hf * 512:(hf + 1) * 512],
                        pf[bku][:, 0:512], op=ALU.mult)),
                        waits=[(cA, apost), (cT, tu[hf]), hu_free[par]], inc=cV)
                    free_u[par] = (cV, v1)
                    bank_free[bku] = (cV, v1)
                    hu_guard[par][hf] = (cV, v1)
                    v2 = p.op("vector", (lambda v, it=it, par=par, hf=hf: v.tensor_tensor(
                        hT[:, it * TC + hf * 512: it * TC + hf * 512 + 512],
                        hu[par][:, hf * 512:(hf + 1) * 512],
                        Wb[:, hf * 512:(hf + 1) * 512], op=ALU.mult)),
                        waits=(([(cV, wb_vpost[1])] + (snapR if e == 0 else []))
                               if (it == 0 and hf == 0) else []), inc=cV)
                    hu_free[par] = (cV, v2)
                    if hf == 1:
                        sg_free[par] = (cV, v2)
            hT_done = v2
            wb_free = (cV, hT_done)
            for m in range(NM):
                bks = ((m % 2) * 2, (m % 2) * 2 + 1)
                td = [0, 0]
                for kg in range(2):
                    par = (2 * m + kg) % 2
                    wd = p.op("gpsimd", (lambda g, e=e, m=m, kg=kg, par=par: g.dma_start(
                        out=dbuf[par][:], in_=down_s[e, m][:, kg * 2048:(kg + 1) * 2048])),
                        waits=[free_d[par]], inc=cW, amt=16)
                    first_waits = [(cW, wd)]
                    if kg == 0:
                        first_waits += [bank_free[bks[1]]]
                        if m < 2:
                            first_waits += [(cV, hT_done), bank_free[bks[0]]]
                    for kk in range(16):
                        k = kg * 16 + kk
                        for hf in range(2):
                            waits = first_waits if (kk == 0 and hf == 0) else []

                            def d_mm(t, par=par, k=k, kk=kk, hf=hf, bk=bks[hf]):
                                return t.matmul(pf[bk][:, 0:512],
                                                dbuf[par][:, kk * 128:(kk + 1) * 128],
                                                hT[:, k * TC + hf * 512: k * TC + hf * 512 + 512],
                                                start=(k == 0), stop=(k == I_T - 1))
                            td[hf] = p.op("tensor", d_mm, waits=waits,
                                          inc=(cT if k == I_T - 1 else None))
                    if kg == 0:
                        tmk = p.op("tensor", (lambda t: t.nop()), inc=cT)
                        free_d[par] = (cT, tmk)
                    else:
                        free_d[par] = (cT, td[1])
                for hf in range(2):
                    vr = p.op("vector", (lambda v, m=m, hf=hf, bk=bks[hf]: v.tensor_tensor(
                        R[:, m * TC + hf * 512: m * TC + hf * 512 + 512],
                        R[:, m * TC + hf * 512: m * TC + hf * 512 + 512],
                        pf[bk][:, 0:512], op=ALU.add)), waits=[(cT, td[hf])], inc=cV)
                    bank_free[bks[hf]] = (cV, vr)
                if e == NE - 1:
                    p.op("sync", (lambda s, m=m: s.dma_start(
                        out=outT.rearrange("(k p) t -> p k t", k=KD, p=128)[:, m],
                        in_=R[:, m * TC:(m + 1) * TC])),
                        waits=[(cV, vr)], inc=cS, amt=16)

        p.op("sync", None, waits=[(cS, cS.n)])

        moe_stk.close()
        mf_stk.close()
        tail_stk.close()
        bigA_stk.close()
        block = L0.enter_context(nc.Block())
        p.emit(block)
    return nc


def _pack(inputs):
    x = np.asarray(inputs["L0_attn_x"], dtype=np.float32)
    w_qkv = np.asarray(inputs["L0_attn_w_qkv"], dtype=np.float32)
    b_qkv = np.asarray(inputs["L0_attn_b_qkv"], dtype=np.float32)
    w_out = np.asarray(inputs["L0_attn_w_out"], dtype=np.float32)
    b_out = np.asarray(inputs["L0_attn_b_out"], dtype=np.float32)
    router_w = np.asarray(inputs["L0_ffn_router_w"], dtype=np.float32)
    gate_w = np.asarray(inputs["L0_ffn_gate_w"], dtype=np.float32)
    up_w = np.asarray(inputs["L0_ffn_up_w"], dtype=np.float32)
    down_w = np.asarray(inputs["L0_ffn_down_w"], dtype=np.float32)

    xT = np.ascontiguousarray(x.reshape(T, D).T)
    qkv_sv = np.ascontiguousarray(
        w_qkv.reshape(QKV_M, 128, KD, 128).transpose(0, 3, 2, 1)).reshape(QKV_M, 128, D)
    wout_sv = np.ascontiguousarray(
        w_out.reshape(NM, 128, KD, 128).transpose(0, 3, 2, 1)).reshape(NM, 128, D)
    bqkv = np.ascontiguousarray(b_qkv.reshape(QKV_M, 128).T)
    boutp = np.ascontiguousarray(b_out.reshape(NM, 128).T)
    router_p = np.ascontiguousarray(
        router_w.T.reshape(KD, 128, NE).transpose(1, 0, 2)).reshape(128, 128)
    gate_sv = np.ascontiguousarray(
        gate_w.reshape(NE, I_T, 128, KD, 128).transpose(0, 1, 4, 3, 2)).reshape(NE, I_T, 128, D)
    up_sv = np.ascontiguousarray(
        up_w.reshape(NE, I_T, 128, KD, 128).transpose(0, 1, 4, 3, 2)).reshape(NE, I_T, 128, D)
    down_sv = np.ascontiguousarray(
        down_w.reshape(NE, NM, 128, I_T, 128).transpose(0, 1, 4, 3, 2)).reshape(NE, NM, 128, I_T * 128)

    lam, decayq, dmaskT, decayk, lamc = _host_consts()
    shared = {
        "qkv_s": qkv_sv, "bqkv": bqkv, "wout_s": wout_sv, "bout": boutp,
        "router_p": router_p, "gate_s": gate_sv, "up_s": up_sv, "down_s": down_sv,
        "dmaskT": dmaskT, "decayq": decayq, "decayk": decayk, "lamc": lamc,
        "tlt": _tlt(), "ident": np.eye(128, dtype=np.float32),
    }
    in_maps = []
    for c in range(NCORES):
        m = dict(shared)
        m["xT"] = np.ascontiguousarray(xT[:, c * TC:(c + 1) * TC])
        m["coef"] = _host_coef(c, lam)
        in_maps.append(m)
    return in_maps


def kernel(**inputs):
    in_maps = _pack(inputs)
    nc = build()
    res = run_bass_kernel_spmd(nc, in_maps, core_ids=list(range(NCORES)))
    if DEBUG:
        _LAST["res"] = res
        _LAST["in_maps"] = in_maps
    parts = [np.asarray(res.results[c]["outT"], dtype=np.float32) for c in range(NCORES)]
    full = np.concatenate(parts, axis=1)
    return np.ascontiguousarray(full.T).reshape(1, T, D)
